# version 02
# baseline (speedup 1.0000x reference)
import numpy as np
from contextlib import ExitStack

import concourse.bass as bass
import concourse.mybir as mybir
from concourse.bass_utils import run_bass_kernel_spmd

F32 = mybir.dt.float32
BF16 = mybir.dt.bfloat16
AF = mybir.ActivationFunctionType
ALU = mybir.AluOpType
AX = mybir.AxisListType

E2 = 2048
H = 16
ET = 16
SOWN = 1024
ST = 8
NE = 10
FT = 11
HT = 16
NEG = -1.0e30
ISQD = 1.0 / float(np.sqrt(128.0))

ENGINES = ("tensor", "vector", "scalar", "sync", "gpsimd")

_STOP = None
_LAST = {}


class _Bld:
    def __init__(self, semnames):
        self.streams = {e: [] for e in ENGINES}
        self.C = {n: 0 for n in semnames}
        self.sems = {}
        self._trun = 0

    def raw(self, eng, fn):
        self.streams[eng].append(fn)

    def wait(self, eng, sem, val):
        if val <= 0:
            return
        assert val <= self.C[sem], f"wait {sem}>={val} but only {self.C[sem]} planned"
        if eng == "tensor":
            self._trun = 0
        self.raw(eng, lambda E, s=sem, v=val: E.wait_ge(self.sems[s], v))

    def op(self, eng, fn, inc=None, n=1):
        if eng == "tensor":
            self._trun += 1
            if self._trun >= 4:
                self.streams[eng].append(lambda E: E.nop())
                self._trun = 0
        if inc is None:
            self.raw(eng, fn)
            return None
        self.C[inc] += n
        self.raw(eng, lambda E, f=fn, s=inc, k=n: f(E).then_inc(self.sems[s], k))
        return self.C[inc]

    def replay(self, eng_name, E):
        for fn in self.streams[eng_name]:
            fn(E)


def _lay_qkv(W):
    return np.ascontiguousarray(
        W.T.reshape(16, 128, 16, 128).transpose(2, 1, 0, 3))


def _prep_shared(inp):
    out = {}
    out["wq"] = _lay_qkv(np.asarray(inp["L0_attn_Wq"], np.float32))
    out["wk"] = _lay_qkv(np.asarray(inp["L0_attn_Wk"], np.float32))
    out["wv"] = _lay_qkv(np.asarray(inp["L0_attn_Wv"], np.float32))
    out["wo"] = _lay_qkv(np.asarray(inp["L0_attn_Wo"], np.float32))
    out["brows"] = np.concatenate(
        [np.asarray(inp[k], np.float32) for k in
         ("L0_attn_bq", "L0_attn_bk", "L0_attn_bv", "L0_attn_bo")]).reshape(1, 8192)
    out["gw"] = np.ascontiguousarray(
        np.asarray(inp["L0_ffn_gate_weight"], np.float32).T
        .reshape(16, 128, 8).transpose(1, 0, 2))
    gus = []
    for e in range(8):
        Wt = np.asarray(inp["L0_ffn_gate_up_weights"][e], np.float32).T
        gus.append(Wt.reshape(16, 128, 2, 11, 128)
                   .transpose(3, 1, 2, 0, 4).reshape(11, 128, 4096))
    St = np.asarray(inp["L0_ffn_shared_gate_up"], np.float32).T
    sg = (St.reshape(16, 128, 2, 2, 11, 128)
          .transpose(3, 4, 1, 2, 0, 5).reshape(2, 11, 128, 4096))
    out["guw"] = np.ascontiguousarray(
        np.concatenate([np.stack(gus, 0), sg], axis=0))
    dns = []
    for e in range(8):
        Dt = np.asarray(inp["L0_ffn_down_weights"][e], np.float32).T
        dns.append(Dt.reshape(11, 128, 8, 2, 128)
                   .transpose(2, 1, 3, 0, 4).reshape(8, 128, 2816))
    St2 = np.asarray(inp["L0_ffn_shared_down"], np.float32).T
    sd2 = (St2.reshape(2, 11, 128, 8, 2, 128)
           .transpose(0, 3, 2, 4, 1, 5).reshape(2, 8, 128, 2816))
    out["dnw"] = np.ascontiguousarray(
        np.concatenate([np.stack(dns, 0), sd2], axis=0))
    out["ident"] = np.eye(128, dtype=np.float32)
    perm = np.zeros((128, 128), np.float32)
    for j in range(128):
        perm[j + 1 if j % 2 == 0 else j - 1, j] = 1.0
    out["perm"] = perm
    out["ones1"] = np.ones((1, 512), np.float32)
    return out


def _rope_tables():
    j = np.arange(128)
    theta = 10000.0 ** (-(2.0 * (j // 2)) / 128.0)
    ang = np.outer(theta, np.arange(2048).astype(np.float64))
    cosg = np.cos(ang).astype(np.float32)
    sign = np.where(j % 2 == 0, -1.0, 1.0)
    sing = (np.sin(ang) * sign[:, None]).astype(np.float32)
    return cosg, sing


def _core_perm(half):
    own = np.concatenate(
        [np.arange(256 * t + 128 * half, 256 * t + 128 * half + 128)
         for t in range(8)])
    other = np.concatenate(
        [np.arange(256 * t + 128 * (1 - half), 256 * t + 128 * (1 - half) + 128)
         for t in range(8)])
    return own, np.concatenate([own, other])


def _prep_core(inp, b, half, cosg, sing):
    own, perm = _core_perm(half)
    d = {}
    d["x"] = np.ascontiguousarray(np.asarray(inp["L0_attn_x"], np.float32)[b][perm])
    d["cosk"] = np.ascontiguousarray(cosg[:, perm])
    d["sink"] = np.ascontiguousarray(sing[:, perm])
    m = np.zeros((128, 8, 256), np.float32)
    tri = np.where(np.arange(128)[None, :] <= np.arange(128)[:, None], 0.0, NEG)
    m[:, :, 0:128] = tri.astype(np.float32)[:, None, :]
    if half == 0:
        m[:, :, 128:256] = NEG
    d["mask"] = m
    return d


def _build(stop=None):
    nc = bass.Bass("TRN2", target_bir_lowering=False)
    dd = {}

    def din(name, shape):
        dd[name] = nc.dram_tensor(name, list(shape), F32, kind="ExternalInput").ap()

    din("x", (2048, 2048))
    din("wq", (16, 128, 16, 128)); din("wk", (16, 128, 16, 128))
    din("wv", (16, 128, 16, 128)); din("wo", (16, 128, 16, 128))
    din("brows", (1, 8192))
    din("gw", (128, 16, 8))
    din("guw", (NE, FT, 128, 4096))
    din("dnw", (NE, 8, 128, 2816))
    din("cosk", (128, 2048)); din("sink", (128, 2048))
    din("mask", (128, 8, 256))
    din("ident", (128, 128)); din("perm", (128, 128)); din("ones1", (1, 512))
    out_d = nc.dram_tensor("out", [ST, 128, 2048], F32, kind="ExternalOutput").ap()
    tap_shapes = {"xn": [128, ET * 2048], "gating": [128, 64],
                  "gat0": [128, 64], "gat1": [128, 64], "h0q": [128, 1024],
                  "h0k": [128, 2048], "h0v": [128, 2048], "h0t0": [128, 2048],
                  "head0": [128, 5 * 2048], "attn": [128, H * 1024],
                  "oproj": [128, HT * 1024], "oproj0": [128, 1024],
                  "ffn": [128, HT * 1024]}
    tap_d = None
    if stop is not None:
        tap_d = nc.dram_tensor("tap", tap_shapes[stop], F32,
                               kind="ExternalOutput").ap()

    SEMS = ("dG", "dS", "mT", "mV", "mC")
    bld = _Bld(SEMS)
    op, wait, C = bld.op, bld.wait, bld.C
    es = ExitStack()

    with es:
        xn_own = es.enter_context(nc.sbuf_tensor("xn_own", [128, ET * 1024], BF16))
        ao_w = 2 * 1024 if stop == "h0t0" else H * 1024
        attn_out = es.enter_context(nc.sbuf_tensor("attn_out", [128, ao_w], BF16))

        combT64 = es.enter_context(nc.sbuf_tensor("combT64", [64, 128], BF16))
        identb = es.enter_context(nc.sbuf_tensor("identb", [128, 128], BF16))
        identf = es.enter_context(nc.sbuf_tensor("identf", [128, 128], F32))
        permb = es.enter_context(nc.sbuf_tensor("permb", [128, 128], BF16))
        onesb = es.enter_context(nc.sbuf_tensor("onesb", [1, 512], BF16))
        nm1 = {}
        for k in ("stdv", "vare", "rstd", "negmrs", "negmax", "negmaxs", "rsum",
                  "rsum2", "rsumT", "rinv", "gnm", "gsum", "grin", "bqc", "bkc",
                  "bvc", "boc"):
            nm1[k] = es.enter_context(nc.sbuf_tensor("v_" + k, [128, 1], F32))
        dumv = es.enter_context(nc.sbuf_tensor("dumv", [128, 1], F32))
        dums = es.enter_context(nc.sbuf_tensor("dums", [128, 1], F32))
        for s_ in SEMS:
            bld.sems[s_] = es.enter_context(nc.semaphore("sem_" + s_))

        def vgap():
            op("vector", lambda E: E.tensor_copy(dumv[:], identf[:, 0:1]))

        def vmark():
            return op("vector", lambda E: E.tensor_copy(
                dumv[:], identf[:, 0:1]), inc="mV")

        def cmark():
            return op("scalar", lambda E: E.copy(dums[:], identf[:, 0:1]),
                      inc="mC")

        def dmaG(out_ap, in_ap):
            return op("gpsimd", lambda E, o=out_ap, i=in_ap: E.dma_start(out=o, in_=i),
                      inc="dG", n=16)

        def dmaS(out_ap, in_ap):
            return op("sync", lambda E, o=out_ap, i=in_ap: E.dma_start(out=o, in_=i),
                      inc="dS", n=16)

        def fin(val, sem="dS"):
            wait("sync", sem, val)

        es_a = ExitStack()
        sba = lambda n_, s_, d_: es_a.enter_context(nc.sbuf_tensor(n_, list(s_), d_))
        psa = lambda n_, s_, d_: es_a.enter_context(nc.psum_tensor(n_, list(s_), d_))
        xnT = sba("xnT", (128, ET * 2048), BF16)
        coskb = sba("coskb", (128, 2048), BF16)
        sinkb = sba("sinkb", (128, 2048), BF16)
        browsb = sba("browsb", (1, 6144), BF16)
        masks = sba("masks", (128, 8 * 256), BF16)
        gw_sb = sba("gw_sb", (128, 128), BF16)
        logits = sba("logits", (128, 64), F32)
        probs = sba("probs", (128, 64), F32)
        top8 = sba("top8", (128, 8), F32)
        mask8 = sba("mask8", (128, 8), F32)
        combf = sba("combf", (128, 64), F32)
        comb_bf = sba("comb_bf", (128, 64), BF16)
        pA = psa("pA", (128, 1024), F32)
        pS = psa("pS", (128, 2048), F32)
        pT = psa("pT", (128, 1024), BF16)
        pT2 = psa("pT2", (128, 1024), BF16)

        dmaG(identb[:], dd["ident"])
        dmaG(permb[:], dd["perm"])
        dmaG(onesb[:], dd["ones1"])
        dmaG(coskb[:], dd["cosk"])
        dmaG(sinkb[:], dd["sink"])
        dmaG(browsb[:], dd["brows"][0:1, 0:6144])
        dG_consts = dmaG(gw_sb[:].rearrange("p (j c) -> p j c", j=16), dd["gw"])
        dmaG(masks[:].rearrange("p (t c) -> p t c", t=8), dd["mask"])
        dS_c = dmaS(identf[:], dd["ident"])

        es_l = ExitStack()
        sbl = lambda n_, s_, d_: es_l.enter_context(nc.sbuf_tensor(n_, list(s_), d_))
        xbuf = [sbl(f"xbuf{k}", (128, 2048), F32) for k in range(2)]
        xnN = [sbl(f"xnN{k}", (128, 2048), BF16) for k in range(2)]
        sqs = sbl("sqs", (128, 2048), F32)
        lnS = [{k: sbl(f"ln_{k}{p}", (128, 1), F32)
                for k in ("s1", "s2", "mean", "varb", "vare", "stdv", "rstd",
                          "negmrs")} for p in range(2)]

        lnC = [0] * 16
        lnT = [[0, 0] for _ in range(16)]
        lnEv = [[0, 0] for _ in range(16)]
        lnXld = [0] * 16
        lnVng = [0] * 16

        def ln_front(i):
            p = lnS[i % 2]
            if i >= 2:
                wait("sync", "mC", lnC[i - 2])
            lnXld[i] = dmaS(xbuf[i % 2][:], dd["x"][128 * i:128 * (i + 1), :])
            wait("vector", "dS", lnXld[i])
            op("vector", lambda E, i_=i, p_=p: E.reduce_sum(
                p_["s1"][:], xbuf[i_ % 2][:], axis=AX.X))
            wait("scalar", "dS", lnXld[i])
            op("scalar", lambda E, i_=i, p_=p: E.activation(
                sqs[:], xbuf[i_ % 2][:], AF.Square, accum_out=p_["s2"][:]))
            csq = cmark()
            wait("vector", "mC", csq)
            op("vector", lambda E, p_=p: E.tensor_scalar_mul(
                p_["varb"][:], p_["s2"][:], 1.0 / 2048.0))
            op("vector", lambda E, p_=p: E.scalar_tensor_tensor(
                p_["vare"][:], p_["s1"][:], 1.0 / (2048.0 * 2048.0), p_["s1"][:],
                ALU.mult, ALU.mult))
            op("vector", lambda E, p_=p: E.tensor_scalar_mul(
                p_["mean"][:], p_["s1"][:], 1.0 / 2048.0))
            op("vector", lambda E, p_=p: E.scalar_tensor_tensor(
                p_["varb"][:], p_["vare"][:], -1.0, p_["varb"][:],
                ALU.mult, ALU.add))
            vgap()
            op("vector", lambda E, p_=p: E.tensor_scalar_add(
                p_["vare"][:], p_["varb"][:], 1.0e-5))
            vst = vmark()
            wait("scalar", "mV", vst)
            op("scalar", lambda E, p_=p: E.activation(
                p_["stdv"][:], p_["vare"][:], AF.Sqrt))
            cst = cmark()
            wait("vector", "mC", cst)
            op("vector", lambda E, p_=p: E.reciprocal(
                p_["rstd"][:], p_["stdv"][:]))
            vgap()
            op("vector", lambda E, p_=p: E.scalar_tensor_tensor(
                p_["negmrs"][:], p_["mean"][:], -1.0, p_["rstd"][:],
                ALU.mult, ALU.mult))
            lnVng[i] = vmark()

        def ln_back(i):
            p = lnS[i % 2]
            wait("scalar", "mV", lnVng[i])
            if i >= 2:
                wait("scalar", "mT", lnT[i - 2][1])
            lnC[i] = op("scalar", lambda E, i_=i, p_=p: E.activation(
                xnN[i_ % 2][:], xbuf[i_ % 2][:], AF.Identity,
                bias=p_["negmrs"][:], scale=p_["rstd"][:]), inc="mC")
            wait("tensor", "dG", dG_consts)
            wait("tensor", "mC", lnC[i])
            for b2 in range(2):
                pTx = pT if b2 == 0 else pT2
                if i >= 1:
                    wait("tensor", "mV", lnEv[i - 1][b2])
                for j8 in range(8):
                    j = 8 * b2 + j8
                    fn = (lambda E, i_=i, j_=j, j8_=j8, px=pTx: E.transpose(
                        px[:, 128 * j8_:128 * (j8_ + 1)],
                        xnN[i_ % 2][:, 128 * j_:128 * (j_ + 1)], identb[:]))
                    if j8 == 7:
                        lnT[i][b2] = op("tensor", fn, inc="mT")
                    else:
                        op("tensor", fn)
                wait("vector", "mT", lnT[i][b2])
                dst = bass.AP(xnT, (8 * b2) * 2048 + 128 * i,
                              [[ET * 2048, 128], [2048, 8], [1, 128]])
                src_ = bass.AP(pTx, 0, [[1024, 128], [128, 8], [1, 128]])
                lnEv[i][b2] = op("vector",
                                 lambda E, d_=dst, s_=src_: E.tensor_copy(d_, s_),
                                 inc="mV")

        for i in range(17):
            if i < 16:
                ln_front(i)
            if i >= 1:
                ln_back(i - 1)
        for j in range(ET):
            vcomp = op("vector", lambda E, j_=j: E.tensor_copy(
                xn_own[:, 1024 * j_:1024 * (j_ + 1)],
                xnT[:, 2048 * j_:2048 * j_ + 1024]), inc="mV")
        xn_own_done = vcomp

        if stop == "xn":
            wait("gpsimd", "mV", xn_own_done)
            fin(dmaG(tap_d, xnT[:]), "dG")

        if stop != "xn":
            wait("tensor", "dG", dG_consts)
            wait("tensor", "mV", xn_own_done)
            for i in range(ST):
                for j in range(ET):
                    fn = (lambda E, i_=i, j_=j: E.matmul(
                        pA[:, 8 * i_:8 * (i_ + 1)],
                        xn_own[:, 1024 * j_ + 128 * i_:1024 * j_ + 128 * (i_ + 1)],
                        gw_sb[:, 8 * j_:8 * (j_ + 1)],
                        start=(j_ == 0), stop=(j_ == 15)))
                    if i == ST - 1 and j == ET - 1:
                        gmm = op("tensor", fn, inc="mT")
                    else:
                        op("tensor", fn)
            wait("vector", "dS", dS_c)
            wait("vector", "mT", gmm)
            op("vector", lambda E: E.tensor_copy(logits[:], pA[:, 0:64]))
            glog = vmark()
            if stop == "gat0":
                wait("sync", "mV", glog)
                fin(dmaS(tap_d, logits[:]))
            for i in range(ST if stop != "gat0" else 0):
                sl = slice(8 * i, 8 * (i + 1))
                op("vector", lambda E, s_=sl: E.reduce_max(
                    nm1["gnm"][:], logits[:, s_], axis=AX.X, negate=True))
                vgap()
                vm = vmark()
                wait("scalar", "mV", vm)
                op("scalar", lambda E, s_=sl: E.activation(
                    probs[:, s_], logits[:, s_], AF.Exp, bias=nm1["gnm"][:],
                    accum_out=nm1["gsum"][:]))
                cse = cmark()
                wait("vector", "mC", cse)
                op("vector", lambda E: E.reciprocal(nm1["grin"][:], nm1["gsum"][:]))
                vgap()
                op("vector", lambda E, s_=sl: E.tensor_scalar_mul(
                    probs[:, s_], probs[:, s_], nm1["grin"][:]))
                vgap()
                op("vector", lambda E, s_=sl: E.max(out=top8[:], in_=probs[:, s_]))
                vgap()
                op("vector", lambda E, s_=sl: E.tensor_tensor(
                    mask8[:], probs[:, s_], top8[:, 1:2].to_broadcast([128, 8]),
                    ALU.is_ge))
                vgap()
                op("vector", lambda E, s_=sl: E.tensor_mul(
                    combf[:, s_], probs[:, s_], mask8[:]))
                vgap()
                vcb = op("vector", lambda E, s_=sl: E.tensor_copy(
                    comb_bf[:, s_], combf[:, s_]), inc="mV")
            if stop == "gat1":
                wait("sync", "mV", vcb)
                fin(dmaS(tap_d, combf[:]))
            if stop not in ("gat0", "gat1"):
                wait("tensor", "mV", vcb)
                wait("tensor", "mV", lnEv[15][0])
                ctr = op("tensor", lambda E: E.transpose(
                    pT[0:64, 0:128], comb_bf[:], identb[:]), inc="mT")
                wait("vector", "mT", ctr)
                vct = op("vector", lambda E: E.tensor_copy(
                    combT64[:], pT[0:64, 0:128]), inc="mV")

        if stop == "gating":
            wait("sync", "mV", vcb)
            fin(dmaS(tap_d, combf[:]))


        es_l.close()
        es_f = es_f1 = es_fo = None

        if stop not in ("xn", "gating", "gat0", "gat1"):
            pass
        if stop not in ("xn", "gating", "gat0", "gat1"):
            wqb = [sba(f"wqb{k}", (128, 2048), BF16) for k in range(2)]
            wkb = [sba(f"wkb{k}", (128, 2048), BF16) for k in range(2)]
            wvb = [sba(f"wvb{k}", (128, 2048), BF16) for k in range(2)]
            stage1 = sba("stage1", (128, 2048), BF16)
            stage2 = sba("stage2", (128, 2048), BF16)
            qh = sba("qh", (128, 1024), BF16)
            kh = sba("kh", (128, 2048), BF16)
            vn = sba("vn", (128, 2048), BF16)
            Pb = sba("Pb", (128, 2048), BF16)
            PTq = sba("PTq", (128, 2048), BF16)

            wld = [0] * H
            wrel = [0] * H
            pvEv = [0] * H
            biasV = [0] * H
            trdone = 0
            ptFree = [vct, lnEv[15][1]]
            nheads = 1 if stop in ("head0", "h0t0") else H

            for h in range(nheads):
                if h >= 2:
                    wait("gpsimd", "mT", wrel[h - 2])
                else:
                    wait("gpsimd", "mT", lnT[15][1])
                    wait("gpsimd", "mC", lnC[15])
                for wb_, nm in ((wqb, "wq"), (wkb, "wk"), (wvb, "wv")):
                    wld[h] = dmaG(wb_[h % 2][:],
                                  dd[nm][h].rearrange("p j d -> p (j d)"))
                wait("tensor", "dG", wld[h])
                if h == 0:
                    wait("tensor", "mV", vct)
                else:
                    wait("tensor", "mV", pvEv[h - 1])
                for c in range(2):
                    for j in range(ET):
                        op("tensor", lambda E, h_=h, c_=c, j_=j: E.matmul(
                            pA[:, 512 * c_:512 * (c_ + 1)],
                            wqb[h_ % 2][:, 128 * j_:128 * (j_ + 1)],
                            xn_own[:, 1024 * j_ + 512 * c_:1024 * j_ + 512 * (c_ + 1)],
                            start=(j_ == 0), stop=False))
                    fnb = (lambda E, h_=h, c_=c: E.matmul(
                        pA[:, 512 * c_:512 * (c_ + 1)],
                        browsb[0:1, 128 * h_:128 * (h_ + 1)],
                        onesb[0:1, 0:512], start=False, stop=True))
                    if c == 1:
                        qmm = op("tensor", fnb, inc="mT")
                    else:
                        op("tensor", fnb)
                wait("scalar", "mT", qmm)
                if h > 0:
                    wait("scalar", "mV", krope)
                qev = op("scalar", lambda E: E.activation(
                    stage1[:, 0:1024], pA[:, 0:1024], AF.Identity), inc="mC")
                wait("tensor", "mC", qev)
                if h > 0:
                    wait("tensor", "mC", ksv)
                for c in range(2):
                    fn = (lambda E, c_=c: E.matmul(
                        pS[:, 512 * c_:512 * (c_ + 1)], permb[:],
                        stage1[:, 512 * c_:512 * (c_ + 1)], start=True, stop=True))
                    if c == 1:
                        qsw = op("tensor", fn, inc="mT")
                    else:
                        op("tensor", fn)
                wait("vector", "mT", qsw)
                op("vector", lambda E: E.tensor_tensor(
                    qh[:], stage1[:, 0:1024], coskb[:, 0:1024], ALU.mult))
                wait("scalar", "mT", qsw)
                if h > 0:
                    wait("scalar", "mV", vnEvLast)
                qsv = op("scalar", lambda E: E.activation(
                    stage2[:, 0:1024], pS[:, 0:1024], AF.Identity), inc="mC")
                wait("vector", "mC", qsv)
                op("vector", lambda E: E.tensor_tensor(
                    stage2[:, 0:1024], stage2[:, 0:1024], sinkb[:, 0:1024], ALU.mult))
                op("vector", lambda E: E.tensor_tensor(
                    qh[:], qh[:], stage2[:, 0:1024], ALU.add))
                qrope = op("vector", lambda E: E.tensor_scalar_mul(
                    qh[:], qh[:], ISQD), inc="mV")
                if stop == "h0q":
                    wait("gpsimd", "mV", qrope)
                    fin(dmaG(tap_d[:, 0:1024], qh[:]), "dG")
                    break
                for r in range(2):
                    wait("tensor", "mC", qsv if r == 0 else kev)
                    for c in range(2):
                        for j in range(ET):
                            op("tensor", lambda E, h_=h, r_=r, c_=c, j_=j: E.matmul(
                                pA[:, 512 * c_:512 * (c_ + 1)],
                                wkb[h_ % 2][:, 128 * j_:128 * (j_ + 1)],
                                xnT[:, 2048 * j_ + 1024 * r_ + 512 * c_:
                                    2048 * j_ + 1024 * r_ + 512 * (c_ + 1)],
                                start=(j_ == 0), stop=False))
                        fnb = (lambda E, h_=h, c_=c: E.matmul(
                            pA[:, 512 * c_:512 * (c_ + 1)],
                            browsb[0:1, 2048 + 128 * h_:2048 + 128 * (h_ + 1)],
                            onesb[0:1, 0:512], start=False, stop=True))
                        if c == 1:
                            kmm = op("tensor", fnb, inc="mT")
                        else:
                            op("tensor", fnb)
                    wait("scalar", "mT", kmm)
                    if r == 0:
                        wait("scalar", "mV", qrope)
                    kev = op("scalar", lambda E, r_=r: E.activation(
                        stage1[:, 1024 * r_:1024 * (r_ + 1)], pA[:, 0:1024],
                        AF.Identity), inc="mC")
                wait("tensor", "mC", kev)
                wait("tensor", "mC", qsv)
                for c in range(4):
                    fn = (lambda E, c_=c: E.matmul(
                        pS[:, 512 * c_:512 * (c_ + 1)], permb[:],
                        stage1[:, 512 * c_:512 * (c_ + 1)], start=True, stop=True))
                    if c == 3:
                        ksw = op("tensor", fn, inc="mT")
                    else:
                        op("tensor", fn)
                wait("vector", "mT", ksw)
                wait("vector", "mC", kev)
                op("vector", lambda E: E.tensor_tensor(
                    kh[:], stage1[:], coskb[:], ALU.mult))
                wait("scalar", "mT", ksw)
                wait("scalar", "mV", qrope)
                ksv = op("scalar", lambda E: E.activation(
                    stage2[:], pS[:], AF.Identity), inc="mC")
                wait("vector", "mC", ksv)
                op("vector", lambda E: E.tensor_tensor(
                    stage2[:], stage2[:], sinkb[:], ALU.mult))
                krope = op("vector", lambda E: E.tensor_tensor(
                    kh[:], kh[:], stage2[:], ALU.add), inc="mV")
                if stop == "h0k":
                    wait("gpsimd", "mV", krope)
                    fin(dmaG(tap_d, kh[:]), "dG")
                    break
                for r in range(2):
                    wait("tensor", "mC", ksv if r == 0 else vev)
                    for c in range(2):
                        for j in range(ET):
                            op("tensor", lambda E, h_=h, r_=r, c_=c, j_=j: E.matmul(
                                pA[:, 512 * c_:512 * (c_ + 1)],
                                wvb[h_ % 2][:, 128 * j_:128 * (j_ + 1)],
                                xnT[:, 2048 * j_ + 1024 * r_ + 512 * c_:
                                    2048 * j_ + 1024 * r_ + 512 * (c_ + 1)],
                                start=(j_ == 0), stop=False))
                        fnb = (lambda E, h_=h, c_=c: E.matmul(
                            pA[:, 512 * c_:512 * (c_ + 1)],
                            browsb[0:1, 4096 + 128 * h_:4096 + 128 * (h_ + 1)],
                            onesb[0:1, 0:512], start=False, stop=True))
                        if c == 1:
                            vmm = op("tensor", fnb, inc="mT")
                        else:
                            op("tensor", fnb)
                    wait("scalar", "mT", vmm)
                    if r == 0:
                        wait("scalar", "mV", krope)
                    vev = op("scalar", lambda E, r_=r: E.activation(
                        stage2[:, 1024 * r_:1024 * (r_ + 1)], pA[:, 0:1024],
                        AF.Identity), inc="mC")
                wait("tensor", "mC", vev)
                vnev = [0, 0]
                for b2 in range(2):
                    pTx = pT if b2 == 0 else pT2
                    wait("tensor", "mV", ptFree[b2])
                    for j8 in range(8):
                        j = 8 * b2 + j8
                        fn = (lambda E, j_=j, j8_=j8, px=pTx: E.transpose(
                            px[:, 128 * j8_:128 * (j8_ + 1)],
                            stage2[:, 128 * j_:128 * (j_ + 1)], identb[:]))
                        if j8 == 7:
                            vtr = op("tensor", fn, inc="mT")
                        else:
                            op("tensor", fn)
                    wait("vector", "mT", vtr)
                    vnev[b2] = op("vector", lambda E, b2_=b2: E.tensor_copy(
                        vn[:, 1024 * b2_:1024 * (b2_ + 1)],
                        (pT if b2_ == 0 else pT2)[:]), inc="mV")
                ptFree = [vnev[0], vnev[1]]
                vnEvLast = vnev[1]
                wrel[h] = vtr
                if stop == "h0v":
                    wait("gpsimd", "mV", vnev[1])
                    fin(dmaG(tap_d, vn[:]), "dG")
                    break

                pvE = [0] * ST
                for t in range(ST):
                    L = 128 * (t + 1)
                    wait("tensor", "mV", qrope)
                    wait("tensor", "mV", krope)
                    if t == 0:
                        wait("tensor", "mC", ksv)
                    else:
                        wait("tensor", "mC", expC)
                    for base in (0, 1024):
                        c0 = 0
                        while c0 < L:
                            c1 = min(L, (c0 // 512 + 1) * 512)
                            fn = (lambda E, t_=t, b_=base, a_=c0, z_=c1:
                                  E.matmul(
                                      pS[:, b_ + a_:b_ + z_],
                                      qh[:, 128 * t_:128 * (t_ + 1)],
                                      kh[:, b_ + a_:b_ + z_],
                                      start=True, stop=True))
                            if base == 1024 and c1 == L:
                                smm = op("tensor", fn, inc="mT")
                            else:
                                op("tensor", fn)
                            c0 = c1
                    wait("vector", "mT", smm)
                    op("vector", lambda E, t_=t: E.tensor_tensor(
                        pS[:, 128 * t_:128 * (t_ + 1)],
                        pS[:, 128 * t_:128 * (t_ + 1)],
                        masks[:, 256 * t_:256 * t_ + 128], ALU.add))
                    op("vector", lambda E, t_=t: E.tensor_tensor(
                        pS[:, 1024 + 128 * t_:1024 + 128 * (t_ + 1)],
                        pS[:, 1024 + 128 * t_:1024 + 128 * (t_ + 1)],
                        masks[:, 256 * t_ + 128:256 * (t_ + 1)], ALU.add))
                    nms = vmark()
                    wait("scalar", "mV", nms)
                    if trdone:
                        wait("scalar", "mT", trdone)
                    op("scalar", lambda E, L_=L: E.activation(
                        Pb[:, 0:L_], pS[:, 0:L_], AF.Exp,
                        accum_out=nm1["rsum"][:]))
                    op("scalar", lambda E, L_=L: E.activation(
                        Pb[:, L_:2 * L_], pS[:, 1024:1024 + L_], AF.Exp,
                        accum_out=nm1["rsum2"][:]))
                    expC = cmark()
                    wait("vector", "mC", expC)
                    op("vector", lambda E: E.tensor_add(
                        nm1["rsumT"][:], nm1["rsum"][:], nm1["rsum2"][:]))
                    vgap()
                    op("vector", lambda E: E.reciprocal(nm1["rinv"][:], nm1["rsumT"][:]))
                    vgap()
                    psc = op("vector", lambda E, L_=L: E.tensor_scalar_mul(
                        Pb[:, 0:2 * L_], Pb[:, 0:2 * L_], nm1["rinv"][:]), inc="mV")
                    wait("tensor", "mV", psc)
                    ptEv = [0, 0]
                    for b2 in range(2):
                        pTx = pT if b2 == 0 else pT2
                        wait("tensor", "mV", ptFree[b2])
                        for u in range(t + 1):
                            src_col = 128 * u if b2 == 0 else L + 128 * u
                            fn = (lambda E, s_=src_col, u_=u, px=pTx: E.transpose(
                                px[:, 128 * u_:128 * (u_ + 1)],
                                Pb[:, s_:s_ + 128], identb[:]))
                            if u == t:
                                tr_ = op("tensor", fn, inc="mT")
                            else:
                                op("tensor", fn)
                        wait("vector", "mT", tr_)
                        ptEv[b2] = op("vector", lambda E, t_=t, b2_=b2: E.tensor_copy(
                            PTq[:, 1024 * b2_:1024 * b2_ + 128 * (t_ + 1)],
                            (pT if b2_ == 0 else pT2)[:, 0:128 * (t_ + 1)]),
                            inc="mV")
                    ptFree = [ptEv[0], ptEv[1]]
                    trdone = C["mT"]
                    pvcol = 512 * (t % 2)
                    wait("tensor", "mV", ptEv[1])
                    if t >= 2:
                        wait("tensor", "mV", pvE[t - 2])
                    elif h > 0:
                        wait("tensor", "mV", pvEv[h - 1])
                    first = True
                    for b2 in range(2):
                        for u in range(t + 1):
                            kslot = u if b2 == 0 else 8 + u
                            vcol = 128 * u if b2 == 0 else 1024 + 128 * u
                            last = (b2 == 1 and u == t)
                            fn = (lambda E, v_=vcol, k_=kslot, p_=pvcol, f_=first,
                                  l_=last: E.matmul(
                                      pA[:, p_:p_ + 128], vn[:, v_:v_ + 128],
                                      PTq[:, 128 * k_:128 * (k_ + 1)],
                                      start=f_, stop=l_))
                            if last:
                                pvt = op("tensor", fn, inc="mT")
                            else:
                                op("tensor", fn)
                            first = False
                    wait("vector", "mT", pvt)
                    pvE[t] = op("vector", lambda E, h_=h, t_=t, p_=pvcol:
                                E.tensor_copy(
                                    attn_out[:, 1024 * h_ + 128 * t_:
                                             1024 * h_ + 128 * (t_ + 1)],
                                    pA[:, p_:p_ + 128]), inc="mV")
                    if stop == "h0t0" and t == 0:
                        wait("gpsimd", "mV", psc)
                        dmaG(tap_d[:, 0:512], Pb[:, 0:512])
                        wait("gpsimd", "mV", ptEv[1])
                        dmaG(tap_d[:, 512:768], PTq[:, 0:256])
                        dmaG(tap_d[:, 1024:1280], PTq[:, 1024:1280])
                        wait("gpsimd", "mV", pvE[0])
                        fin(dmaG(tap_d[:, 1536:1664],
                                 attn_out[:, 1024 * h:1024 * h + 128]), "dG")
                        break
                pvEv[h] = pvE[ST - 1]

            attn_done_V = pvEv[nheads - 1]

            if stop == "head0":
                wait("gpsimd", "mV", qrope)
                dmaG(tap_d[:, 0:1024], qh[:])
                wait("gpsimd", "mV", krope)
                dmaG(tap_d[:, 2048:4096], kh[:])
                wait("gpsimd", "mV", vnEvLast)
                dmaG(tap_d[:, 4096:6144], vn[:])
                wait("gpsimd", "mV", attn_done_V)
                fin(dmaG(tap_d[:, 6144:7168], attn_out[:, 0:1024]), "dG")
            if stop == "attn":
                wait("gpsimd", "mV", attn_done_V)
                fin(dmaG(tap_d, attn_out[:]), "dG")

        if stop in (None, "oproj", "oproj0", "ffn"):
            es_a.close()
            bT, bV, bC = C["mT"], C["mV"], C["mC"]
            for eng in ENGINES:
                wait(eng, "mT", bT)
                wait(eng, "mV", bV)
                wait(eng, "mC", bC)

            es_f = ExitStack()
            sbf_ = lambda n_, s_, d_: es_f.enter_context(
                nc.sbuf_tensor(n_, list(s_), d_))
            psf_ = lambda n_, s_, d_: es_f.enter_context(
                nc.psum_tensor(n_, list(s_), d_))
            acc = sbf_("acc", (128, HT * 1024), F32)
            borowsb = sbf_("borowsb", (1, 2048), BF16)
            pW = psf_("pW", (128, 1024), F32)
            pX = psf_("pX", (128, 1024), F32)
            pY = psf_("pY", (128, 1024), F32)
            pZ = psf_("pZ", (128, 1024), F32)
            es_fo = ExitStack()
            wob = [es_fo.enter_context(nc.sbuf_tensor(f"wob{k}", [128, 2048], BF16))
                   for k in range(2)]
            dG_bo = dmaG(borowsb[:], dd["brows"][0:1, 6144:8192])

            wait("tensor", "dG", dG_bo)
            woL = {}
            woR = {}
            oEv = [0] * HT
            woL[0] = dmaG(wob[0][:], dd["wo"][0].rearrange("p j d -> p (j d)"))
            woL[1] = dmaG(wob[1][:], dd["wo"][1].rearrange("p j d -> p (j d)"))
            for fb in range(HT if stop != "oproj0" else 16):
                pP = pW if fb % 2 == 0 else pX
                wait("tensor", "dG", woL[fb])
                if fb >= 2:
                    wait("tensor", "mC", oEv[fb - 2])
                for c in range(2):
                    for j in range(ET):
                        op("tensor", lambda E, f_=fb, c_=c, j_=j, p_=pP: E.matmul(
                            p_[:, 512 * c_:512 * (c_ + 1)],
                            wob[f_ % 2][:, 128 * j_:128 * (j_ + 1)],
                            attn_out[:, 1024 * j_ + 512 * c_:
                                     1024 * j_ + 512 * (c_ + 1)],
                            start=(j_ == 0), stop=False))
                    fnb = (lambda E, f_=fb, c_=c, p_=pP: E.matmul(
                        p_[:, 512 * c_:512 * (c_ + 1)],
                        borowsb[0:1, 128 * f_:128 * (f_ + 1)],
                        onesb[0:1, 0:512], start=False, stop=True))
                    if c == 1:
                        woR[fb] = op("tensor", fnb, inc="mT")
                    else:
                        op("tensor", fnb)
                if fb + 2 < HT:
                    wait("gpsimd", "mT", woR[fb])
                    woL[fb + 2] = dmaG(wob[fb % 2][:],
                                       dd["wo"][fb + 2].rearrange("p j d -> p (j d)"))
                wait("scalar", "mT", woR[fb])
                oEv[fb] = op("scalar", lambda E, f_=fb, p_=pP: E.activation(
                    acc[:, 1024 * f_:1024 * (f_ + 1)], p_[:], AF.Identity),
                    inc="mC")
            o_done = oEv[HT - 1] if stop != "oproj0" else oEv[0]
            if stop == "oproj0":
                wait("sync", "mC", o_done)
                fin(dmaS(tap_d, acc[:, 0:1024]))

            if stop == "oproj":
                wait("sync", "mC", o_done)
                fin(dmaS(tap_d, acc[:]))

        if stop in (None, "ffn"):
            es_fo.close()
            es_f1 = ExitStack()
            sbf1 = lambda n_, s_, d_: es_f1.enter_context(
                nc.sbuf_tensor(n_, list(s_), d_))
            gub = [sbf1(f"gub{k}", (128, 4096), BF16) for k in range(2)]
            dnb = [sbf1(f"dnb{k}", (128, 2816), BF16) for k in range(2)]
            act = sbf1("act", (128, FT * 1024), BF16)
            bc_sb = sbf1("bc_sb", (128, 1024), BF16)
            t1_sb = sbf1("t1_sb", (128, 1024), BF16)
            combT2 = sbf1("combT2", (1, 8192), BF16)
            wait("gpsimd", "mT", woR[HT - 1])
            wait("gpsimd", "mV", vct)
            for e in range(8):
                dG_comb = dmaG(
                    bass.AP(combT2, e * 1024, [[8192, 1], [128, 8], [1, 128]]),
                    bass.AP(combT64, e * 128, [[8 * 128, 8], [1, 128]]))
            guL = {}
            guR = {}
            dnL = {}
            dnR = {}
            guL[0] = dmaG(gub[0][:], dd["guw"][0, 0])
            guL[1] = dmaG(gub[1][:], dd["guw"][0, 1])
            dnL[0] = dmaG(dnb[0][:], dd["dnw"][0, 0])
            dnL[1] = dmaG(dnb[1][:], dd["dnw"][0, 1])

            dnAdd = 0
            bcV = 0
            guEvLast = 0
            for e in range(NE):
                gated = e < 8
                if gated:
                    wait("tensor", "dG", dG_comb)
                    if e > 0:
                        wait("tensor", "mV", bcV)
                        wait("tensor", "mV", guEvLast)
                    else:
                        wait("tensor", "mC", o_done)
                    for c in range(2):
                        fn = (lambda E, e_=e, c_=c: E.matmul(
                            pY[:, 512 * c_:512 * (c_ + 1)], onesb[0:1, 0:128],
                            combT2[0:1, 1024 * e_ + 512 * c_:
                                   1024 * e_ + 512 * (c_ + 1)],
                            start=True, stop=True))
                        if c == 1:
                            bcT = op("tensor", fn, inc="mT")
                        else:
                            op("tensor", fn)
                    wait("vector", "mT", bcT)
                    bcV = op("vector", lambda E: E.tensor_copy(bc_sb[:], pY[:]),
                             inc="mV")
                guEv = [0] * FT
                for jg in range(FT):
                    k = e * FT + jg
                    pair = (pW, pX) if jg % 2 == 0 else (pY, pZ)
                    wait("tensor", "dG", guL[k])
                    if jg >= 2:
                        wait("tensor", "mV", guEv[jg - 2])
                    elif e == 0:
                        wait("tensor", "mC", o_done)
                    else:
                        wait("tensor", "mV", dnAdd)
                        if jg == 1 and gated:
                            wait("tensor", "mV", bcV)
                    for half2 in range(2):
                        pP = pair[half2]
                        for c in range(2):
                            for j in range(ET):
                                fn = (lambda E, k_=k, h2=half2, c_=c, j_=j, p_=pP:
                                      E.matmul(
                                          p_[:, 512 * c_:512 * (c_ + 1)],
                                          gub[k_ % 2][:, 2048 * h2 + 128 * j_:
                                                      2048 * h2 + 128 * (j_ + 1)],
                                          xn_own[:, 1024 * j_ + 512 * c_:
                                                 1024 * j_ + 512 * (c_ + 1)],
                                          start=(j_ == 0), stop=(j_ == 15)))
                                if half2 == 1 and c == 1 and j == ET - 1:
                                    guR[k] = op("tensor", fn, inc="mT")
                                else:
                                    op("tensor", fn)
                    if k + 2 < NE * FT:
                        e2, jg2 = divmod(k + 2, FT)
                        wait("gpsimd", "mT", guR[k])
                        guL[k + 2] = dmaG(gub[k % 2][:], dd["guw"][e2, jg2])
                    wait("scalar", "mT", guR[k])
                    if jg >= 1:
                        wait("scalar", "mV", guEv[jg - 1])
                    elif e > 0:
                        wait("scalar", "mV", guEvPrevLast)
                    sil = op("scalar", lambda E, p_=pair[0]: E.activation(
                        t1_sb[:], p_[:], AF.Silu), inc="mC")
                    wait("vector", "mC", sil)
                    if gated:
                        wait("vector", "mV", bcV)
                    op("vector", lambda E, p_=pair[1]: E.tensor_tensor(
                        t1_sb[:], t1_sb[:], p_[:], ALU.mult))
                    if gated:
                        guEv[jg] = op("vector", lambda E, g_=jg: E.tensor_tensor(
                            act[:, 1024 * g_:1024 * (g_ + 1)], t1_sb[:], bc_sb[:],
                            ALU.mult), inc="mV")
                    else:
                        guEv[jg] = op("vector", lambda E, g_=jg: E.tensor_copy(
                            act[:, 1024 * g_:1024 * (g_ + 1)], t1_sb[:]), inc="mV")
                guEvPrevLast = guEv[FT - 1]
                guEvLast = guEv[FT - 1]
                dnAddPrev = [guEv[FT - 1], guEv[FT - 2]]
                for hb in range(HT):
                    c4, hb2 = hb // 2, hb % 2
                    m = e * 8 + c4
                    pP = pW if hb % 2 == 0 else pX
                    wait("tensor", "dG", dnL[m])
                    wait("tensor", "mV", guEvLast)
                    if hb >= 2:
                        wait("tensor", "mV", dnAddPrev[hb % 2])
                    for f in range(FT):
                        for c in range(2):
                            fn = (lambda E, m_=m, h2=hb2, f_=f, c_=c, p_=pP:
                                  E.matmul(
                                      p_[:, 512 * c_:512 * (c_ + 1)],
                                      dnb[m_ % 2][:, 1408 * h2 + 128 * f_:
                                                  1408 * h2 + 128 * (f_ + 1)],
                                      act[:, 1024 * f_ + 512 * c_:
                                          1024 * f_ + 512 * (c_ + 1)],
                                      start=(f_ == 0), stop=(f_ == FT - 1)))
                            if f == FT - 1 and c == 1:
                                dmm = op("tensor", fn, inc="mT")
                            else:
                                op("tensor", fn)
                    if hb2 == 1:
                        dnR[m] = dmm
                        if m + 2 < NE * 8:
                            e2, c42 = divmod(m + 2, 8)
                            wait("gpsimd", "mT", dnR[m])
                            dnL[m + 2] = dmaG(dnb[m % 2][:], dd["dnw"][e2, c42])
                    wait("vector", "mT", dmm)
                    dnAdd = op("vector", lambda E, hb_=hb, p_=pP: E.tensor_tensor(
                        acc[:, 1024 * hb_:1024 * (hb_ + 1)],
                        acc[:, 1024 * hb_:1024 * (hb_ + 1)], p_[:], ALU.add),
                        inc="mV")
                    dnAddPrev[hb % 2] = dnAdd
            ffn_done = dnAdd

            if stop == "ffn":
                wait("sync", "mV", ffn_done)
                fin(dmaS(tap_d, acc[:]))

        if stop is None:
            es_f1.close()
            xrb = []
            outb = []
            for k in range(2):
                xrb.append(es_f.enter_context(
                    nc.sbuf_tensor(f"xrb{k}", [128, 2048], F32)))
                outb.append(es_f.enter_context(
                    nc.sbuf_tensor(f"outb{k}", [128, 2048], F32)))
            wait("sync", "mV", ffn_done)
            finV = [0] * ST
            stO = [0] * ST
            for i in range(ST):
                if i >= 2:
                    wait("sync", "mV", finV[i - 2])
                xrl = dmaS(xrb[i % 2][:], dd["x"][128 * i:128 * (i + 1), :])
                pPair = (pY, pZ) if i % 2 == 0 else (pW, pX)
                wait("tensor", "mV", ffn_done)
                if i >= 2:
                    wait("tensor", "mV", finV[i - 2])
                trF = [0, 0]
                for g in range(2):
                    pP = pPair[g]
                    for b4 in range(8):
                        et = 8 * g + b4
                        fn = (lambda E, e_=et, b_=b4, p_=pP, i_=i: E.transpose(
                            p_[:, 128 * b_:128 * (b_ + 1)],
                            acc[:, 1024 * e_ + 128 * i_:1024 * e_ + 128 * (i_ + 1)],
                            identf[:]))
                        if b4 == 7:
                            trF[g] = op("tensor", fn, inc="mT")
                        else:
                            op("tensor", fn)
                wait("vector", "mT", trF[1])
                wait("vector", "dS", xrl)
                if i >= 2:
                    wait("vector", "dS", stO[i - 2])
                for g in range(2):
                    pP = pPair[g]
                    for c in range(2):
                        fn = (lambda E, g_=g, c_=c, p_=pP, i_=i: E.tensor_tensor(
                            outb[i_ % 2][:, 1024 * g_ + 512 * c_:
                                         1024 * g_ + 512 * (c_ + 1)],
                            xrb[i_ % 2][:, 1024 * g_ + 512 * c_:
                                        1024 * g_ + 512 * (c_ + 1)],
                            p_[:, 512 * c_:512 * (c_ + 1)], ALU.add))
                        if g == 1 and c == 1:
                            finV[i] = op("vector", fn, inc="mV")
                        else:
                            op("vector", fn)
                wait("sync", "mV", finV[i])
                stO[i] = dmaS(out_d[i], outb[i % 2][:])
            fin(stO[ST - 1])

        if es_f1 is not None:
            es_f1.close()
        if es_fo is not None:
            es_fo.close()
        if es_f is not None:
            es_f.close()
        es_a.close()

        block = es.enter_context(nc.Block())

        @block.tensor
        def _(E):
            bld.replay("tensor", E)

        @block.vector
        def _(E):
            bld.replay("vector", E)

        @block.scalar
        def _(E):
            bld.replay("scalar", E)

        @block.sync
        def _(E):
            bld.replay("sync", E)

        @block.gpsimd
        def _(E):
            bld.replay("gpsimd", E)

    nc.finalize()
    return nc


def kernel(**inputs):
    global _LAST
    shared = _prep_shared(inputs)
    cosg, sing = _rope_tables()
    nc = _build(_STOP)
    in_maps = []
    for c in range(8):
        b, half = c // 2, c % 2
        m = dict(shared)
        m.update(_prep_core(inputs, b, half, cosg, sing))
        in_maps.append(m)
    res = run_bass_kernel_spmd(nc, in_maps, core_ids=list(range(8)))
    out = np.zeros((4, 2048, 2048), np.float32)
    _LAST = {}
    for c in range(8):
        b, half = c // 2, c % 2
        own, _ = _core_perm(half)
        out[b][own] = np.asarray(res.results[c]["out"]).reshape(1024, 2048)
        if _STOP is not None:
            _LAST[c] = np.asarray(res.results[c]["tap"])
    return out


# version 06
# speedup vs baseline: 1.1095x; 1.1095x over previous
import numpy as np
from contextlib import ExitStack

import concourse.bass as bass
import concourse.mybir as mybir
from concourse.bass_utils import run_bass_kernel_spmd

F32 = mybir.dt.float32
BF16 = mybir.dt.bfloat16
AF = mybir.ActivationFunctionType
ALU = mybir.AluOpType
AX = mybir.AxisListType

E2 = 2048
H = 16
ET = 16
SOWN = 1024
ST = 8
NE = 10
FT = 11
HT = 16
NEG = -1.0e30
ISQD = 1.0 / float(np.sqrt(128.0))

ENGINES = ("tensor", "vector", "scalar", "sync", "gpsimd")

_STOP = None
_LAST = {}


class _Bld:
    def __init__(self, semnames):
        self.streams = {e: [] for e in ENGINES}
        self.C = {n: 0 for n in semnames}
        self.sems = {}
        self._trun = 0

    def raw(self, eng, fn):
        self.streams[eng].append(fn)

    def wait(self, eng, sem, val):
        if val <= 0:
            return
        assert val <= self.C[sem], f"wait {sem}>={val} but only {self.C[sem]} planned"
        if eng == "tensor":
            self._trun = 0
        self.raw(eng, lambda E, s=sem, v=val: E.wait_ge(self.sems[s], v))

    def op(self, eng, fn, inc=None, n=1):
        if eng == "tensor":
            self._trun += 1
            if self._trun >= 4:
                self.streams[eng].append(lambda E: E.nop())
                self._trun = 0
        if inc is None:
            self.raw(eng, fn)
            return None
        self.C[inc] += n
        self.raw(eng, lambda E, f=fn, s=inc, k=n: f(E).then_inc(self.sems[s], k))
        return self.C[inc]

    def replay(self, eng_name, E):
        for fn in self.streams[eng_name]:
            fn(E)


def _lay_qkv(W):
    return np.ascontiguousarray(
        W.T.reshape(16, 128, 16, 128).transpose(2, 1, 0, 3))


def _prep_shared(inp):
    out = {}
    out["wq"] = _lay_qkv(np.asarray(inp["L0_attn_Wq"], np.float32))
    out["wk"] = _lay_qkv(np.asarray(inp["L0_attn_Wk"], np.float32))
    out["wv"] = _lay_qkv(np.asarray(inp["L0_attn_Wv"], np.float32))
    out["wo"] = _lay_qkv(np.asarray(inp["L0_attn_Wo"], np.float32))
    bflat = np.concatenate(
        [np.asarray(inp[k], np.float32) for k in
         ("L0_attn_bq", "L0_attn_bk", "L0_attn_bv", "L0_attn_bo")]).reshape(64, 128)
    b4 = np.zeros((3, 22, 128), np.float32)
    for r in range(64):
        b4[r % 3, r // 3] = bflat[r]
    out["brows"] = b4.reshape(3, 2816)
    out["gw"] = np.ascontiguousarray(
        np.asarray(inp["L0_ffn_gate_weight"], np.float32).T
        .reshape(16, 128, 8).transpose(1, 0, 2))
    gus = []
    for e in range(8):
        Wt = np.asarray(inp["L0_ffn_gate_up_weights"][e], np.float32).T
        gus.append(Wt.reshape(16, 128, 2, 11, 128)
                   .transpose(3, 1, 2, 0, 4).reshape(11, 128, 4096))
    St = np.asarray(inp["L0_ffn_shared_gate_up"], np.float32).T
    sg = (St.reshape(16, 128, 2, 2, 11, 128)
          .transpose(3, 4, 1, 2, 0, 5).reshape(2, 11, 128, 4096))
    out["guw"] = np.ascontiguousarray(
        np.concatenate([np.stack(gus, 0), sg], axis=0))
    dns = []
    for e in range(8):
        Dt = np.asarray(inp["L0_ffn_down_weights"][e], np.float32).T
        dns.append(Dt.reshape(11, 128, 8, 2, 128)
                   .transpose(2, 1, 3, 0, 4).reshape(8, 128, 2816))
    St2 = np.asarray(inp["L0_ffn_shared_down"], np.float32).T
    sd2 = (St2.reshape(2, 11, 128, 8, 2, 128)
           .transpose(0, 3, 2, 4, 1, 5).reshape(2, 8, 128, 2816))
    out["dnw"] = np.ascontiguousarray(
        np.concatenate([np.stack(dns, 0), sd2], axis=0))
    out["ident"] = np.eye(128, dtype=np.float32)
    perm = np.zeros((128, 128), np.float32)
    for j in range(128):
        perm[j + 1 if j % 2 == 0 else j - 1, j] = 1.0
    out["perm"] = perm
    out["ones1"] = np.ones((128, 512), np.float32)
    return out


def _rope_tables():
    j = np.arange(128)
    theta = 10000.0 ** (-(2.0 * (j // 2)) / 128.0)
    ang = np.outer(theta, np.arange(2048).astype(np.float64))
    cosg = np.cos(ang).astype(np.float32)
    sign = np.where(j % 2 == 0, -1.0, 1.0)
    sing = (np.sin(ang) * sign[:, None]).astype(np.float32)
    return cosg, sing


def _core_perm(half):
    own = np.concatenate(
        [np.arange(256 * t + 128 * half, 256 * t + 128 * half + 128)
         for t in range(8)])
    other = np.concatenate(
        [np.arange(256 * t + 128 * (1 - half), 256 * t + 128 * (1 - half) + 128)
         for t in range(8)])
    return own, np.concatenate([own, other])


def _prep_core(inp, b, half, cosg, sing):
    own, perm = _core_perm(half)
    d = {}
    d["x"] = np.ascontiguousarray(np.asarray(inp["L0_attn_x"], np.float32)[b][perm])
    d["cosk"] = np.ascontiguousarray(cosg[:, perm])
    d["sink"] = np.ascontiguousarray(sing[:, perm])
    m = np.zeros((128, 256), np.float32)
    tri = np.where(np.arange(128)[None, :] <= np.arange(128)[:, None], 0.0, NEG)
    m[:, 0:128] = tri.astype(np.float32)
    if half == 0:
        m[:, 128:256] = NEG
    d["mask"] = m
    return d


def _build(stop=None):
    nc = bass.Bass("TRN2", target_bir_lowering=False)
    dd = {}

    def din(name, shape):
        dd[name] = nc.dram_tensor(name, list(shape), F32, kind="ExternalInput").ap()

    din("x", (2048, 2048))
    din("wq", (16, 128, 16, 128)); din("wk", (16, 128, 16, 128))
    din("wv", (16, 128, 16, 128)); din("wo", (16, 128, 16, 128))
    din("brows", (3, 2816))
    din("gw", (128, 16, 8))
    din("guw", (NE, FT, 128, 4096))
    din("dnw", (NE, 8, 128, 2816))
    din("cosk", (128, 2048)); din("sink", (128, 2048))
    din("mask", (128, 256))
    din("ident", (128, 128)); din("perm", (128, 128)); din("ones1", (128, 512))
    out_d = nc.dram_tensor("out", [ST, 128, 2048], F32, kind="ExternalOutput").ap()
    tap_shapes = {"xn": [128, ET * 2048], "gating": [128, 64],
                  "gat0": [128, 64], "gat1": [128, 64], "h0q": [128, 1024],
                  "h0k": [128, 2048], "h0v": [128, 2048], "h0t0": [128, 2048],
                  "head0": [128, 5 * 2048], "attn": [128, H * 1024],
                  "oproj": [128, HT * 1024], "oproj0": [128, 1024],
                  "ffn": [128, HT * 1024]}
    tap_d = None
    if stop is not None:
        tap_d = nc.dram_tensor("tap", tap_shapes[stop], F32,
                               kind="ExternalOutput").ap()

    SEMS = ("dG", "dS", "mT", "mV", "mC")
    bld = _Bld(SEMS)
    op, wait, C = bld.op, bld.wait, bld.C
    es = ExitStack()

    with es:
        xn_own = es.enter_context(nc.sbuf_tensor("xn_own", [128, ET * 1024], BF16))
        ao_w = 2 * 1024 if stop == "h0t0" else H * 1024
        attn_out = es.enter_context(nc.sbuf_tensor("attn_out", [128, ao_w], BF16))

        combT64 = es.enter_context(nc.sbuf_tensor("combT64", [64, 128], BF16))
        bias4 = es.enter_context(nc.sbuf_tensor("bias4", [128, 2816], BF16))
        identb = es.enter_context(nc.sbuf_tensor("identb", [128, 128], BF16))
        identf = es.enter_context(nc.sbuf_tensor("identf", [128, 128], F32))
        permb = es.enter_context(nc.sbuf_tensor("permb", [128, 128], BF16))
        onesb = es.enter_context(nc.sbuf_tensor("onesb", [128, 512], BF16))
        nm1 = {}
        for k in ("stdv", "vare", "rstd", "negmrs", "negmax", "negmaxs", "rsum",
                  "rsum2", "rsumT", "rinv", "gnm", "gsum", "grin", "bqc", "bkc",
                  "bvc", "boc"):
            nm1[k] = es.enter_context(nc.sbuf_tensor("v_" + k, [128, 1], F32))
        dumv = es.enter_context(nc.sbuf_tensor("dumv", [128, 1], F32))
        dums = es.enter_context(nc.sbuf_tensor("dums", [128, 1], F32))
        for s_ in SEMS:
            bld.sems[s_] = es.enter_context(nc.semaphore("sem_" + s_))

        def vgap():
            op("vector", lambda E: E.tensor_copy(dumv[:], identf[:, 0:1]))

        def vmark():
            return op("vector", lambda E: E.tensor_copy(
                dumv[:], identf[:, 0:1]), inc="mV")

        def cmark():
            return op("scalar", lambda E: E.copy(dums[:], identf[:, 0:1]),
                      inc="mC")

        def dmaG(out_ap, in_ap):
            return op("gpsimd", lambda E, o=out_ap, i=in_ap: E.dma_start(out=o, in_=i),
                      inc="dG", n=16)

        def dmaS(out_ap, in_ap):
            return op("sync", lambda E, o=out_ap, i=in_ap: E.dma_start(out=o, in_=i),
                      inc="dS", n=16)

        def fin(val, sem="dS"):
            wait("sync", sem, val)

        es_a = ExitStack()
        sba = lambda n_, s_, d_: es_a.enter_context(nc.sbuf_tensor(n_, list(s_), d_))
        psa = lambda n_, s_, d_: es_a.enter_context(nc.psum_tensor(n_, list(s_), d_))
        xnT = sba("xnT", (128, ET * 2048), BF16)
        coskb = sba("coskb", (128, 2048), BF16)
        sinkb = sba("sinkb", (128, 2048), BF16)

        masks = sba("masks", (128, 256), BF16)
        gw_sb = sba("gw_sb", (128, 128), BF16)
        logits = sba("logits", (128, 64), F32)
        probs = sba("probs", (128, 64), F32)
        top8 = sba("top8", (128, 8), F32)
        mask8 = sba("mask8", (128, 8), F32)
        combf = sba("combf", (128, 64), F32)
        comb_bf = sba("comb_bf", (128, 64), BF16)
        pA = psa("pA", (128, 1024), F32)
        pS = psa("pS", (128, 2048), F32)
        pT = psa("pT", (128, 1024), BF16)
        pT2 = psa("pT2", (128, 1024), BF16)

        dmaG(identb[:], dd["ident"])
        dmaG(permb[:], dd["perm"])
        dmaG(onesb[:], dd["ones1"])
        dmaG(coskb[:], dd["cosk"])
        dmaG(sinkb[:], dd["sink"])
        for p4 in range(3):
            dmaG(bass.AP(bias4, 32 * p4 * 2816, [[2816, 1], [1, 2816]]),
                 dd["brows"][p4:p4 + 1, :])
        dG_consts = dmaG(gw_sb[:].rearrange("p (j c) -> p j c", j=16), dd["gw"])
        dmaG(masks[:], dd["mask"])
        dS_c = dmaS(identf[:], dd["ident"])

        es_l = ExitStack()
        sbl = lambda n_, s_, d_: es_l.enter_context(nc.sbuf_tensor(n_, list(s_), d_))
        xbuf = [sbl(f"xbuf{k}", (128, 2048), F32) for k in range(2)]
        xnN = [sbl(f"xnN{k}", (128, 2048), BF16) for k in range(2)]
        sqs = sbl("sqs", (128, 2048), F32)
        lnS = [{k: sbl(f"ln_{k}{p}", (128, 1), F32)
                for k in ("s1", "s2", "mean", "varb", "vare", "stdv", "rstd",
                          "negmrs")} for p in range(2)]

        lnC = [0] * 16
        lnT = [[0, 0] for _ in range(16)]
        lnEv = [[0, 0] for _ in range(16)]
        lnXld = [0] * 16
        lnVng = [0] * 16

        def ln_front(i):
            p = lnS[i % 2]
            if i >= 2:
                wait("sync", "mC", lnC[i - 2])
            lnXld[i] = dmaS(xbuf[i % 2][:], dd["x"][128 * i:128 * (i + 1), :])
            wait("vector", "dS", lnXld[i])
            op("vector", lambda E, i_=i, p_=p: E.reduce_sum(
                p_["s1"][:], xbuf[i_ % 2][:], axis=AX.X))
            wait("scalar", "dS", lnXld[i])
            op("scalar", lambda E, i_=i, p_=p: E.activation(
                sqs[:], xbuf[i_ % 2][:], AF.Square, accum_out=p_["s2"][:]))
            csq = cmark()
            wait("vector", "mC", csq)
            op("vector", lambda E, p_=p: E.tensor_scalar_mul(
                p_["varb"][:], p_["s2"][:], 1.0 / 2048.0))
            op("vector", lambda E, p_=p: E.scalar_tensor_tensor(
                p_["vare"][:], p_["s1"][:], 1.0 / (2048.0 * 2048.0), p_["s1"][:],
                ALU.mult, ALU.mult))
            op("vector", lambda E, p_=p: E.tensor_scalar_mul(
                p_["mean"][:], p_["s1"][:], 1.0 / 2048.0))
            op("vector", lambda E, p_=p: E.scalar_tensor_tensor(
                p_["varb"][:], p_["vare"][:], -1.0, p_["varb"][:],
                ALU.mult, ALU.add))
            vgap()
            op("vector", lambda E, p_=p: E.tensor_scalar_add(
                p_["vare"][:], p_["varb"][:], 1.0e-5))
            vst = vmark()
            wait("scalar", "mV", vst)
            op("scalar", lambda E, p_=p: E.activation(
                p_["stdv"][:], p_["vare"][:], AF.Sqrt))
            cst = cmark()
            wait("vector", "mC", cst)
            op("vector", lambda E, p_=p: E.reciprocal(
                p_["rstd"][:], p_["stdv"][:]))
            vgap()
            op("vector", lambda E, p_=p: E.scalar_tensor_tensor(
                p_["negmrs"][:], p_["mean"][:], -1.0, p_["rstd"][:],
                ALU.mult, ALU.mult))
            lnVng[i] = vmark()

        def ln_back(i):
            p = lnS[i % 2]
            wait("scalar", "mV", lnVng[i])
            if i >= 2:
                wait("scalar", "mT", lnT[i - 2][1])
            lnC[i] = op("scalar", lambda E, i_=i, p_=p: E.activation(
                xnN[i_ % 2][:], xbuf[i_ % 2][:], AF.Identity,
                bias=p_["negmrs"][:], scale=p_["rstd"][:]), inc="mC")
            wait("tensor", "dG", dG_consts)
            wait("tensor", "mC", lnC[i])
            for b2 in range(2):
                pTx = pT if b2 == 0 else pT2
                if i >= 1:
                    wait("tensor", "mV", lnEv[i - 1][b2])
                for j8 in range(8):
                    j = 8 * b2 + j8
                    fn = (lambda E, i_=i, j_=j, j8_=j8, px=pTx: E.transpose(
                        px[:, 128 * j8_:128 * (j8_ + 1)],
                        xnN[i_ % 2][:, 128 * j_:128 * (j_ + 1)], identb[:]))
                    if j8 == 7:
                        lnT[i][b2] = op("tensor", fn, inc="mT")
                    else:
                        op("tensor", fn)
                wait("vector", "mT", lnT[i][b2])
                dst = bass.AP(xnT, (8 * b2) * 2048 + 128 * i,
                              [[ET * 2048, 128], [2048, 8], [1, 128]])
                src_ = bass.AP(pTx, 0, [[1024, 128], [128, 8], [1, 128]])
                lnEv[i][b2] = op("vector",
                                 lambda E, d_=dst, s_=src_: E.tensor_copy(d_, s_),
                                 inc="mV")

        for i in range(17):
            if i < 16:
                ln_front(i)
            if i >= 1:
                ln_back(i - 1)
        for j in range(ET):
            vcomp = op("vector", lambda E, j_=j: E.tensor_copy(
                xn_own[:, 1024 * j_:1024 * (j_ + 1)],
                xnT[:, 2048 * j_:2048 * j_ + 1024]), inc="mV")
        xn_own_done = vcomp

        if stop == "xn":
            wait("gpsimd", "mV", xn_own_done)
            fin(dmaG(tap_d, xnT[:]), "dG")

        if stop != "xn":
            wait("tensor", "dG", dG_consts)
            wait("tensor", "mV", xn_own_done)
            for i in range(ST):
                for j in range(ET):
                    fn = (lambda E, i_=i, j_=j: E.matmul(
                        pA[:, 8 * i_:8 * (i_ + 1)],
                        xn_own[:, 1024 * j_ + 128 * i_:1024 * j_ + 128 * (i_ + 1)],
                        gw_sb[:, 8 * j_:8 * (j_ + 1)],
                        start=(j_ == 0), stop=(j_ == 15)))
                    if i == ST - 1 and j == ET - 1:
                        gmm = op("tensor", fn, inc="mT")
                    else:
                        op("tensor", fn)
            wait("vector", "dS", dS_c)
            wait("vector", "mT", gmm)
            op("vector", lambda E: E.tensor_copy(logits[:], pA[:, 0:64]))
            glog = vmark()
            if stop == "gat0":
                wait("sync", "mV", glog)
                fin(dmaS(tap_d, logits[:]))
            for i in range(ST if stop != "gat0" else 0):
                sl = slice(8 * i, 8 * (i + 1))
                op("vector", lambda E, s_=sl: E.reduce_max(
                    nm1["gnm"][:], logits[:, s_], axis=AX.X, negate=True))
                vgap()
                vm = vmark()
                wait("scalar", "mV", vm)
                op("scalar", lambda E, s_=sl: E.activation(
                    probs[:, s_], logits[:, s_], AF.Exp, bias=nm1["gnm"][:],
                    accum_out=nm1["gsum"][:]))
                cse = cmark()
                wait("vector", "mC", cse)
                op("vector", lambda E: E.reciprocal(nm1["grin"][:], nm1["gsum"][:]))
                vgap()
                op("vector", lambda E, s_=sl: E.tensor_scalar_mul(
                    probs[:, s_], probs[:, s_], nm1["grin"][:]))
                vgap()
                op("vector", lambda E, s_=sl: E.max(out=top8[:], in_=probs[:, s_]))
                vgap()
                op("vector", lambda E, s_=sl: E.tensor_tensor(
                    mask8[:], probs[:, s_], top8[:, 1:2].to_broadcast([128, 8]),
                    ALU.is_ge))
                vgap()
                op("vector", lambda E, s_=sl: E.tensor_mul(
                    combf[:, s_], probs[:, s_], mask8[:]))
                vgap()
                vcb = op("vector", lambda E, s_=sl: E.tensor_copy(
                    comb_bf[:, s_], combf[:, s_]), inc="mV")
            if stop == "gat1":
                wait("sync", "mV", vcb)
                fin(dmaS(tap_d, combf[:]))
            if stop not in ("gat0", "gat1"):
                wait("tensor", "mV", vcb)
                wait("tensor", "mV", lnEv[15][0])
                ctr = op("tensor", lambda E: E.transpose(
                    pT[0:64, 0:128], comb_bf[:], identb[:]), inc="mT")
                wait("vector", "mT", ctr)
                vct = op("vector", lambda E: E.tensor_copy(
                    combT64[:], pT[0:64, 0:128]), inc="mV")

        if stop == "gating":
            wait("sync", "mV", vcb)
            fin(dmaS(tap_d, combf[:]))


        es_l.close()
        es_f = es_f1 = es_fo = None

        if stop not in ("xn", "gating", "gat0", "gat1"):
            pass
        if stop not in ("xn", "gating", "gat0", "gat1"):
            wqb = [sba(f"wqb{k}", (128, 2048), BF16) for k in range(2)]
            wkb = [sba(f"wkb{k}", (128, 2048), BF16) for k in range(2)]
            wvb = [sba(f"wvb{k}", (128, 2048), BF16) for k in range(2)]
            stage1 = sba("stage1", (128, 2048), BF16)
            stage2 = sba("stage2", (128, 2048), BF16)
            qh = sba("qh", (128, 1024), BF16)
            kh = sba("kh", (128, 2048), BF16)
            vn = sba("vn", (128, 2048), BF16)
            Pb2 = [sba(f"Pb{k}", (128, 2048), BF16) for k in range(2)]
            PTq2 = [sba(f"PTq{k}", (128, 2048), BF16) for k in range(2)]
            rs8 = [[sba(f"rs_{p}_{i}", (128, 1), F32) for i in range(4)]
                   for p in range(2)]
            rsx = [{k: sba(f"rsx_{k}{p}", (128, 1), F32)
                    for k in ("a", "b", "t", "i")} for p in range(2)]

            wld = [0] * H
            wrel = [0] * H
            pvEv = [0] * H
            nheads = 1 if stop in ("head0", "h0t0") else H
            expHist = []
            trPar = [0, 0]
            ptFree = [vct, lnEv[15][1]]
            kropeL = 0
            vevL = 0

            expBank = [0, 0, 0, 0]
            scmmLast = 0

            def preg(i, c):
                if i == 0:
                    return pA[:, 512 * c:512 * (c + 1)]
                if i == 1:
                    return pS[:, 512 * c:512 * (c + 1)]
                return pS[:, 1024 + 512 * c:1024 + 512 * (c + 1)]

            for h in range(nheads):
                if h >= 2:
                    wait("gpsimd", "mT", wrel[h - 2])
                else:
                    wait("gpsimd", "mT", lnT[15][1])
                    wait("gpsimd", "mC", lnC[15])
                for wb_, nm in ((wqb, "wq"), (wkb, "wk"), (wvb, "wv")):
                    wld[h] = dmaG(wb_[h % 2][:],
                                  dd[nm][h].rearrange("p j d -> p (j d)"))

                def proj_mm(wbuf, reg, bofs, bidx):
                    last = None
                    for c in range(2):
                        for j in range(ET):
                            op("tensor", lambda E, c_=c, j_=j, w_=wbuf, r_=reg:
                               E.matmul(
                                   preg(r_, c_),
                                   w_[:, 128 * j_:128 * (j_ + 1)],
                                   (xn_own if bofs == 0 else xnT)[
                                       :, 0:0] if False else
                                   (xn_own[:, 1024 * j_ + 512 * c_:
                                           1024 * j_ + 512 * (c_ + 1)]
                                    if bofs is None else
                                    xnT[:, 2048 * j_ + bofs + 512 * c_:
                                        2048 * j_ + bofs + 512 * (c_ + 1)]),
                                   start=(j_ == 0), stop=False))
                        r = bidx
                        fnb = (lambda E, c_=c, r_=reg, rr=r: E.matmul(
                            preg(r_, c_),
                            bass.AP(bias4, 32 * (rr % 3) * 2816 + 128 * (rr // 3),
                                    [[2816, 1], [1, 128]]),
                            bass.AP(onesb, 32 * (rr % 3) * 512, [[512, 1], [1, 512]]),
                            start=False, stop=True))
                        if c == 1:
                            last = op("tensor", fnb, inc="mT")
                        else:
                            op("tensor", fnb)
                    return last

                def swap_mm(reg, src_ap_fn):
                    last = None
                    for c in range(2):
                        fn = (lambda E, c_=c, r_=reg, sf=src_ap_fn: E.matmul(
                            preg(r_, c_), permb[:], sf(c_), start=True, stop=True))
                        if c == 1:
                            last = op("tensor", fn, inc="mT")
                        else:
                            op("tensor", fn)
                    return last

                wait("tensor", "dG", wld[h])
                if h == 0:
                    wait("tensor", "mV", glog)
                else:
                    wait("tensor", "mV", pvEv[h - 1])
                qmm = proj_mm(wqb[h % 2], 0, None, h)
                wait("tensor", "mC", expBank[0])
                wait("tensor", "mC", expBank[1])
                kmm0 = proj_mm(wkb[h % 2], 1, 0, 16 + h)
                wait("tensor", "mC", expBank[2])
                wait("tensor", "mC", expBank[3])
                kmm1 = proj_mm(wkb[h % 2], 2, 1024, 16 + h)
                wait("scalar", "mT", qmm)
                if h > 0:
                    wait("scalar", "mT", scmmLast)
                qev = op("scalar", lambda E: E.activation(
                    qh[:], pA[:, 0:1024], AF.Identity), inc="mC")
                wait("scalar", "mT", kmm0)
                kev0 = op("scalar", lambda E: E.activation(
                    kh[:, 0:1024], pS[:, 0:1024], AF.Identity), inc="mC")
                wait("scalar", "mT", kmm1)
                kev1 = op("scalar", lambda E: E.activation(
                    kh[:, 1024:2048], pS[:, 1024:2048], AF.Identity), inc="mC")
                wait("tensor", "mC", qev)
                qsw = swap_mm(0, lambda c: qh[:, 512 * c:512 * (c + 1)])
                wait("tensor", "mC", kev0)
                ksw0 = swap_mm(1, lambda c: kh[:, 512 * c:512 * (c + 1)])
                wait("tensor", "mC", kev1)
                ksw1 = swap_mm(2, lambda c: kh[:, 1024 + 512 * c:1536 + 512 * c])
                wait("scalar", "mT", qsw)
                if h > 0:
                    wait("scalar", "mT", wrel[h - 1])
                qsv = op("scalar", lambda E: E.activation(
                    stage2[:, 0:1024], pA[:, 0:1024], AF.Identity), inc="mC")
                wait("scalar", "mT", ksw0)
                ksv0 = op("scalar", lambda E: E.activation(
                    stage2[:, 1024:2048], pS[:, 0:1024], AF.Identity), inc="mC")
                wait("vector", "mT", qsw)
                op("vector", lambda E: E.tensor_tensor(
                    qh[:], qh[:], coskb[:, 0:1024], ALU.mult))
                wait("vector", "mC", qsv)
                op("vector", lambda E: E.tensor_tensor(
                    stage2[:, 0:1024], stage2[:, 0:1024], sinkb[:, 0:1024],
                    ALU.mult))
                op("vector", lambda E: E.tensor_tensor(
                    qh[:], qh[:], stage2[:, 0:1024], ALU.add))
                qrope = op("vector", lambda E: E.tensor_scalar_mul(
                    qh[:], qh[:], ISQD), inc="mV")
                wait("scalar", "mT", ksw1)
                wait("scalar", "mV", qrope)
                ksv1 = op("scalar", lambda E: E.activation(
                    stage2[:, 0:1024], pS[:, 1024:2048], AF.Identity), inc="mC")
                wait("vector", "mT", ksw0)
                op("vector", lambda E: E.tensor_tensor(
                    kh[:, 0:1024], kh[:, 0:1024], coskb[:, 0:1024], ALU.mult))
                wait("vector", "mC", ksv0)
                op("vector", lambda E: E.tensor_tensor(
                    stage2[:, 1024:2048], stage2[:, 1024:2048],
                    sinkb[:, 0:1024], ALU.mult))
                krope0 = op("vector", lambda E: E.tensor_tensor(
                    kh[:, 0:1024], kh[:, 0:1024], stage2[:, 1024:2048], ALU.add),
                    inc="mV")
                wait("vector", "mT", ksw1)
                op("vector", lambda E: E.tensor_tensor(
                    kh[:, 1024:2048], kh[:, 1024:2048], coskb[:, 1024:2048],
                    ALU.mult))
                wait("vector", "mC", ksv1)
                op("vector", lambda E: E.tensor_tensor(
                    stage2[:, 0:1024], stage2[:, 0:1024], sinkb[:, 1024:2048],
                    ALU.mult))
                kropeL = op("vector", lambda E: E.tensor_tensor(
                    kh[:, 1024:2048], kh[:, 1024:2048], stage2[:, 0:1024],
                    ALU.add), inc="mV")
                if stop == "h0q":
                    wait("gpsimd", "mV", qrope)
                    fin(dmaG(tap_d[:, 0:1024], qh[:]), "dG")
                    break
                if stop == "h0k":
                    wait("gpsimd", "mV", kropeL)
                    fin(dmaG(tap_d, kh[:]), "dG")
                    break
                wait("tensor", "mC", qsv)
                vmm0 = proj_mm(wvb[h % 2], 0, 0, 32 + h)
                wait("tensor", "mC", ksv0)
                vmm1 = proj_mm(wvb[h % 2], 1, 1024, 32 + h)
                wait("scalar", "mT", vmm0)
                wait("scalar", "mV", krope0)
                vev0 = op("scalar", lambda E: E.activation(
                    stage2[:, 1024:2048], pA[:, 0:1024], AF.Identity), inc="mC")
                wait("scalar", "mT", vmm1)
                wait("scalar", "mV", kropeL)
                vev1 = op("scalar", lambda E: E.activation(
                    stage2[:, 0:1024], pS[:, 0:1024], AF.Identity), inc="mC")
                vevL = vev1
                vnev = [0, 0]
                for b2 in range(2):
                    pTx = pT if b2 == 0 else pT2
                    wait("tensor", "mC", vev0 if b2 == 0 else vev1)
                    wait("tensor", "mV", ptFree[b2])
                    for j8 in range(8):
                        scol = (1024 + 128 * j8) if b2 == 0 else (128 * j8)
                        fn = (lambda E, s_=scol, j8_=j8, px=pTx: E.transpose(
                            px[:, 128 * j8_:128 * (j8_ + 1)],
                            stage2[:, s_:s_ + 128], identb[:]))
                        if j8 == 7:
                            vtr = op("tensor", fn, inc="mT")
                        else:
                            op("tensor", fn)
                    wait("vector", "mT", vtr)
                    vnev[b2] = op("vector", lambda E, b2_=b2: E.tensor_copy(
                        vn[:, 1024 * b2_:1024 * (b2_ + 1)],
                        (pT if b2_ == 0 else pT2)[:]), inc="mV")
                ptFree = [vnev[0], vnev[1]]
                wrel[h] = vtr
                if stop == "h0v":
                    wait("gpsimd", "mV", vnev[1])
                    fin(dmaG(tap_d, vn[:]), "dG")
                    break

                pvE = [0] * ST
                pend = None

                def attn_tail(t, psc):
                    nonlocal ptFree
                    par = t % 2
                    L = 128 * (t + 1)
                    wait("tensor", "mV", psc)
                    ptEv = [0, 0]
                    for b2 in range(2):
                        pTx = pT if b2 == 0 else pT2
                        wait("tensor", "mV", ptFree[b2])
                        for u in range(t + 1):
                            src_col = (L if b2 else 0) + 128 * u
                            fn = (lambda E, s_=src_col, u_=u, px=pTx, p_=par:
                                  E.transpose(
                                      px[:, 128 * u_:128 * (u_ + 1)],
                                      Pb2[p_][:, s_:s_ + 128], identb[:]))
                            if u == t:
                                tr_ = op("tensor", fn, inc="mT")
                            else:
                                op("tensor", fn)
                        wait("vector", "mT", tr_)
                        ptEv[b2] = op("vector", lambda E, t_=t, b2_=b2, p_=par:
                                      E.tensor_copy(
                                          PTq2[p_][:, 1024 * b2_:
                                                    1024 * b2_ + 128 * (t_ + 1)],
                                          (pT if b2_ == 0 else pT2)
                                          [:, 0:128 * (t_ + 1)]), inc="mV")
                    ptFree = [ptEv[0], ptEv[1]]
                    trPar[par] = tr_
                    pvreg = 512 * par
                    wait("tensor", "mV", ptEv[1])
                    if t == 0:
                        wait("tensor", "mC", vevL)
                    if t >= 2:
                        wait("tensor", "mV", pvE[t - 2])
                    elif h > 0:
                        wait("tensor", "mV", pvEv[h - 1])
                    first = True
                    for b2 in range(2):
                        for u in range(t + 1):
                            kslot = u if b2 == 0 else 8 + u
                            vcol = 128 * u if b2 == 0 else 1024 + 128 * u
                            last = (b2 == 1 and u == t)
                            fn = (lambda E, v_=vcol, k_=kslot, p_=pvreg, f_=first,
                                  l_=last, pp=par: E.matmul(
                                      pA[:, p_:p_ + 128], vn[:, v_:v_ + 128],
                                      PTq2[pp][:, 128 * k_:128 * (k_ + 1)],
                                      start=f_, stop=l_))
                            if last:
                                pvt_ = op("tensor", fn, inc="mT")
                            else:
                                op("tensor", fn)
                            first = False
                    wait("vector", "mT", pvt_)
                    pvE[t] = op("vector", lambda E, h_=h, t_=t, p_=pvreg:
                                E.tensor_copy(
                                    attn_out[:, 1024 * h_ + 128 * t_:
                                             1024 * h_ + 128 * (t_ + 1)],
                                    pA[:, p_:p_ + 128]), inc="mV")

                for t in range(ST):
                    L = 128 * (t + 1)
                    par = t % 2
                    chunks = []
                    for b2, base in ((0, 0), (1, 1024)):
                        off = 0
                        while off < L:
                            cw = min(512, L - off)
                            chunks.append((base, off, cw, b2))
                            off += cw
                    nch = len(chunks)
                    for ci, (base, off, cw, b2) in enumerate(chunks):
                        gci = len(expHist)
                        bank = gci % 4
                        wait("tensor", "mC", expBank[bank])
                        if t == 0 and ci == 0:
                            wait("tensor", "mV", qrope)
                            wait("tensor", "mV", kropeL)
                            wait("tensor", "mC", vevL)
                        scmm = op("tensor",
                                  lambda E, t_=t, b_=base, o_=off, w_=cw, bk=bank:
                                  E.matmul(
                                      pS[:, 512 * bk:512 * bk + w_],
                                      qh[:, 128 * t_:128 * (t_ + 1)],
                                      kh[:, b_ + o_:b_ + o_ + w_],
                                      start=True, stop=True), inc="mT")
                        scmmLast = scmm
                        ismask = (off + cw == L)
                        if ismask:
                            mlocal = 128 * t - off
                            wait("vector", "mT", scmm)
                            op("vector", lambda E, b2_=b2, bk=bank, ml=mlocal:
                               E.tensor_tensor(
                                   pS[:, 512 * bk + ml:512 * bk + ml + 128],
                                   pS[:, 512 * bk + ml:512 * bk + ml + 128],
                                   masks[:, 128 * b2_:128 * b2_ + 128],
                                   ALU.add))
                            mrd = vmark()
                            wait("scalar", "mV", mrd)
                        else:
                            wait("scalar", "mT", scmm)
                        if ci == 0:
                            wait("scalar", "mT", trPar[par])
                        outc = (L + off) if b2 else off
                        expi = op("scalar",
                                  lambda E, p_=par, oc=outc, w_=cw, bk=bank,
                                  ci_=ci: E.activation(
                                      Pb2[p_][:, oc:oc + w_],
                                      pS[:, 512 * bk:512 * bk + w_], AF.Exp,
                                      accum_out=rs8[p_][ci_][:]), inc="mC")
                        expHist.append(expi)
                        expBank[bank] = expi
                    wait("vector", "mC", expHist[-1])
                    rx = rsx[par]
                    if nch == 2:
                        op("vector", lambda E, p_=par: E.tensor_add(
                            rx["t"][:], rs8[p_][0][:], rs8[p_][1][:]))
                    else:
                        op("vector", lambda E, p_=par: E.tensor_add(
                            rx["a"][:], rs8[p_][0][:], rs8[p_][1][:]))
                        op("vector", lambda E, p_=par: E.tensor_add(
                            rx["b"][:], rs8[p_][2][:], rs8[p_][3][:]))
                        vgap()
                        op("vector", lambda E: E.tensor_add(
                            rx["t"][:], rx["a"][:], rx["b"][:]))
                    vgap()
                    op("vector", lambda E: E.reciprocal(rx["i"][:], rx["t"][:]))
                    vgap()
                    psc = op("vector", lambda E, p_=par, L_=L: E.tensor_scalar_mul(
                        Pb2[p_][:, 0:2 * L_], Pb2[p_][:, 0:2 * L_], rx["i"][:]),
                        inc="mV")
                    if pend is not None:
                        attn_tail(*pend)
                    pend = (t, psc)
                attn_tail(*pend)
                pvEv[h] = pvE[ST - 1]

            attn_done_V = pvEv[nheads - 1]

            if stop == "head0":
                wait("gpsimd", "mV", qrope)
                dmaG(tap_d[:, 0:1024], qh[:])
                wait("gpsimd", "mV", kropeL)
                dmaG(tap_d[:, 2048:4096], kh[:])
                wait("gpsimd", "mV", vnev[1])
                dmaG(tap_d[:, 4096:6144], vn[:])
                wait("gpsimd", "mV", attn_done_V)
                fin(dmaG(tap_d[:, 6144:7168], attn_out[:, 0:1024]), "dG")
            if stop == "attn":
                wait("gpsimd", "mV", attn_done_V)
                fin(dmaG(tap_d, attn_out[:]), "dG")

        if stop in (None, "oproj", "oproj0", "ffn"):
            es_a.close()
            bT, bV, bC = C["mT"], C["mV"], C["mC"]
            for eng in ENGINES:
                wait(eng, "mT", bT)
                wait(eng, "mV", bV)
                wait(eng, "mC", bC)

            es_f = ExitStack()
            sbf_ = lambda n_, s_, d_: es_f.enter_context(
                nc.sbuf_tensor(n_, list(s_), d_))
            psf_ = lambda n_, s_, d_: es_f.enter_context(
                nc.psum_tensor(n_, list(s_), d_))
            acc = sbf_("acc", (128, HT * 1024), F32)
            pW = psf_("pW", (128, 1024), F32)
            pX = psf_("pX", (128, 1024), F32)
            pY = psf_("pY", (128, 1024), F32)
            pZ = psf_("pZ", (128, 1024), F32)
            es_fo = ExitStack()
            wob = [es_fo.enter_context(nc.sbuf_tensor(f"wob{k}", [128, 2048], BF16))
                   for k in range(2)]

            woL = {}
            woR = {}
            oEv = [0] * HT
            woL[0] = dmaG(wob[0][:], dd["wo"][0].rearrange("p j d -> p (j d)"))
            woL[1] = dmaG(wob[1][:], dd["wo"][1].rearrange("p j d -> p (j d)"))
            for fb in range(HT if stop != "oproj0" else 16):
                pP = pW if fb % 2 == 0 else pX
                wait("tensor", "dG", woL[fb])
                if fb >= 2:
                    wait("tensor", "mC", oEv[fb - 2])
                for c in range(2):
                    for j in range(ET):
                        op("tensor", lambda E, f_=fb, c_=c, j_=j, p_=pP: E.matmul(
                            p_[:, 512 * c_:512 * (c_ + 1)],
                            wob[f_ % 2][:, 128 * j_:128 * (j_ + 1)],
                            attn_out[:, 1024 * j_ + 512 * c_:
                                     1024 * j_ + 512 * (c_ + 1)],
                            start=(j_ == 0), stop=False))
                    fnb = (lambda E, f_=fb, c_=c, p_=pP: E.matmul(
                        p_[:, 512 * c_:512 * (c_ + 1)],
                        bass.AP(bias4,
                                32 * ((48 + f_) % 3) * 2816 + 128 * ((48 + f_) // 3),
                                [[2816, 1], [1, 128]]),
                        bass.AP(onesb, 32 * ((48 + f_) % 3) * 512,
                                [[512, 1], [1, 512]]),
                        start=False, stop=True))
                    if c == 1:
                        woR[fb] = op("tensor", fnb, inc="mT")
                    else:
                        op("tensor", fnb)
                if fb + 2 < HT:
                    wait("gpsimd", "mT", woR[fb])
                    woL[fb + 2] = dmaG(wob[fb % 2][:],
                                       dd["wo"][fb + 2].rearrange("p j d -> p (j d)"))
                wait("scalar", "mT", woR[fb])
                oEv[fb] = op("scalar", lambda E, f_=fb, p_=pP: E.activation(
                    acc[:, 1024 * f_:1024 * (f_ + 1)], p_[:], AF.Identity),
                    inc="mC")
            o_done = oEv[HT - 1] if stop != "oproj0" else oEv[0]
            if stop == "oproj0":
                wait("sync", "mC", o_done)
                fin(dmaS(tap_d, acc[:, 0:1024]))

            if stop == "oproj":
                wait("sync", "mC", o_done)
                fin(dmaS(tap_d, acc[:]))

        if stop in (None, "ffn"):
            es_fo.close()
            es_f1 = ExitStack()
            sbf1 = lambda n_, s_, d_: es_f1.enter_context(
                nc.sbuf_tensor(n_, list(s_), d_))
            gub = [sbf1(f"gub{k}", (128, 4096), BF16) for k in range(2)]
            dnb = [sbf1(f"dnb{k}", (128, 2816), BF16) for k in range(2)]
            act = sbf1("act", (128, FT * 1024), BF16)
            bc_sb = sbf1("bc_sb", (128, 1024), BF16)
            t1_sb = sbf1("t1_sb", (128, 1024), BF16)
            combT2 = sbf1("combT2", (1, 8192), BF16)
            wait("gpsimd", "mT", woR[HT - 1])
            wait("gpsimd", "mV", vct)
            for e in range(8):
                dG_comb = dmaG(
                    bass.AP(combT2, e * 1024, [[8192, 1], [128, 8], [1, 128]]),
                    bass.AP(combT64, e * 128, [[8 * 128, 8], [1, 128]]))
            guL = {}
            guR = {}
            dnL = {}
            dnR = {}
            guL[0] = dmaG(gub[0][:], dd["guw"][0, 0])
            guL[1] = dmaG(gub[1][:], dd["guw"][0, 1])
            dnL[0] = dmaG(dnb[0][:], dd["dnw"][0, 0])
            dnL[1] = dmaG(dnb[1][:], dd["dnw"][0, 1])

            dnAdd = 0
            bcV = 0
            guEvLast = 0
            for e in range(NE):
                gated = e < 8
                if gated:
                    wait("tensor", "dG", dG_comb)
                    if e > 0:
                        wait("tensor", "mV", bcV)
                        wait("tensor", "mV", guEvLast)
                    else:
                        wait("tensor", "mC", o_done)
                    for c in range(2):
                        fn = (lambda E, e_=e, c_=c: E.matmul(
                            pY[:, 512 * c_:512 * (c_ + 1)], onesb[0:1, 0:128],
                            combT2[0:1, 1024 * e_ + 512 * c_:
                                   1024 * e_ + 512 * (c_ + 1)],
                            start=True, stop=True))
                        if c == 1:
                            bcT = op("tensor", fn, inc="mT")
                        else:
                            op("tensor", fn)
                    wait("vector", "mT", bcT)
                    bcV = op("vector", lambda E: E.tensor_copy(bc_sb[:], pY[:]),
                             inc="mV")
                guEv = [0] * FT
                for jg in range(FT):
                    k = e * FT + jg
                    pair = (pW, pX) if jg % 2 == 0 else (pY, pZ)
                    wait("tensor", "dG", guL[k])
                    if jg >= 2:
                        wait("tensor", "mV", guEv[jg - 2])
                    elif e == 0:
                        wait("tensor", "mC", o_done)
                    else:
                        wait("tensor", "mV", dnAdd)
                        if jg == 1 and gated:
                            wait("tensor", "mV", bcV)
                    for half2 in range(2):
                        pP = pair[half2]
                        for c in range(2):
                            for j in range(ET):
                                fn = (lambda E, k_=k, h2=half2, c_=c, j_=j, p_=pP:
                                      E.matmul(
                                          p_[:, 512 * c_:512 * (c_ + 1)],
                                          gub[k_ % 2][:, 2048 * h2 + 128 * j_:
                                                      2048 * h2 + 128 * (j_ + 1)],
                                          xn_own[:, 1024 * j_ + 512 * c_:
                                                 1024 * j_ + 512 * (c_ + 1)],
                                          start=(j_ == 0), stop=(j_ == 15)))
                                if half2 == 1 and c == 1 and j == ET - 1:
                                    guR[k] = op("tensor", fn, inc="mT")
                                else:
                                    op("tensor", fn)
                    if k + 2 < NE * FT:
                        e2, jg2 = divmod(k + 2, FT)
                        wait("gpsimd", "mT", guR[k])
                        guL[k + 2] = dmaG(gub[k % 2][:], dd["guw"][e2, jg2])
                    wait("scalar", "mT", guR[k])
                    if jg >= 1:
                        wait("scalar", "mV", guEv[jg - 1])
                    elif e > 0:
                        wait("scalar", "mV", guEvPrevLast)
                    sil = op("scalar", lambda E, p_=pair[0]: E.activation(
                        t1_sb[:], p_[:], AF.Silu), inc="mC")
                    wait("vector", "mC", sil)
                    if gated:
                        wait("vector", "mV", bcV)
                    op("vector", lambda E, p_=pair[1]: E.tensor_tensor(
                        t1_sb[:], t1_sb[:], p_[:], ALU.mult))
                    if gated:
                        guEv[jg] = op("vector", lambda E, g_=jg: E.tensor_tensor(
                            act[:, 1024 * g_:1024 * (g_ + 1)], t1_sb[:], bc_sb[:],
                            ALU.mult), inc="mV")
                    else:
                        guEv[jg] = op("vector", lambda E, g_=jg: E.tensor_copy(
                            act[:, 1024 * g_:1024 * (g_ + 1)], t1_sb[:]), inc="mV")
                guEvPrevLast = guEv[FT - 1]
                guEvLast = guEv[FT - 1]
                dnAddPrev = [guEv[FT - 1], guEv[FT - 2]]
                for hb in range(HT):
                    c4, hb2 = hb // 2, hb % 2
                    m = e * 8 + c4
                    pP = pW if hb % 2 == 0 else pX
                    wait("tensor", "dG", dnL[m])
                    wait("tensor", "mV", guEvLast)
                    if hb >= 2:
                        wait("tensor", "mV", dnAddPrev[hb % 2])
                    for f in range(FT):
                        for c in range(2):
                            fn = (lambda E, m_=m, h2=hb2, f_=f, c_=c, p_=pP:
                                  E.matmul(
                                      p_[:, 512 * c_:512 * (c_ + 1)],
                                      dnb[m_ % 2][:, 1408 * h2 + 128 * f_:
                                                  1408 * h2 + 128 * (f_ + 1)],
                                      act[:, 1024 * f_ + 512 * c_:
                                          1024 * f_ + 512 * (c_ + 1)],
                                      start=(f_ == 0), stop=(f_ == FT - 1)))
                            if f == FT - 1 and c == 1:
                                dmm = op("tensor", fn, inc="mT")
                            else:
                                op("tensor", fn)
                    if hb2 == 1:
                        dnR[m] = dmm
                        if m + 2 < NE * 8:
                            e2, c42 = divmod(m + 2, 8)
                            wait("gpsimd", "mT", dnR[m])
                            dnL[m + 2] = dmaG(dnb[m % 2][:], dd["dnw"][e2, c42])
                    wait("vector", "mT", dmm)
                    dnAdd = op("vector", lambda E, hb_=hb, p_=pP: E.tensor_tensor(
                        acc[:, 1024 * hb_:1024 * (hb_ + 1)],
                        acc[:, 1024 * hb_:1024 * (hb_ + 1)], p_[:], ALU.add),
                        inc="mV")
                    dnAddPrev[hb % 2] = dnAdd
            ffn_done = dnAdd

            if stop == "ffn":
                wait("sync", "mV", ffn_done)
                fin(dmaS(tap_d, acc[:]))

        if stop is None:
            es_f1.close()
            xrb = []
            outb = []
            for k in range(2):
                xrb.append(es_f.enter_context(
                    nc.sbuf_tensor(f"xrb{k}", [128, 2048], F32)))
                outb.append(es_f.enter_context(
                    nc.sbuf_tensor(f"outb{k}", [128, 2048], F32)))
            wait("sync", "mV", ffn_done)
            finV = [0] * ST
            stO = [0] * ST
            for i in range(ST):
                if i >= 2:
                    wait("sync", "mV", finV[i - 2])
                xrl = dmaS(xrb[i % 2][:], dd["x"][128 * i:128 * (i + 1), :])
                pPair = (pY, pZ) if i % 2 == 0 else (pW, pX)
                wait("tensor", "mV", ffn_done)
                if i >= 2:
                    wait("tensor", "mV", finV[i - 2])
                trF = [0, 0]
                for g in range(2):
                    pP = pPair[g]
                    for b4 in range(8):
                        et = 8 * g + b4
                        fn = (lambda E, e_=et, b_=b4, p_=pP, i_=i: E.transpose(
                            p_[:, 128 * b_:128 * (b_ + 1)],
                            acc[:, 1024 * e_ + 128 * i_:1024 * e_ + 128 * (i_ + 1)],
                            identf[:]))
                        if b4 == 7:
                            trF[g] = op("tensor", fn, inc="mT")
                        else:
                            op("tensor", fn)
                wait("vector", "mT", trF[1])
                wait("vector", "dS", xrl)
                if i >= 2:
                    wait("vector", "dS", stO[i - 2])
                for g in range(2):
                    pP = pPair[g]
                    for c in range(2):
                        fn = (lambda E, g_=g, c_=c, p_=pP, i_=i: E.tensor_tensor(
                            outb[i_ % 2][:, 1024 * g_ + 512 * c_:
                                         1024 * g_ + 512 * (c_ + 1)],
                            xrb[i_ % 2][:, 1024 * g_ + 512 * c_:
                                        1024 * g_ + 512 * (c_ + 1)],
                            p_[:, 512 * c_:512 * (c_ + 1)], ALU.add))
                        if g == 1 and c == 1:
                            finV[i] = op("vector", fn, inc="mV")
                        else:
                            op("vector", fn)
                wait("sync", "mV", finV[i])
                stO[i] = dmaS(out_d[i], outb[i % 2][:])
            fin(stO[ST - 1])

        if es_f1 is not None:
            es_f1.close()
        if es_fo is not None:
            es_fo.close()
        if es_f is not None:
            es_f.close()
        es_a.close()

        block = es.enter_context(nc.Block())

        @block.tensor
        def _(E):
            bld.replay("tensor", E)

        @block.vector
        def _(E):
            bld.replay("vector", E)

        @block.scalar
        def _(E):
            bld.replay("scalar", E)

        @block.sync
        def _(E):
            bld.replay("sync", E)

        @block.gpsimd
        def _(E):
            bld.replay("gpsimd", E)

    nc.finalize()
    return nc


def kernel(**inputs):
    global _LAST
    shared = _prep_shared(inputs)
    cosg, sing = _rope_tables()
    nc = _build(_STOP)
    in_maps = []
    for c in range(8):
        b, half = c // 2, c % 2
        m = dict(shared)
        m.update(_prep_core(inputs, b, half, cosg, sing))
        in_maps.append(m)
    res = run_bass_kernel_spmd(nc, in_maps, core_ids=list(range(8)))
    out = np.zeros((4, 2048, 2048), np.float32)
    _LAST = {}
    for c in range(8):
        b, half = c // 2, c % 2
        own, _ = _core_perm(half)
        out[b][own] = np.asarray(res.results[c]["out"]).reshape(1024, 2048)
        if _STOP is not None:
            _LAST[c] = np.asarray(res.results[c]["tap"])
    return out


# version 07
# speedup vs baseline: 1.1121x; 1.0024x over previous
import numpy as np
from contextlib import ExitStack

import concourse.bass as bass
import concourse.mybir as mybir
from concourse.bass_utils import run_bass_kernel_spmd

F32 = mybir.dt.float32
BF16 = mybir.dt.bfloat16
AF = mybir.ActivationFunctionType
ALU = mybir.AluOpType
AX = mybir.AxisListType

E2 = 2048
H = 16
ET = 16
SOWN = 1024
ST = 8
NE = 10
FT = 11
HT = 16
NEG = -1.0e30
ISQD = 1.0 / float(np.sqrt(128.0))

ENGINES = ("tensor", "vector", "scalar", "sync", "gpsimd")

_STOP = None
_LAST = {}


class _Bld:
    def __init__(self, semnames):
        self.streams = {e: [] for e in ENGINES}
        self.C = {n: 0 for n in semnames}
        self.sems = {}
        self._trun = 0

    def raw(self, eng, fn):
        self.streams[eng].append(fn)

    def wait(self, eng, sem, val):
        if val <= 0:
            return
        assert val <= self.C[sem], f"wait {sem}>={val} but only {self.C[sem]} planned"
        if eng == "tensor":
            self._trun = 0
        self.raw(eng, lambda E, s=sem, v=val: E.wait_ge(self.sems[s], v))

    def op(self, eng, fn, inc=None, n=1):
        if eng == "tensor":
            self._trun += 1
            if self._trun >= 4:
                self.streams[eng].append(lambda E: E.nop())
                self._trun = 0
        if inc is None:
            self.raw(eng, fn)
            return None
        self.C[inc] += n
        self.raw(eng, lambda E, f=fn, s=inc, k=n: f(E).then_inc(self.sems[s], k))
        return self.C[inc]

    def replay(self, eng_name, E):
        for fn in self.streams[eng_name]:
            fn(E)


def _lay_qkv(W):
    return np.ascontiguousarray(
        W.T.reshape(16, 128, 16, 128).transpose(2, 1, 0, 3))


def _prep_shared(inp):
    out = {}
    out["wq"] = _lay_qkv(np.asarray(inp["L0_attn_Wq"], np.float32))
    out["wk"] = _lay_qkv(np.asarray(inp["L0_attn_Wk"], np.float32))
    out["wv"] = _lay_qkv(np.asarray(inp["L0_attn_Wv"], np.float32))
    out["wo"] = _lay_qkv(np.asarray(inp["L0_attn_Wo"], np.float32))
    bflat = np.concatenate(
        [np.asarray(inp[k], np.float32) for k in
         ("L0_attn_bq", "L0_attn_bk", "L0_attn_bv", "L0_attn_bo")]).reshape(64, 128)
    b4 = np.zeros((3, 22, 128), np.float32)
    for r in range(64):
        b4[r % 3, r // 3] = bflat[r]
    out["brows"] = b4.reshape(3, 2816)
    out["gw"] = np.ascontiguousarray(
        np.asarray(inp["L0_ffn_gate_weight"], np.float32).T
        .reshape(16, 128, 8).transpose(1, 0, 2))
    gus = []
    for e in range(8):
        Wt = np.asarray(inp["L0_ffn_gate_up_weights"][e], np.float32).T
        gus.append(Wt.reshape(16, 128, 2, 11, 128)
                   .transpose(3, 1, 2, 0, 4).reshape(11, 128, 4096))
    St = np.asarray(inp["L0_ffn_shared_gate_up"], np.float32).T
    sg = (St.reshape(16, 128, 2, 2, 11, 128)
          .transpose(3, 4, 1, 2, 0, 5).reshape(2, 11, 128, 4096))
    out["guw"] = np.ascontiguousarray(
        np.concatenate([np.stack(gus, 0), sg], axis=0))
    dns = []
    for e in range(8):
        Dt = np.asarray(inp["L0_ffn_down_weights"][e], np.float32).T
        dns.append(Dt.reshape(11, 128, 8, 2, 128)
                   .transpose(2, 1, 3, 0, 4).reshape(8, 128, 2816))
    St2 = np.asarray(inp["L0_ffn_shared_down"], np.float32).T
    sd2 = (St2.reshape(2, 11, 128, 8, 2, 128)
           .transpose(0, 3, 2, 4, 1, 5).reshape(2, 8, 128, 2816))
    out["dnw"] = np.ascontiguousarray(
        np.concatenate([np.stack(dns, 0), sd2], axis=0))
    out["ident"] = np.eye(128, dtype=np.float32)
    perm = np.zeros((128, 128), np.float32)
    for j in range(128):
        perm[j + 1 if j % 2 == 0 else j - 1, j] = 1.0
    out["perm"] = perm
    out["ones1"] = np.ones((128, 512), np.float32)
    return out


def _rope_tables():
    j = np.arange(128)
    theta = 10000.0 ** (-(2.0 * (j // 2)) / 128.0)
    ang = np.outer(theta, np.arange(2048).astype(np.float64))
    cosg = np.cos(ang).astype(np.float32)
    sign = np.where(j % 2 == 0, -1.0, 1.0)
    sing = (np.sin(ang) * sign[:, None]).astype(np.float32)
    return cosg, sing


def _core_perm(half):
    own = np.concatenate(
        [np.arange(256 * t + 128 * half, 256 * t + 128 * half + 128)
         for t in range(8)])
    other = np.concatenate(
        [np.arange(256 * t + 128 * (1 - half), 256 * t + 128 * (1 - half) + 128)
         for t in range(8)])
    return own, np.concatenate([own, other])


def _prep_core(inp, b, half, cosg, sing):
    own, perm = _core_perm(half)
    d = {}
    d["x"] = np.ascontiguousarray(np.asarray(inp["L0_attn_x"], np.float32)[b][perm])
    d["cosk"] = np.ascontiguousarray(cosg[:, perm])
    d["sink"] = np.ascontiguousarray(sing[:, perm])
    m = np.zeros((128, 256), np.float32)
    tri = np.where(np.arange(128)[None, :] <= np.arange(128)[:, None], 0.0, NEG)
    m[:, 0:128] = tri.astype(np.float32)
    if half == 0:
        m[:, 128:256] = NEG
    d["mask"] = m
    return d


def _build(stop=None):
    nc = bass.Bass("TRN2", target_bir_lowering=False)
    dd = {}

    def din(name, shape):
        dd[name] = nc.dram_tensor(name, list(shape), F32, kind="ExternalInput").ap()

    din("x", (2048, 2048))
    din("wq", (16, 128, 16, 128)); din("wk", (16, 128, 16, 128))
    din("wv", (16, 128, 16, 128)); din("wo", (16, 128, 16, 128))
    din("brows", (3, 2816))
    din("gw", (128, 16, 8))
    din("guw", (NE, FT, 128, 4096))
    din("dnw", (NE, 8, 128, 2816))
    din("cosk", (128, 2048)); din("sink", (128, 2048))
    din("mask", (128, 256))
    din("ident", (128, 128)); din("perm", (128, 128)); din("ones1", (128, 512))
    out_d = nc.dram_tensor("out", [ST, 128, 2048], F32, kind="ExternalOutput").ap()
    tap_shapes = {"xn": [128, ET * 2048], "gating": [128, 64],
                  "gat0": [128, 64], "gat1": [128, 64], "h0q": [128, 1024],
                  "h0k": [128, 2048], "h0v": [128, 2048], "h0t0": [128, 2048],
                  "head0": [128, 5 * 2048], "attn": [128, H * 1024],
                  "oproj": [128, HT * 1024], "oproj0": [128, 1024],
                  "ffn": [128, HT * 1024]}
    tap_d = None
    if stop is not None:
        tap_d = nc.dram_tensor("tap", tap_shapes[stop], F32,
                               kind="ExternalOutput").ap()

    SEMS = ("dG", "dS", "mT", "mV", "mC")
    bld = _Bld(SEMS)
    op, wait, C = bld.op, bld.wait, bld.C
    es = ExitStack()

    with es:
        xn_own = es.enter_context(nc.sbuf_tensor("xn_own", [128, ET * 1024], BF16))
        ao_w = 2 * 1024 if stop == "h0t0" else H * 1024
        attn_out = es.enter_context(nc.sbuf_tensor("attn_out", [128, ao_w], BF16))

        combT64 = es.enter_context(nc.sbuf_tensor("combT64", [64, 128], BF16))
        bias4 = es.enter_context(nc.sbuf_tensor("bias4", [128, 2816], BF16))
        identb = es.enter_context(nc.sbuf_tensor("identb", [128, 128], BF16))
        identf = es.enter_context(nc.sbuf_tensor("identf", [128, 128], F32))
        permb = es.enter_context(nc.sbuf_tensor("permb", [128, 128], BF16))
        onesb = es.enter_context(nc.sbuf_tensor("onesb", [128, 512], BF16))
        nm1 = {}
        for k in ("stdv", "vare", "rstd", "negmrs", "negmax", "negmaxs", "rsum",
                  "rsum2", "rsumT", "rinv", "gnm", "gsum", "grin", "bqc", "bkc",
                  "bvc", "boc"):
            nm1[k] = es.enter_context(nc.sbuf_tensor("v_" + k, [128, 1], F32))
        dumv = es.enter_context(nc.sbuf_tensor("dumv", [128, 1], F32))
        dums = es.enter_context(nc.sbuf_tensor("dums", [128, 1], F32))
        for s_ in SEMS:
            bld.sems[s_] = es.enter_context(nc.semaphore("sem_" + s_))

        def vgap():
            op("vector", lambda E: E.tensor_copy(dumv[:], identf[:, 0:1]))

        def vmark():
            return op("vector", lambda E: E.tensor_copy(
                dumv[:], identf[:, 0:1]), inc="mV")

        def cmark():
            return op("scalar", lambda E: E.copy(dums[:], identf[:, 0:1]),
                      inc="mC")

        def dmaG(out_ap, in_ap):
            return op("gpsimd", lambda E, o=out_ap, i=in_ap: E.dma_start(out=o, in_=i),
                      inc="dG", n=16)

        def dmaS(out_ap, in_ap):
            return op("sync", lambda E, o=out_ap, i=in_ap: E.dma_start(out=o, in_=i),
                      inc="dS", n=16)

        def fin(val, sem="dS"):
            wait("sync", sem, val)

        es_a = ExitStack()
        sba = lambda n_, s_, d_: es_a.enter_context(nc.sbuf_tensor(n_, list(s_), d_))
        psa = lambda n_, s_, d_: es_a.enter_context(nc.psum_tensor(n_, list(s_), d_))
        xnT = sba("xnT", (128, ET * 2048), BF16)
        coskb = sba("coskb", (128, 2048), BF16)
        sinkb = sba("sinkb", (128, 2048), BF16)

        masks = sba("masks", (128, 256), BF16)
        gw_sb = sba("gw_sb", (128, 128), BF16)
        logits = sba("logits", (128, 64), F32)
        probs = sba("probs", (128, 64), F32)
        top8 = sba("top8", (128, 8), F32)
        mask8 = sba("mask8", (128, 8), F32)
        combf = sba("combf", (128, 64), F32)
        comb_bf = sba("comb_bf", (128, 64), BF16)
        pA = psa("pA", (128, 1024), F32)
        pS = psa("pS", (128, 2048), F32)
        pT = psa("pT", (128, 1024), BF16)
        pT2 = psa("pT2", (128, 1024), BF16)

        dmaG(identb[:], dd["ident"])
        dmaG(permb[:], dd["perm"])
        dmaG(onesb[:], dd["ones1"])
        dmaG(coskb[:], dd["cosk"])
        dmaG(sinkb[:], dd["sink"])
        for p4 in range(3):
            dmaG(bass.AP(bias4, 32 * p4 * 2816, [[2816, 1], [1, 2816]]),
                 dd["brows"][p4:p4 + 1, :])
        dG_consts = dmaG(gw_sb[:].rearrange("p (j c) -> p j c", j=16), dd["gw"])
        dmaG(masks[:], dd["mask"])
        dS_c = dmaS(identf[:], dd["ident"])

        es_l = ExitStack()
        sbl = lambda n_, s_, d_: es_l.enter_context(nc.sbuf_tensor(n_, list(s_), d_))
        xbuf = [sbl(f"xbuf{k}", (128, 2048), F32) for k in range(2)]
        xnN = [sbl(f"xnN{k}", (128, 2048), BF16) for k in range(2)]
        sqs = sbl("sqs", (128, 2048), F32)
        lnS = [{k: sbl(f"ln_{k}{p}", (128, 1), F32)
                for k in ("s1", "s2", "mean", "varb", "vare", "stdv", "rstd",
                          "negmrs")} for p in range(2)]

        lnC = [0] * 16
        lnT = [[0, 0] for _ in range(16)]
        lnEv = [[0, 0] for _ in range(16)]
        lnXld = [0] * 16
        lnVng = [0] * 16

        def ln_front(i):
            p = lnS[i % 2]
            if i >= 2:
                wait("sync", "mC", lnC[i - 2])
            lnXld[i] = dmaS(xbuf[i % 2][:], dd["x"][128 * i:128 * (i + 1), :])
            wait("vector", "dS", lnXld[i])
            op("vector", lambda E, i_=i, p_=p: E.reduce_sum(
                p_["s1"][:], xbuf[i_ % 2][:], axis=AX.X))
            wait("scalar", "dS", lnXld[i])
            op("scalar", lambda E, i_=i, p_=p: E.activation(
                sqs[:], xbuf[i_ % 2][:], AF.Square, accum_out=p_["s2"][:]))
            csq = cmark()
            wait("vector", "mC", csq)
            op("vector", lambda E, p_=p: E.tensor_scalar_mul(
                p_["varb"][:], p_["s2"][:], 1.0 / 2048.0))
            op("vector", lambda E, p_=p: E.scalar_tensor_tensor(
                p_["vare"][:], p_["s1"][:], 1.0 / (2048.0 * 2048.0), p_["s1"][:],
                ALU.mult, ALU.mult))
            op("vector", lambda E, p_=p: E.tensor_scalar_mul(
                p_["mean"][:], p_["s1"][:], 1.0 / 2048.0))
            op("vector", lambda E, p_=p: E.scalar_tensor_tensor(
                p_["varb"][:], p_["vare"][:], -1.0, p_["varb"][:],
                ALU.mult, ALU.add))
            vgap()
            op("vector", lambda E, p_=p: E.tensor_scalar_add(
                p_["vare"][:], p_["varb"][:], 1.0e-5))
            vst = vmark()
            wait("scalar", "mV", vst)
            op("scalar", lambda E, p_=p: E.activation(
                p_["stdv"][:], p_["vare"][:], AF.Sqrt))
            cst = cmark()
            wait("vector", "mC", cst)
            op("vector", lambda E, p_=p: E.reciprocal(
                p_["rstd"][:], p_["stdv"][:]))
            vgap()
            op("vector", lambda E, p_=p: E.scalar_tensor_tensor(
                p_["negmrs"][:], p_["mean"][:], -1.0, p_["rstd"][:],
                ALU.mult, ALU.mult))
            lnVng[i] = vmark()

        def ln_back(i):
            p = lnS[i % 2]
            wait("scalar", "mV", lnVng[i])
            if i >= 2:
                wait("scalar", "mT", lnT[i - 2][1])
            lnC[i] = op("scalar", lambda E, i_=i, p_=p: E.activation(
                xnN[i_ % 2][:], xbuf[i_ % 2][:], AF.Identity,
                bias=p_["negmrs"][:], scale=p_["rstd"][:]), inc="mC")
            wait("tensor", "dG", dG_consts)
            wait("tensor", "mC", lnC[i])
            for b2 in range(2):
                pTx = pT if b2 == 0 else pT2
                if i >= 1:
                    wait("tensor", "mV", lnEv[i - 1][b2])
                for j8 in range(8):
                    j = 8 * b2 + j8
                    fn = (lambda E, i_=i, j_=j, j8_=j8, px=pTx: E.transpose(
                        px[:, 128 * j8_:128 * (j8_ + 1)],
                        xnN[i_ % 2][:, 128 * j_:128 * (j_ + 1)], identb[:]))
                    if j8 == 7:
                        lnT[i][b2] = op("tensor", fn, inc="mT")
                    else:
                        op("tensor", fn)
                wait("vector", "mT", lnT[i][b2])
                dst = bass.AP(xnT, (8 * b2) * 2048 + 128 * i,
                              [[ET * 2048, 128], [2048, 8], [1, 128]])
                src_ = bass.AP(pTx, 0, [[1024, 128], [128, 8], [1, 128]])
                lnEv[i][b2] = op("vector",
                                 lambda E, d_=dst, s_=src_: E.tensor_copy(d_, s_),
                                 inc="mV")

        for i in range(17):
            if i < 16:
                ln_front(i)
            if i >= 1:
                ln_back(i - 1)
        for j in range(ET):
            vcomp = op("vector", lambda E, j_=j: E.tensor_copy(
                xn_own[:, 1024 * j_:1024 * (j_ + 1)],
                xnT[:, 2048 * j_:2048 * j_ + 1024]), inc="mV")
        xn_own_done = vcomp

        if stop == "xn":
            wait("gpsimd", "mV", xn_own_done)
            fin(dmaG(tap_d, xnT[:]), "dG")

        if stop != "xn":
            wait("tensor", "dG", dG_consts)
            wait("tensor", "mV", xn_own_done)
            for i in range(ST):
                for j in range(ET):
                    fn = (lambda E, i_=i, j_=j: E.matmul(
                        pA[:, 8 * i_:8 * (i_ + 1)],
                        xn_own[:, 1024 * j_ + 128 * i_:1024 * j_ + 128 * (i_ + 1)],
                        gw_sb[:, 8 * j_:8 * (j_ + 1)],
                        start=(j_ == 0), stop=(j_ == 15)))
                    if i == ST - 1 and j == ET - 1:
                        gmm = op("tensor", fn, inc="mT")
                    else:
                        op("tensor", fn)
            wait("vector", "dS", dS_c)
            wait("vector", "mT", gmm)
            op("vector", lambda E: E.tensor_copy(logits[:], pA[:, 0:64]))
            glog = vmark()
            if stop == "gat0":
                wait("sync", "mV", glog)
                fin(dmaS(tap_d, logits[:]))
            for i in range(ST if stop != "gat0" else 0):
                sl = slice(8 * i, 8 * (i + 1))
                op("vector", lambda E, s_=sl: E.reduce_max(
                    nm1["gnm"][:], logits[:, s_], axis=AX.X, negate=True))
                vgap()
                vm = vmark()
                wait("scalar", "mV", vm)
                op("scalar", lambda E, s_=sl: E.activation(
                    probs[:, s_], logits[:, s_], AF.Exp, bias=nm1["gnm"][:],
                    accum_out=nm1["gsum"][:]))
                cse = cmark()
                wait("vector", "mC", cse)
                op("vector", lambda E: E.reciprocal(nm1["grin"][:], nm1["gsum"][:]))
                vgap()
                op("vector", lambda E, s_=sl: E.tensor_scalar_mul(
                    probs[:, s_], probs[:, s_], nm1["grin"][:]))
                vgap()
                op("vector", lambda E, s_=sl: E.max(out=top8[:], in_=probs[:, s_]))
                vgap()
                op("vector", lambda E, s_=sl: E.tensor_tensor(
                    mask8[:], probs[:, s_], top8[:, 1:2].to_broadcast([128, 8]),
                    ALU.is_ge))
                vgap()
                op("vector", lambda E, s_=sl: E.tensor_mul(
                    combf[:, s_], probs[:, s_], mask8[:]))
                vgap()
                vcb = op("vector", lambda E, s_=sl: E.tensor_copy(
                    comb_bf[:, s_], combf[:, s_]), inc="mV")
            if stop == "gat1":
                wait("sync", "mV", vcb)
                fin(dmaS(tap_d, combf[:]))
            if stop not in ("gat0", "gat1"):
                wait("tensor", "mV", vcb)
                wait("tensor", "mV", lnEv[15][0])
                ctr = op("tensor", lambda E: E.transpose(
                    pT[0:64, 0:128], comb_bf[:], identb[:]), inc="mT")
                wait("vector", "mT", ctr)
                vct = op("vector", lambda E: E.tensor_copy(
                    combT64[:], pT[0:64, 0:128]), inc="mV")

        if stop == "gating":
            wait("sync", "mV", vcb)
            fin(dmaS(tap_d, combf[:]))


        es_l.close()
        es_f = es_f1 = es_fo = None

        if stop not in ("xn", "gating", "gat0", "gat1"):
            pass
        if stop not in ("xn", "gating", "gat0", "gat1"):
            wqb = [sba(f"wqb{k}", (128, 2048), BF16) for k in range(2)]
            wkb = [sba(f"wkb{k}", (128, 2048), BF16) for k in range(2)]
            wvb = [sba(f"wvb{k}", (128, 2048), BF16) for k in range(2)]
            stage1 = sba("stage1", (128, 2048), BF16)
            stage2 = sba("stage2", (128, 2048), BF16)
            qh = sba("qh", (128, 1024), BF16)
            kh = sba("kh", (128, 2048), BF16)
            vn = sba("vn", (128, 2048), BF16)
            Pb2 = [sba(f"Pb{k}", (128, 2048), BF16) for k in range(2)]
            PTq2 = [sba(f"PTq{k}", (128, 2048), BF16) for k in range(2)]
            rs8 = [[sba(f"rs_{p}_{i}", (128, 1), F32) for i in range(4)]
                   for p in range(2)]
            rsx = [{k: sba(f"rsx_{k}{p}", (128, 1), F32)
                    for k in ("a", "b", "t", "i")} for p in range(2)]

            wld = [0] * H
            wrel = [0] * H
            pvEv = [0] * H
            nheads = 1 if stop in ("head0", "h0t0") else H
            expHist = []
            trPar = [0, 0]
            ptFree = [vct, lnEv[15][1]]
            kropeL = 0
            vevL = 0

            expBank = [0, 0, 0, 0]
            scmmLast = 0

            def preg(i, c):
                if i == 0:
                    return pA[:, 512 * c:512 * (c + 1)]
                if i == 1:
                    return pS[:, 512 * c:512 * (c + 1)]
                return pS[:, 1024 + 512 * c:1024 + 512 * (c + 1)]

            for h in range(nheads):
                if h >= 2:
                    wait("gpsimd", "mT", wrel[h - 2])
                else:
                    wait("gpsimd", "mT", lnT[15][1])
                    wait("gpsimd", "mC", lnC[15])
                for wb_, nm in ((wqb, "wq"), (wkb, "wk"), (wvb, "wv")):
                    wld[h] = dmaG(wb_[h % 2][:],
                                  dd[nm][h].rearrange("p j d -> p (j d)"))

                def proj_mm(wbuf, reg, bofs, bidx):
                    last = None
                    for j in range(ET):
                        for c in range(2):
                            op("tensor", lambda E, c_=c, j_=j, w_=wbuf, r_=reg:
                               E.matmul(
                                   preg(r_, c_),
                                   w_[:, 128 * j_:128 * (j_ + 1)],
                                   (xn_own[:, 1024 * j_ + 512 * c_:
                                           1024 * j_ + 512 * (c_ + 1)]
                                    if bofs is None else
                                    xnT[:, 2048 * j_ + bofs + 512 * c_:
                                        2048 * j_ + bofs + 512 * (c_ + 1)]),
                                   start=(j_ == 0), stop=False))
                    for c in range(2):
                        fnb = (lambda E, c_=c, r_=reg, rr=bidx: E.matmul(
                            preg(r_, c_),
                            bass.AP(bias4, 32 * (rr % 3) * 2816 + 128 * (rr // 3),
                                    [[2816, 1], [1, 128]]),
                            bass.AP(onesb, 32 * (rr % 3) * 512, [[512, 1], [1, 512]]),
                            start=False, stop=True))
                        if c == 1:
                            last = op("tensor", fnb, inc="mT")
                        else:
                            op("tensor", fnb)
                    return last

                def swap_mm(reg, src_ap_fn):
                    last = None
                    for c in range(2):
                        fn = (lambda E, c_=c, r_=reg, sf=src_ap_fn: E.matmul(
                            preg(r_, c_), permb[:], sf(c_), start=True, stop=True))
                        if c == 1:
                            last = op("tensor", fn, inc="mT")
                        else:
                            op("tensor", fn)
                    return last

                wait("tensor", "dG", wld[h])
                if h == 0:
                    wait("tensor", "mV", glog)
                else:
                    wait("tensor", "mV", pvEv[h - 1])
                qmm = proj_mm(wqb[h % 2], 0, None, h)
                wait("tensor", "mC", expBank[0])
                wait("tensor", "mC", expBank[1])
                kmm0 = proj_mm(wkb[h % 2], 1, 0, 16 + h)
                wait("tensor", "mC", expBank[2])
                wait("tensor", "mC", expBank[3])
                kmm1 = proj_mm(wkb[h % 2], 2, 1024, 16 + h)
                wait("scalar", "mT", qmm)
                if h > 0:
                    wait("scalar", "mT", scmmLast)
                qev = op("scalar", lambda E: E.activation(
                    qh[:], pA[:, 0:1024], AF.Identity), inc="mC")
                wait("scalar", "mT", kmm0)
                kev0 = op("scalar", lambda E: E.activation(
                    kh[:, 0:1024], pS[:, 0:1024], AF.Identity), inc="mC")
                wait("scalar", "mT", kmm1)
                kev1 = op("scalar", lambda E: E.activation(
                    kh[:, 1024:2048], pS[:, 1024:2048], AF.Identity), inc="mC")
                wait("tensor", "mC", qev)
                qsw = swap_mm(0, lambda c: qh[:, 512 * c:512 * (c + 1)])
                wait("tensor", "mC", kev0)
                ksw0 = swap_mm(1, lambda c: kh[:, 512 * c:512 * (c + 1)])
                wait("tensor", "mC", kev1)
                ksw1 = swap_mm(2, lambda c: kh[:, 1024 + 512 * c:1536 + 512 * c])
                wait("scalar", "mT", qsw)
                if h > 0:
                    wait("scalar", "mT", wrel[h - 1])
                qsv = op("scalar", lambda E: E.activation(
                    stage2[:, 0:1024], pA[:, 0:1024], AF.Identity), inc="mC")
                wait("scalar", "mT", ksw0)
                ksv0 = op("scalar", lambda E: E.activation(
                    stage2[:, 1024:2048], pS[:, 0:1024], AF.Identity), inc="mC")
                wait("vector", "mT", qsw)
                op("vector", lambda E: E.tensor_tensor(
                    qh[:], qh[:], coskb[:, 0:1024], ALU.mult))
                wait("vector", "mC", qsv)
                op("vector", lambda E: E.tensor_tensor(
                    stage2[:, 0:1024], stage2[:, 0:1024], sinkb[:, 0:1024],
                    ALU.mult))
                op("vector", lambda E: E.tensor_tensor(
                    qh[:], qh[:], stage2[:, 0:1024], ALU.add))
                qrope = op("vector", lambda E: E.tensor_scalar_mul(
                    qh[:], qh[:], ISQD), inc="mV")
                wait("scalar", "mT", ksw1)
                wait("scalar", "mV", qrope)
                ksv1 = op("scalar", lambda E: E.activation(
                    stage2[:, 0:1024], pS[:, 1024:2048], AF.Identity), inc="mC")
                wait("vector", "mT", ksw0)
                op("vector", lambda E: E.tensor_tensor(
                    kh[:, 0:1024], kh[:, 0:1024], coskb[:, 0:1024], ALU.mult))
                wait("vector", "mC", ksv0)
                op("vector", lambda E: E.tensor_tensor(
                    stage2[:, 1024:2048], stage2[:, 1024:2048],
                    sinkb[:, 0:1024], ALU.mult))
                krope0 = op("vector", lambda E: E.tensor_tensor(
                    kh[:, 0:1024], kh[:, 0:1024], stage2[:, 1024:2048], ALU.add),
                    inc="mV")
                wait("vector", "mT", ksw1)
                op("vector", lambda E: E.tensor_tensor(
                    kh[:, 1024:2048], kh[:, 1024:2048], coskb[:, 1024:2048],
                    ALU.mult))
                wait("vector", "mC", ksv1)
                op("vector", lambda E: E.tensor_tensor(
                    stage2[:, 0:1024], stage2[:, 0:1024], sinkb[:, 1024:2048],
                    ALU.mult))
                kropeL = op("vector", lambda E: E.tensor_tensor(
                    kh[:, 1024:2048], kh[:, 1024:2048], stage2[:, 0:1024],
                    ALU.add), inc="mV")
                if stop == "h0q":
                    wait("gpsimd", "mV", qrope)
                    fin(dmaG(tap_d[:, 0:1024], qh[:]), "dG")
                    break
                if stop == "h0k":
                    wait("gpsimd", "mV", kropeL)
                    fin(dmaG(tap_d, kh[:]), "dG")
                    break
                wait("tensor", "mC", qsv)
                vmm0 = proj_mm(wvb[h % 2], 0, 0, 32 + h)
                wait("tensor", "mC", ksv0)
                vmm1 = proj_mm(wvb[h % 2], 1, 1024, 32 + h)
                wait("scalar", "mT", vmm0)
                wait("scalar", "mV", krope0)
                vev0 = op("scalar", lambda E: E.activation(
                    stage2[:, 1024:2048], pA[:, 0:1024], AF.Identity), inc="mC")
                wait("scalar", "mT", vmm1)
                wait("scalar", "mV", kropeL)
                vev1 = op("scalar", lambda E: E.activation(
                    stage2[:, 0:1024], pS[:, 0:1024], AF.Identity), inc="mC")
                vevL = vev1
                vnev = [0, 0]
                for b2 in range(2):
                    pTx = pT if b2 == 0 else pT2
                    wait("tensor", "mC", vev0 if b2 == 0 else vev1)
                    wait("tensor", "mV", ptFree[b2])
                    for j8 in range(8):
                        scol = (1024 + 128 * j8) if b2 == 0 else (128 * j8)
                        fn = (lambda E, s_=scol, j8_=j8, px=pTx: E.transpose(
                            px[:, 128 * j8_:128 * (j8_ + 1)],
                            stage2[:, s_:s_ + 128], identb[:]))
                        if j8 == 7:
                            vtr = op("tensor", fn, inc="mT")
                        else:
                            op("tensor", fn)
                    wait("vector", "mT", vtr)
                    vnev[b2] = op("vector", lambda E, b2_=b2: E.tensor_copy(
                        vn[:, 1024 * b2_:1024 * (b2_ + 1)],
                        (pT if b2_ == 0 else pT2)[:]), inc="mV")
                ptFree = [vnev[0], vnev[1]]
                wrel[h] = vtr
                if stop == "h0v":
                    wait("gpsimd", "mV", vnev[1])
                    fin(dmaG(tap_d, vn[:]), "dG")
                    break

                pvE = [0] * ST
                pend = None

                def attn_tail(t, psc):
                    nonlocal ptFree
                    par = t % 2
                    L = 128 * (t + 1)
                    wait("tensor", "mV", psc)
                    ptEv = [0, 0]
                    for b2 in range(2):
                        pTx = pT if b2 == 0 else pT2
                        wait("tensor", "mV", ptFree[b2])
                        for u in range(t + 1):
                            src_col = (L if b2 else 0) + 128 * u
                            fn = (lambda E, s_=src_col, u_=u, px=pTx, p_=par:
                                  E.transpose(
                                      px[:, 128 * u_:128 * (u_ + 1)],
                                      Pb2[p_][:, s_:s_ + 128], identb[:]))
                            if u == t:
                                tr_ = op("tensor", fn, inc="mT")
                            else:
                                op("tensor", fn)
                        wait("vector", "mT", tr_)
                        ptEv[b2] = op("vector", lambda E, t_=t, b2_=b2, p_=par:
                                      E.tensor_copy(
                                          PTq2[p_][:, 1024 * b2_:
                                                    1024 * b2_ + 128 * (t_ + 1)],
                                          (pT if b2_ == 0 else pT2)
                                          [:, 0:128 * (t_ + 1)]), inc="mV")
                    ptFree = [ptEv[0], ptEv[1]]
                    trPar[par] = tr_
                    pvreg = 512 * par
                    wait("tensor", "mV", ptEv[1])
                    if t == 0:
                        wait("tensor", "mC", vevL)
                    if t >= 2:
                        wait("tensor", "mV", pvE[t - 2])
                    elif h > 0:
                        wait("tensor", "mV", pvEv[h - 1])
                    first = True
                    for b2 in range(2):
                        for u in range(t + 1):
                            kslot = u if b2 == 0 else 8 + u
                            vcol = 128 * u if b2 == 0 else 1024 + 128 * u
                            last = (b2 == 1 and u == t)
                            fn = (lambda E, v_=vcol, k_=kslot, p_=pvreg, f_=first,
                                  l_=last, pp=par: E.matmul(
                                      pA[:, p_:p_ + 128], vn[:, v_:v_ + 128],
                                      PTq2[pp][:, 128 * k_:128 * (k_ + 1)],
                                      start=f_, stop=l_))
                            if last:
                                pvt_ = op("tensor", fn, inc="mT")
                            else:
                                op("tensor", fn)
                            first = False
                    wait("vector", "mT", pvt_)
                    pvE[t] = op("vector", lambda E, h_=h, t_=t, p_=pvreg:
                                E.tensor_copy(
                                    attn_out[:, 1024 * h_ + 128 * t_:
                                             1024 * h_ + 128 * (t_ + 1)],
                                    pA[:, p_:p_ + 128]), inc="mV")

                for t in range(ST):
                    L = 128 * (t + 1)
                    par = t % 2
                    chunks = []
                    for b2, base in ((0, 0), (1, 1024)):
                        off = 0
                        while off < L:
                            cw = min(512, L - off)
                            chunks.append((base, off, cw, b2))
                            off += cw
                    nch = len(chunks)
                    for ci, (base, off, cw, b2) in enumerate(chunks):
                        gci = len(expHist)
                        bank = gci % 4
                        wait("tensor", "mC", expBank[bank])
                        if t == 0 and ci == 0:
                            wait("tensor", "mV", qrope)
                            wait("tensor", "mV", kropeL)
                            wait("tensor", "mC", vevL)
                        scmm = op("tensor",
                                  lambda E, t_=t, b_=base, o_=off, w_=cw, bk=bank:
                                  E.matmul(
                                      pS[:, 512 * bk:512 * bk + w_],
                                      qh[:, 128 * t_:128 * (t_ + 1)],
                                      kh[:, b_ + o_:b_ + o_ + w_],
                                      start=True, stop=True), inc="mT")
                        scmmLast = scmm
                        ismask = (off + cw == L)
                        if ismask:
                            mlocal = 128 * t - off
                            wait("vector", "mT", scmm)
                            op("vector", lambda E, b2_=b2, bk=bank, ml=mlocal:
                               E.tensor_tensor(
                                   pS[:, 512 * bk + ml:512 * bk + ml + 128],
                                   pS[:, 512 * bk + ml:512 * bk + ml + 128],
                                   masks[:, 128 * b2_:128 * b2_ + 128],
                                   ALU.add))
                            mrd = vmark()
                            wait("scalar", "mV", mrd)
                        else:
                            wait("scalar", "mT", scmm)
                        if ci == 0:
                            wait("scalar", "mT", trPar[par])
                        outc = (L + off) if b2 else off
                        expi = op("scalar",
                                  lambda E, p_=par, oc=outc, w_=cw, bk=bank,
                                  ci_=ci: E.activation(
                                      Pb2[p_][:, oc:oc + w_],
                                      pS[:, 512 * bk:512 * bk + w_], AF.Exp,
                                      accum_out=rs8[p_][ci_][:]), inc="mC")
                        expHist.append(expi)
                        expBank[bank] = expi
                    wait("vector", "mC", expHist[-1])
                    rx = rsx[par]
                    if nch == 2:
                        op("vector", lambda E, p_=par: E.tensor_add(
                            rx["t"][:], rs8[p_][0][:], rs8[p_][1][:]))
                    else:
                        op("vector", lambda E, p_=par: E.tensor_add(
                            rx["a"][:], rs8[p_][0][:], rs8[p_][1][:]))
                        op("vector", lambda E, p_=par: E.tensor_add(
                            rx["b"][:], rs8[p_][2][:], rs8[p_][3][:]))
                        vgap()
                        op("vector", lambda E: E.tensor_add(
                            rx["t"][:], rx["a"][:], rx["b"][:]))
                    vgap()
                    op("vector", lambda E: E.reciprocal(rx["i"][:], rx["t"][:]))
                    vgap()
                    psc = op("vector", lambda E, p_=par, L_=L: E.tensor_scalar_mul(
                        Pb2[p_][:, 0:2 * L_], Pb2[p_][:, 0:2 * L_], rx["i"][:]),
                        inc="mV")
                    if pend is not None:
                        attn_tail(*pend)
                    pend = (t, psc)
                attn_tail(*pend)
                pvEv[h] = pvE[ST - 1]

            attn_done_V = pvEv[nheads - 1]

            if stop == "head0":
                wait("gpsimd", "mV", qrope)
                dmaG(tap_d[:, 0:1024], qh[:])
                wait("gpsimd", "mV", kropeL)
                dmaG(tap_d[:, 2048:4096], kh[:])
                wait("gpsimd", "mV", vnev[1])
                dmaG(tap_d[:, 4096:6144], vn[:])
                wait("gpsimd", "mV", attn_done_V)
                fin(dmaG(tap_d[:, 6144:7168], attn_out[:, 0:1024]), "dG")
            if stop == "attn":
                wait("gpsimd", "mV", attn_done_V)
                fin(dmaG(tap_d, attn_out[:]), "dG")

        if stop in (None, "oproj", "oproj0", "ffn"):
            es_a.close()
            bT, bV, bC = C["mT"], C["mV"], C["mC"]
            for eng in ENGINES:
                wait(eng, "mT", bT)
                wait(eng, "mV", bV)
                wait(eng, "mC", bC)

            es_f = ExitStack()
            sbf_ = lambda n_, s_, d_: es_f.enter_context(
                nc.sbuf_tensor(n_, list(s_), d_))
            psf_ = lambda n_, s_, d_: es_f.enter_context(
                nc.psum_tensor(n_, list(s_), d_))
            acc = sbf_("acc", (128, HT * 1024), F32)
            pW = psf_("pW", (128, 1024), F32)
            pX = psf_("pX", (128, 1024), F32)
            pY = psf_("pY", (128, 1024), F32)
            pZ = psf_("pZ", (128, 1024), F32)
            es_fo = ExitStack()
            wob = [es_fo.enter_context(nc.sbuf_tensor(f"wob{k}", [128, 2048], BF16))
                   for k in range(2)]

            woL = {}
            woR = {}
            oEv = [0] * HT
            woL[0] = dmaG(wob[0][:], dd["wo"][0].rearrange("p j d -> p (j d)"))
            woL[1] = dmaG(wob[1][:], dd["wo"][1].rearrange("p j d -> p (j d)"))
            for fb in range(HT if stop != "oproj0" else 16):
                pP = pW if fb % 2 == 0 else pX
                wait("tensor", "dG", woL[fb])
                if fb >= 2:
                    wait("tensor", "mC", oEv[fb - 2])
                for j in range(ET):
                    for c in range(2):
                        op("tensor", lambda E, f_=fb, c_=c, j_=j, p_=pP: E.matmul(
                            p_[:, 512 * c_:512 * (c_ + 1)],
                            wob[f_ % 2][:, 128 * j_:128 * (j_ + 1)],
                            attn_out[:, 1024 * j_ + 512 * c_:
                                     1024 * j_ + 512 * (c_ + 1)],
                            start=(j_ == 0), stop=False))
                for c in range(2):
                    fnb = (lambda E, f_=fb, c_=c, p_=pP: E.matmul(
                        p_[:, 512 * c_:512 * (c_ + 1)],
                        bass.AP(bias4,
                                32 * ((48 + f_) % 3) * 2816 + 128 * ((48 + f_) // 3),
                                [[2816, 1], [1, 128]]),
                        bass.AP(onesb, 32 * ((48 + f_) % 3) * 512,
                                [[512, 1], [1, 512]]),
                        start=False, stop=True))
                    if c == 1:
                        woR[fb] = op("tensor", fnb, inc="mT")
                    else:
                        op("tensor", fnb)
                if fb + 2 < HT:
                    wait("gpsimd", "mT", woR[fb])
                    woL[fb + 2] = dmaG(wob[fb % 2][:],
                                       dd["wo"][fb + 2].rearrange("p j d -> p (j d)"))
                wait("scalar", "mT", woR[fb])
                oEv[fb] = op("scalar", lambda E, f_=fb, p_=pP: E.activation(
                    acc[:, 1024 * f_:1024 * (f_ + 1)], p_[:], AF.Identity),
                    inc="mC")
            o_done = oEv[HT - 1] if stop != "oproj0" else oEv[0]
            if stop == "oproj0":
                wait("sync", "mC", o_done)
                fin(dmaS(tap_d, acc[:, 0:1024]))

            if stop == "oproj":
                wait("sync", "mC", o_done)
                fin(dmaS(tap_d, acc[:]))

        if stop in (None, "ffn"):
            es_fo.close()
            es_f1 = ExitStack()
            sbf1 = lambda n_, s_, d_: es_f1.enter_context(
                nc.sbuf_tensor(n_, list(s_), d_))
            gub = [sbf1(f"gub{k}", (128, 4096), BF16) for k in range(2)]
            dnb = [sbf1(f"dnb{k}", (128, 2816), BF16) for k in range(2)]
            act = sbf1("act", (128, FT * 1024), BF16)
            bc_sb = sbf1("bc_sb", (128, 1024), BF16)
            t1_sb = sbf1("t1_sb", (128, 1024), BF16)
            combT2 = sbf1("combT2", (1, 8192), BF16)
            wait("gpsimd", "mT", woR[HT - 1])
            wait("gpsimd", "mV", vct)
            for e in range(8):
                dG_comb = dmaG(
                    bass.AP(combT2, e * 1024, [[8192, 1], [128, 8], [1, 128]]),
                    bass.AP(combT64, e * 128, [[8 * 128, 8], [1, 128]]))
            guL = {}
            guR = {}
            dnL = {}
            dnR = {}
            guL[0] = dmaG(gub[0][:], dd["guw"][0, 0])
            guL[1] = dmaG(gub[1][:], dd["guw"][0, 1])
            dnL[0] = dmaG(dnb[0][:], dd["dnw"][0, 0])
            dnL[1] = dmaG(dnb[1][:], dd["dnw"][0, 1])

            dnAdd = 0
            bcV = 0
            guEvLast = 0
            for e in range(NE):
                gated = e < 8
                if gated:
                    wait("tensor", "dG", dG_comb)
                    if e > 0:
                        wait("tensor", "mV", bcV)
                        wait("tensor", "mV", guEvLast)
                    else:
                        wait("tensor", "mC", o_done)
                    for c in range(2):
                        fn = (lambda E, e_=e, c_=c: E.matmul(
                            pY[:, 512 * c_:512 * (c_ + 1)], onesb[0:1, 0:128],
                            combT2[0:1, 1024 * e_ + 512 * c_:
                                   1024 * e_ + 512 * (c_ + 1)],
                            start=True, stop=True))
                        if c == 1:
                            bcT = op("tensor", fn, inc="mT")
                        else:
                            op("tensor", fn)
                    wait("vector", "mT", bcT)
                    bcV = op("vector", lambda E: E.tensor_copy(bc_sb[:], pY[:]),
                             inc="mV")
                guEv = [0] * FT
                for jg in range(FT):
                    k = e * FT + jg
                    pair = (pW, pX) if jg % 2 == 0 else (pY, pZ)
                    wait("tensor", "dG", guL[k])
                    if jg >= 2:
                        wait("tensor", "mV", guEv[jg - 2])
                    elif e == 0:
                        wait("tensor", "mC", o_done)
                    else:
                        wait("tensor", "mV", dnAdd)
                        if jg == 1 and gated:
                            wait("tensor", "mV", bcV)
                    for half2 in range(2):
                        pP = pair[half2]
                        for j in range(ET):
                            for c in range(2):
                                fn = (lambda E, k_=k, h2=half2, c_=c, j_=j, p_=pP:
                                      E.matmul(
                                          p_[:, 512 * c_:512 * (c_ + 1)],
                                          gub[k_ % 2][:, 2048 * h2 + 128 * j_:
                                                      2048 * h2 + 128 * (j_ + 1)],
                                          xn_own[:, 1024 * j_ + 512 * c_:
                                                 1024 * j_ + 512 * (c_ + 1)],
                                          start=(j_ == 0), stop=(j_ == 15)))
                                if half2 == 1 and c == 1 and j == ET - 1:
                                    guR[k] = op("tensor", fn, inc="mT")
                                else:
                                    op("tensor", fn)
                    if k + 2 < NE * FT:
                        e2, jg2 = divmod(k + 2, FT)
                        wait("gpsimd", "mT", guR[k])
                        guL[k + 2] = dmaG(gub[k % 2][:], dd["guw"][e2, jg2])
                    wait("scalar", "mT", guR[k])
                    if jg >= 1:
                        wait("scalar", "mV", guEv[jg - 1])
                    elif e > 0:
                        wait("scalar", "mV", guEvPrevLast)
                    sil = op("scalar", lambda E, p_=pair[0]: E.activation(
                        t1_sb[:], p_[:], AF.Silu), inc="mC")
                    wait("vector", "mC", sil)
                    if gated:
                        wait("vector", "mV", bcV)
                    op("vector", lambda E, p_=pair[1]: E.tensor_tensor(
                        t1_sb[:], t1_sb[:], p_[:], ALU.mult))
                    if gated:
                        guEv[jg] = op("vector", lambda E, g_=jg: E.tensor_tensor(
                            act[:, 1024 * g_:1024 * (g_ + 1)], t1_sb[:], bc_sb[:],
                            ALU.mult), inc="mV")
                    else:
                        guEv[jg] = op("vector", lambda E, g_=jg: E.tensor_copy(
                            act[:, 1024 * g_:1024 * (g_ + 1)], t1_sb[:]), inc="mV")
                guEvPrevLast = guEv[FT - 1]
                guEvLast = guEv[FT - 1]
                dnAddPrev = [guEv[FT - 1], guEv[FT - 2]]
                for hb in range(HT):
                    c4, hb2 = hb // 2, hb % 2
                    m = e * 8 + c4
                    pP = pW if hb % 2 == 0 else pX
                    wait("tensor", "dG", dnL[m])
                    wait("tensor", "mV", guEvLast)
                    if hb >= 2:
                        wait("tensor", "mV", dnAddPrev[hb % 2])
                    for f in range(FT):
                        for c in range(2):
                            fn = (lambda E, m_=m, h2=hb2, f_=f, c_=c, p_=pP:
                                  E.matmul(
                                      p_[:, 512 * c_:512 * (c_ + 1)],
                                      dnb[m_ % 2][:, 1408 * h2 + 128 * f_:
                                                  1408 * h2 + 128 * (f_ + 1)],
                                      act[:, 1024 * f_ + 512 * c_:
                                          1024 * f_ + 512 * (c_ + 1)],
                                      start=(f_ == 0), stop=(f_ == FT - 1)))
                            if f == FT - 1 and c == 1:
                                dmm = op("tensor", fn, inc="mT")
                            else:
                                op("tensor", fn)
                    if hb2 == 1:
                        dnR[m] = dmm
                        if m + 2 < NE * 8:
                            e2, c42 = divmod(m + 2, 8)
                            wait("gpsimd", "mT", dnR[m])
                            dnL[m + 2] = dmaG(dnb[m % 2][:], dd["dnw"][e2, c42])
                    wait("vector", "mT", dmm)
                    dnAdd = op("vector", lambda E, hb_=hb, p_=pP: E.tensor_tensor(
                        acc[:, 1024 * hb_:1024 * (hb_ + 1)],
                        acc[:, 1024 * hb_:1024 * (hb_ + 1)], p_[:], ALU.add),
                        inc="mV")
                    dnAddPrev[hb % 2] = dnAdd
            ffn_done = dnAdd

            if stop == "ffn":
                wait("sync", "mV", ffn_done)
                fin(dmaS(tap_d, acc[:]))

        if stop is None:
            es_f1.close()
            xrb = []
            outb = []
            for k in range(2):
                xrb.append(es_f.enter_context(
                    nc.sbuf_tensor(f"xrb{k}", [128, 2048], F32)))
                outb.append(es_f.enter_context(
                    nc.sbuf_tensor(f"outb{k}", [128, 2048], F32)))
            wait("sync", "mV", ffn_done)
            finV = [0] * ST
            stO = [0] * ST
            for i in range(ST):
                if i >= 2:
                    wait("sync", "mV", finV[i - 2])
                xrl = dmaS(xrb[i % 2][:], dd["x"][128 * i:128 * (i + 1), :])
                pPair = (pY, pZ) if i % 2 == 0 else (pW, pX)
                wait("tensor", "mV", ffn_done)
                if i >= 2:
                    wait("tensor", "mV", finV[i - 2])
                trF = [0, 0]
                for g in range(2):
                    pP = pPair[g]
                    for b4 in range(8):
                        et = 8 * g + b4
                        fn = (lambda E, e_=et, b_=b4, p_=pP, i_=i: E.transpose(
                            p_[:, 128 * b_:128 * (b_ + 1)],
                            acc[:, 1024 * e_ + 128 * i_:1024 * e_ + 128 * (i_ + 1)],
                            identf[:]))
                        if b4 == 7:
                            trF[g] = op("tensor", fn, inc="mT")
                        else:
                            op("tensor", fn)
                wait("vector", "mT", trF[1])
                wait("vector", "dS", xrl)
                if i >= 2:
                    wait("vector", "dS", stO[i - 2])
                for g in range(2):
                    pP = pPair[g]
                    for c in range(2):
                        fn = (lambda E, g_=g, c_=c, p_=pP, i_=i: E.tensor_tensor(
                            outb[i_ % 2][:, 1024 * g_ + 512 * c_:
                                         1024 * g_ + 512 * (c_ + 1)],
                            xrb[i_ % 2][:, 1024 * g_ + 512 * c_:
                                        1024 * g_ + 512 * (c_ + 1)],
                            p_[:, 512 * c_:512 * (c_ + 1)], ALU.add))
                        if g == 1 and c == 1:
                            finV[i] = op("vector", fn, inc="mV")
                        else:
                            op("vector", fn)
                wait("sync", "mV", finV[i])
                stO[i] = dmaS(out_d[i], outb[i % 2][:])
            fin(stO[ST - 1])

        if es_f1 is not None:
            es_f1.close()
        if es_fo is not None:
            es_fo.close()
        if es_f is not None:
            es_f.close()
        es_a.close()

        block = es.enter_context(nc.Block())

        @block.tensor
        def _(E):
            bld.replay("tensor", E)

        @block.vector
        def _(E):
            bld.replay("vector", E)

        @block.scalar
        def _(E):
            bld.replay("scalar", E)

        @block.sync
        def _(E):
            bld.replay("sync", E)

        @block.gpsimd
        def _(E):
            bld.replay("gpsimd", E)

    nc.finalize()
    return nc


def kernel(**inputs):
    global _LAST
    shared = _prep_shared(inputs)
    cosg, sing = _rope_tables()
    nc = _build(_STOP)
    in_maps = []
    for c in range(8):
        b, half = c // 2, c % 2
        m = dict(shared)
        m.update(_prep_core(inputs, b, half, cosg, sing))
        in_maps.append(m)
    res = run_bass_kernel_spmd(nc, in_maps, core_ids=list(range(8)))
    out = np.zeros((4, 2048, 2048), np.float32)
    _LAST = {}
    for c in range(8):
        b, half = c // 2, c % 2
        own, _ = _core_perm(half)
        out[b][own] = np.asarray(res.results[c]["out"]).reshape(1024, 2048)
        if _STOP is not None:
            _LAST[c] = np.asarray(res.results[c]["tap"])
    return out


# version 09
# speedup vs baseline: 1.1133x; 1.0010x over previous
import numpy as np
from contextlib import ExitStack

import concourse.bass as bass
import concourse.mybir as mybir
from concourse.bass_utils import run_bass_kernel_spmd

F32 = mybir.dt.float32
BF16 = mybir.dt.bfloat16
AF = mybir.ActivationFunctionType
ALU = mybir.AluOpType
AX = mybir.AxisListType

E2 = 2048
H = 16
ET = 16
SOWN = 1024
ST = 8
NE = 10
FT = 11
HT = 16
NEG = -1.0e30
ISQD = 1.0 / float(np.sqrt(128.0))

ENGINES = ("tensor", "vector", "scalar", "sync", "gpsimd")

_STOP = None
_LAST = {}


class _Bld:
    def __init__(self, semnames):
        self.streams = {e: [] for e in ENGINES}
        self.C = {n: 0 for n in semnames}
        self.sems = {}
        self._trun = 0

    def raw(self, eng, fn):
        self.streams[eng].append(fn)

    def wait(self, eng, sem, val):
        if val <= 0:
            return
        assert val <= self.C[sem], f"wait {sem}>={val} but only {self.C[sem]} planned"
        if eng == "tensor":
            self._trun = 0
        self.raw(eng, lambda E, s=sem, v=val: E.wait_ge(self.sems[s], v))

    def op(self, eng, fn, inc=None, n=1):
        if eng == "tensor":
            self._trun += 1
            if self._trun >= 4:
                self.streams[eng].append(lambda E: E.nop())
                self._trun = 0
        if inc is None:
            self.raw(eng, fn)
            return None
        self.C[inc] += n
        self.raw(eng, lambda E, f=fn, s=inc, k=n: f(E).then_inc(self.sems[s], k))
        return self.C[inc]

    def replay(self, eng_name, E):
        for fn in self.streams[eng_name]:
            fn(E)


def _lay_qkv(W):
    return np.ascontiguousarray(
        W.T.reshape(16, 128, 16, 128).transpose(2, 1, 0, 3))


def _prep_shared(inp):
    out = {}
    out["wq"] = _lay_qkv(np.asarray(inp["L0_attn_Wq"], np.float32))
    out["wk"] = _lay_qkv(np.asarray(inp["L0_attn_Wk"], np.float32))
    out["wv"] = _lay_qkv(np.asarray(inp["L0_attn_Wv"], np.float32))
    out["wo"] = _lay_qkv(np.asarray(inp["L0_attn_Wo"], np.float32))
    bflat = np.concatenate(
        [np.asarray(inp[k], np.float32) for k in
         ("L0_attn_bq", "L0_attn_bk", "L0_attn_bv", "L0_attn_bo")]).reshape(64, 128)
    b4 = np.zeros((3, 22, 128), np.float32)
    for r in range(64):
        b4[r % 3, r // 3] = bflat[r]
    out["brows"] = b4.reshape(3, 2816)
    out["gw"] = np.ascontiguousarray(
        np.asarray(inp["L0_ffn_gate_weight"], np.float32).T
        .reshape(16, 128, 8).transpose(1, 0, 2))
    gus = []
    for e in range(8):
        Wt = np.asarray(inp["L0_ffn_gate_up_weights"][e], np.float32).T
        gus.append(Wt.reshape(16, 128, 2, 11, 128)
                   .transpose(3, 1, 2, 0, 4).reshape(11, 128, 4096))
    St = np.asarray(inp["L0_ffn_shared_gate_up"], np.float32).T
    sg = (St.reshape(16, 128, 2, 2, 11, 128)
          .transpose(3, 4, 1, 2, 0, 5).reshape(2, 11, 128, 4096))
    out["guw"] = np.ascontiguousarray(
        np.concatenate([np.stack(gus, 0), sg], axis=0))
    dns = []
    for e in range(8):
        Dt = np.asarray(inp["L0_ffn_down_weights"][e], np.float32).T
        dns.append(Dt.reshape(11, 128, 8, 2, 128)
                   .transpose(2, 1, 3, 0, 4).reshape(8, 128, 2816))
    St2 = np.asarray(inp["L0_ffn_shared_down"], np.float32).T
    sd2 = (St2.reshape(2, 11, 128, 8, 2, 128)
           .transpose(0, 3, 2, 4, 1, 5).reshape(2, 8, 128, 2816))
    out["dnw"] = np.ascontiguousarray(
        np.concatenate([np.stack(dns, 0), sd2], axis=0))
    out["ident"] = np.eye(128, dtype=np.float32)
    perm = np.zeros((128, 128), np.float32)
    for j in range(128):
        perm[j + 1 if j % 2 == 0 else j - 1, j] = 1.0
    out["perm"] = perm
    out["ones1"] = np.ones((128, 512), np.float32)
    return out


def _rope_tables():
    j = np.arange(128)
    theta = 10000.0 ** (-(2.0 * (j // 2)) / 128.0)
    ang = np.outer(theta, np.arange(2048).astype(np.float64))
    cosg = np.cos(ang).astype(np.float32)
    sign = np.where(j % 2 == 0, -1.0, 1.0)
    sing = (np.sin(ang) * sign[:, None]).astype(np.float32)
    return cosg, sing


def _core_perm(half):
    own = np.concatenate(
        [np.arange(256 * t + 128 * half, 256 * t + 128 * half + 128)
         for t in range(8)])
    other = np.concatenate(
        [np.arange(256 * t + 128 * (1 - half), 256 * t + 128 * (1 - half) + 128)
         for t in range(8)])
    return own, np.concatenate([own, other])


def _prep_core(inp, b, half, cosg, sing):
    own, perm = _core_perm(half)
    d = {}
    d["x"] = np.ascontiguousarray(np.asarray(inp["L0_attn_x"], np.float32)[b][perm])
    d["cosk"] = np.ascontiguousarray(cosg[:, perm])
    d["sink"] = np.ascontiguousarray(sing[:, perm])
    m = np.zeros((128, 256), np.float32)
    tri = np.where(np.arange(128)[None, :] <= np.arange(128)[:, None], 0.0, NEG)
    m[:, 0:128] = tri.astype(np.float32)
    if half == 0:
        m[:, 128:256] = NEG
    d["mask"] = m
    return d


def _build(stop=None):
    nc = bass.Bass("TRN2", target_bir_lowering=False)
    dd = {}

    def din(name, shape):
        dd[name] = nc.dram_tensor(name, list(shape), F32, kind="ExternalInput").ap()

    din("x", (2048, 2048))
    din("wq", (16, 128, 16, 128)); din("wk", (16, 128, 16, 128))
    din("wv", (16, 128, 16, 128)); din("wo", (16, 128, 16, 128))
    din("brows", (3, 2816))
    din("gw", (128, 16, 8))
    din("guw", (NE, FT, 128, 4096))
    din("dnw", (NE, 8, 128, 2816))
    din("cosk", (128, 2048)); din("sink", (128, 2048))
    din("mask", (128, 256))
    din("ident", (128, 128)); din("perm", (128, 128)); din("ones1", (128, 512))
    out_d = nc.dram_tensor("out", [ST, 128, 2048], F32, kind="ExternalOutput").ap()
    tap_shapes = {"xn": [128, ET * 2048], "gating": [128, 64],
                  "gat0": [128, 64], "gat1": [128, 64], "h0q": [128, 1024],
                  "h0k": [128, 2048], "h0v": [128, 2048], "h0t0": [128, 2048],
                  "head0": [128, 5 * 2048], "attn": [128, H * 1024],
                  "oproj": [128, HT * 1024], "oproj0": [128, 1024],
                  "ffn": [128, HT * 1024]}
    tap_d = None
    if stop is not None:
        tap_d = nc.dram_tensor("tap", tap_shapes[stop], F32,
                               kind="ExternalOutput").ap()

    SEMS = ("dG", "dS", "mT", "mV", "mC")
    bld = _Bld(SEMS)
    op, wait, C = bld.op, bld.wait, bld.C
    es = ExitStack()

    with es:
        xn_own = es.enter_context(nc.sbuf_tensor("xn_own", [128, ET * 1024], BF16))
        ao_w = 2 * 1024 if stop == "h0t0" else H * 1024
        attn_out = es.enter_context(nc.sbuf_tensor("attn_out", [128, ao_w], BF16))

        combT64 = es.enter_context(nc.sbuf_tensor("combT64", [64, 128], BF16))
        bias4 = es.enter_context(nc.sbuf_tensor("bias4", [128, 2816], BF16))
        identb = es.enter_context(nc.sbuf_tensor("identb", [128, 128], BF16))
        identf = es.enter_context(nc.sbuf_tensor("identf", [128, 128], F32))
        permb = es.enter_context(nc.sbuf_tensor("permb", [128, 128], BF16))
        onesb = es.enter_context(nc.sbuf_tensor("onesb", [128, 512], BF16))
        nm1 = {}
        for k in ("stdv", "vare", "rstd", "negmrs", "negmax", "negmaxs", "rsum",
                  "rsum2", "rsumT", "rinv", "gnm", "gsum", "grin", "bqc", "bkc",
                  "bvc", "boc"):
            nm1[k] = es.enter_context(nc.sbuf_tensor("v_" + k, [128, 1], F32))
        dumv = es.enter_context(nc.sbuf_tensor("dumv", [128, 1], F32))
        dums = es.enter_context(nc.sbuf_tensor("dums", [128, 1], F32))
        for s_ in SEMS:
            bld.sems[s_] = es.enter_context(nc.semaphore("sem_" + s_))

        def vgap():
            op("vector", lambda E: E.tensor_copy(dumv[:], identf[:, 0:1]))

        def vmark():
            return op("vector", lambda E: E.tensor_copy(
                dumv[:], identf[:, 0:1]), inc="mV")

        def cmark():
            return op("scalar", lambda E: E.copy(dums[:], identf[:, 0:1]),
                      inc="mC")

        def dmaG(out_ap, in_ap):
            return op("gpsimd", lambda E, o=out_ap, i=in_ap: E.dma_start(out=o, in_=i),
                      inc="dG", n=16)

        def dmaS(out_ap, in_ap):
            return op("sync", lambda E, o=out_ap, i=in_ap: E.dma_start(out=o, in_=i),
                      inc="dS", n=16)

        def fin(val, sem="dS"):
            wait("sync", sem, val)

        es_a = ExitStack()
        sba = lambda n_, s_, d_: es_a.enter_context(nc.sbuf_tensor(n_, list(s_), d_))
        psa = lambda n_, s_, d_: es_a.enter_context(nc.psum_tensor(n_, list(s_), d_))
        xnT = sba("xnT", (128, ET * 2048), BF16)
        coskb = sba("coskb", (128, 2048), BF16)
        sinkb = sba("sinkb", (128, 2048), BF16)

        masks = sba("masks", (128, 256), BF16)
        gw_sb = sba("gw_sb", (128, 128), BF16)
        logits = sba("logits", (128, 64), F32)
        probs = sba("probs", (128, 64), F32)
        top8 = sba("top8", (128, 8), F32)
        mask8 = sba("mask8", (128, 8), F32)
        combf = sba("combf", (128, 64), F32)
        comb_bf = sba("comb_bf", (128, 64), BF16)
        pA = psa("pA", (128, 1024), F32)
        pS = psa("pS", (128, 2048), F32)
        pT = psa("pT", (128, 1024), BF16)
        pT2 = psa("pT2", (128, 1024), BF16)

        dmaG(identb[:], dd["ident"])
        dmaG(permb[:], dd["perm"])
        dmaG(onesb[:], dd["ones1"])
        dmaG(coskb[:], dd["cosk"])
        dmaG(sinkb[:], dd["sink"])
        for p4 in range(3):
            dmaG(bass.AP(bias4, 32 * p4 * 2816, [[2816, 1], [1, 2816]]),
                 dd["brows"][p4:p4 + 1, :])
        dG_consts = dmaG(gw_sb[:].rearrange("p (j c) -> p j c", j=16), dd["gw"])
        dmaG(masks[:], dd["mask"])
        dS_c = dmaS(identf[:], dd["ident"])

        es_l = ExitStack()
        sbl = lambda n_, s_, d_: es_l.enter_context(nc.sbuf_tensor(n_, list(s_), d_))
        xbuf = [sbl(f"xbuf{k}", (128, 2048), F32) for k in range(2)]
        xnN = [sbl(f"xnN{k}", (128, 2048), BF16) for k in range(2)]
        sqs = sbl("sqs", (128, 2048), F32)
        lnS = [{k: sbl(f"ln_{k}{p}", (128, 1), F32)
                for k in ("s1", "s2", "mean", "varb", "vare", "stdv", "rstd",
                          "negmrs")} for p in range(2)]

        lnC = [0] * 16
        lnT = [[0, 0] for _ in range(16)]
        lnEv = [[0, 0] for _ in range(16)]
        lnXld = [0] * 16
        lnVng = [0] * 16

        def ln_front(i):
            p = lnS[i % 2]
            if i >= 2:
                wait("sync", "mC", lnC[i - 2])
            lnXld[i] = dmaS(xbuf[i % 2][:], dd["x"][128 * i:128 * (i + 1), :])
            wait("vector", "dS", lnXld[i])
            op("vector", lambda E, i_=i, p_=p: E.tensor_scalar(
                sqs[:], xbuf[i_ % 2][:], 1.0, 0.0, ALU.mult, ALU.add,
                accum_out=p_["s1"][:]))
            wait("scalar", "dS", lnXld[i])
            op("scalar", lambda E, i_=i, p_=p: E.activation(
                sqs[:], xbuf[i_ % 2][:], AF.Square, accum_out=p_["s2"][:]))
            csq = cmark()
            wait("vector", "mC", csq)
            op("vector", lambda E, p_=p: E.tensor_scalar_mul(
                p_["varb"][:], p_["s2"][:], 1.0 / 2048.0))
            op("vector", lambda E, p_=p: E.scalar_tensor_tensor(
                p_["vare"][:], p_["s1"][:], 1.0 / (2048.0 * 2048.0), p_["s1"][:],
                ALU.mult, ALU.mult))
            op("vector", lambda E, p_=p: E.tensor_scalar_mul(
                p_["mean"][:], p_["s1"][:], 1.0 / 2048.0))
            op("vector", lambda E, p_=p: E.scalar_tensor_tensor(
                p_["varb"][:], p_["vare"][:], -1.0, p_["varb"][:],
                ALU.mult, ALU.add))
            vgap()
            op("vector", lambda E, p_=p: E.tensor_scalar_add(
                p_["vare"][:], p_["varb"][:], 1.0e-5))
            vst = vmark()
            wait("scalar", "mV", vst)
            op("scalar", lambda E, p_=p: E.activation(
                p_["stdv"][:], p_["vare"][:], AF.Sqrt))
            cst = cmark()
            wait("vector", "mC", cst)
            op("vector", lambda E, p_=p: E.reciprocal(
                p_["rstd"][:], p_["stdv"][:]))
            vgap()
            op("vector", lambda E, p_=p: E.scalar_tensor_tensor(
                p_["negmrs"][:], p_["mean"][:], -1.0, p_["rstd"][:],
                ALU.mult, ALU.mult))
            lnVng[i] = vmark()

        def ln_back(i):
            p = lnS[i % 2]
            wait("scalar", "mV", lnVng[i])
            if i >= 2:
                wait("scalar", "mT", lnT[i - 2][1])
            lnC[i] = op("scalar", lambda E, i_=i, p_=p: E.activation(
                xnN[i_ % 2][:], xbuf[i_ % 2][:], AF.Identity,
                bias=p_["negmrs"][:], scale=p_["rstd"][:]), inc="mC")
            wait("tensor", "dG", dG_consts)
            wait("tensor", "mC", lnC[i])
            for b2 in range(2):
                pTx = pT if b2 == 0 else pT2
                if i >= 1:
                    wait("tensor", "mV", lnEv[i - 1][b2])
                for j8 in range(8):
                    j = 8 * b2 + j8
                    fn = (lambda E, i_=i, j_=j, j8_=j8, px=pTx: E.transpose(
                        px[:, 128 * j8_:128 * (j8_ + 1)],
                        xnN[i_ % 2][:, 128 * j_:128 * (j_ + 1)], identb[:]))
                    if j8 == 7:
                        lnT[i][b2] = op("tensor", fn, inc="mT")
                    else:
                        op("tensor", fn)
                wait("vector", "mT", lnT[i][b2])
                dst = bass.AP(xnT, (8 * b2) * 2048 + 128 * i,
                              [[ET * 2048, 128], [2048, 8], [1, 128]])
                src_ = bass.AP(pTx, 0, [[1024, 128], [128, 8], [1, 128]])
                lnEv[i][b2] = op("vector",
                                 lambda E, d_=dst, s_=src_: E.tensor_copy(d_, s_),
                                 inc="mV")

        for i in range(17):
            if i < 16:
                ln_front(i)
            if i >= 1:
                ln_back(i - 1)
        for j in range(ET):
            vcomp = op("vector", lambda E, j_=j: E.tensor_copy(
                xn_own[:, 1024 * j_:1024 * (j_ + 1)],
                xnT[:, 2048 * j_:2048 * j_ + 1024]), inc="mV")
        xn_own_done = vcomp

        if stop == "xn":
            wait("gpsimd", "mV", xn_own_done)
            fin(dmaG(tap_d, xnT[:]), "dG")

        if stop != "xn":
            wait("tensor", "dG", dG_consts)
            wait("tensor", "mV", xn_own_done)
            for i in range(ST):
                for j in range(ET):
                    fn = (lambda E, i_=i, j_=j: E.matmul(
                        pA[:, 8 * i_:8 * (i_ + 1)],
                        xn_own[:, 1024 * j_ + 128 * i_:1024 * j_ + 128 * (i_ + 1)],
                        gw_sb[:, 8 * j_:8 * (j_ + 1)],
                        start=(j_ == 0), stop=(j_ == 15)))
                    if i == ST - 1 and j == ET - 1:
                        gmm = op("tensor", fn, inc="mT")
                    else:
                        op("tensor", fn)
            wait("vector", "dS", dS_c)
            wait("vector", "mT", gmm)
            op("vector", lambda E: E.tensor_copy(logits[:], pA[:, 0:64]))
            glog = vmark()
            if stop == "gat0":
                wait("sync", "mV", glog)
                fin(dmaS(tap_d, logits[:]))
            for i in range(ST if stop != "gat0" else 0):
                sl = slice(8 * i, 8 * (i + 1))
                op("vector", lambda E, s_=sl: E.reduce_max(
                    nm1["gnm"][:], logits[:, s_], axis=AX.X, negate=True))
                vgap()
                vm = vmark()
                wait("scalar", "mV", vm)
                op("scalar", lambda E, s_=sl: E.activation(
                    probs[:, s_], logits[:, s_], AF.Exp, bias=nm1["gnm"][:],
                    accum_out=nm1["gsum"][:]))
                cse = cmark()
                wait("vector", "mC", cse)
                op("vector", lambda E: E.reciprocal(nm1["grin"][:], nm1["gsum"][:]))
                vgap()
                op("vector", lambda E, s_=sl: E.tensor_scalar_mul(
                    probs[:, s_], probs[:, s_], nm1["grin"][:]))
                vgap()
                op("vector", lambda E, s_=sl: E.max(out=top8[:], in_=probs[:, s_]))
                vgap()
                op("vector", lambda E, s_=sl: E.tensor_tensor(
                    mask8[:], probs[:, s_], top8[:, 1:2].to_broadcast([128, 8]),
                    ALU.is_ge))
                vgap()
                op("vector", lambda E, s_=sl: E.tensor_mul(
                    combf[:, s_], probs[:, s_], mask8[:]))
                vgap()
                vcb = op("vector", lambda E, s_=sl: E.tensor_copy(
                    comb_bf[:, s_], combf[:, s_]), inc="mV")
            if stop == "gat1":
                wait("sync", "mV", vcb)
                fin(dmaS(tap_d, combf[:]))
            if stop not in ("gat0", "gat1"):
                wait("tensor", "mV", vcb)
                wait("tensor", "mV", lnEv[15][0])
                ctr = op("tensor", lambda E: E.transpose(
                    pT[0:64, 0:128], comb_bf[:], identb[:]), inc="mT")
                wait("vector", "mT", ctr)
                vct = op("vector", lambda E: E.tensor_copy(
                    combT64[:], pT[0:64, 0:128]), inc="mV")

        if stop == "gating":
            wait("sync", "mV", vcb)
            fin(dmaS(tap_d, combf[:]))


        es_l.close()
        es_f = es_f1 = es_fo = None

        if stop not in ("xn", "gating", "gat0", "gat1"):
            pass
        if stop not in ("xn", "gating", "gat0", "gat1"):
            wqb = [sba(f"wqb{k}", (128, 2048), BF16) for k in range(2)]
            wkb = [sba(f"wkb{k}", (128, 2048), BF16) for k in range(2)]
            wvb = [sba(f"wvb{k}", (128, 2048), BF16) for k in range(2)]
            stage1 = sba("stage1", (128, 2048), BF16)
            stage2 = sba("stage2", (128, 2048), BF16)
            qh = sba("qh", (128, 1024), BF16)
            kh = sba("kh", (128, 2048), BF16)
            vn = sba("vn", (128, 2048), BF16)
            Pb2 = [sba(f"Pb{k}", (128, 2048), BF16) for k in range(2)]
            PTq2 = [sba(f"PTq{k}", (128, 2048), BF16) for k in range(2)]
            rs8 = [[sba(f"rs_{p}_{i}", (128, 1), F32) for i in range(4)]
                   for p in range(2)]
            rsx = [{k: sba(f"rsx_{k}{p}", (128, 1), F32)
                    for k in ("a", "b", "t", "i")} for p in range(2)]

            wld = [0] * H
            wrel = [0] * H
            pvEv = [0] * H
            nheads = 1 if stop in ("head0", "h0t0") else H
            expHist = []
            trPar = [0, 0]
            ptFree = [vct, lnEv[15][1]]
            kropeL = 0
            vevL = 0

            expBank = [0, 0, 0, 0]
            scmmLast = 0

            def preg(i, c):
                if i == 0:
                    return pA[:, 512 * c:512 * (c + 1)]
                if i == 1:
                    return pS[:, 512 * c:512 * (c + 1)]
                return pS[:, 1024 + 512 * c:1024 + 512 * (c + 1)]

            for h in range(nheads):
                if h >= 2:
                    wait("gpsimd", "mT", wrel[h - 2])
                else:
                    wait("gpsimd", "mT", lnT[15][1])
                    wait("gpsimd", "mC", lnC[15])
                for wb_, nm in ((wqb, "wq"), (wkb, "wk"), (wvb, "wv")):
                    wld[h] = dmaG(wb_[h % 2][:],
                                  dd[nm][h].rearrange("p j d -> p (j d)"))

                def proj_mm(wbuf, reg, bofs, bidx):
                    last = None
                    for j in range(ET):
                        for c in range(2):
                            op("tensor", lambda E, c_=c, j_=j, w_=wbuf, r_=reg:
                               E.matmul(
                                   preg(r_, c_),
                                   w_[:, 128 * j_:128 * (j_ + 1)],
                                   (xn_own[:, 1024 * j_ + 512 * c_:
                                           1024 * j_ + 512 * (c_ + 1)]
                                    if bofs is None else
                                    xnT[:, 2048 * j_ + bofs + 512 * c_:
                                        2048 * j_ + bofs + 512 * (c_ + 1)]),
                                   start=(j_ == 0), stop=False))
                    for c in range(2):
                        fnb = (lambda E, c_=c, r_=reg, rr=bidx: E.matmul(
                            preg(r_, c_),
                            bass.AP(bias4, 32 * (rr % 3) * 2816 + 128 * (rr // 3),
                                    [[2816, 1], [1, 128]]),
                            bass.AP(onesb, 32 * (rr % 3) * 512, [[512, 1], [1, 512]]),
                            start=False, stop=True))
                        if c == 1:
                            last = op("tensor", fnb, inc="mT")
                        else:
                            op("tensor", fnb)
                    return last

                def swap_mm(reg, src_ap_fn):
                    last = None
                    for c in range(2):
                        fn = (lambda E, c_=c, r_=reg, sf=src_ap_fn: E.matmul(
                            preg(r_, c_), permb[:], sf(c_), start=True, stop=True))
                        if c == 1:
                            last = op("tensor", fn, inc="mT")
                        else:
                            op("tensor", fn)
                    return last

                wait("tensor", "dG", wld[h])
                wait("tensor", "mC", expBank[0])
                wait("tensor", "mC", expBank[1])
                qmm = proj_mm(wqb[h % 2], 1, None, h)
                wait("tensor", "mC", expBank[2])
                wait("tensor", "mC", expBank[3])
                kmm0 = proj_mm(wkb[h % 2], 2, 0, 16 + h)
                if h == 0:
                    wait("tensor", "mV", glog)
                else:
                    wait("tensor", "mV", pvEv[h - 1])
                kmm1 = proj_mm(wkb[h % 2], 0, 1024, 16 + h)
                wait("scalar", "mT", qmm)
                if h > 0:
                    wait("scalar", "mT", scmmLast)
                qev = op("scalar", lambda E: E.activation(
                    qh[:], pS[:, 0:1024], AF.Identity), inc="mC")
                wait("scalar", "mT", kmm0)
                kev0 = op("scalar", lambda E: E.activation(
                    kh[:, 0:1024], pS[:, 1024:2048], AF.Identity), inc="mC")
                wait("scalar", "mT", kmm1)
                kev1 = op("scalar", lambda E: E.activation(
                    kh[:, 1024:2048], pA[:, 0:1024], AF.Identity), inc="mC")
                wait("tensor", "mC", qev)
                qsw = swap_mm(1, lambda c: qh[:, 512 * c:512 * (c + 1)])
                wait("tensor", "mC", kev0)
                ksw0 = swap_mm(2, lambda c: kh[:, 512 * c:512 * (c + 1)])
                wait("tensor", "mC", kev1)
                ksw1 = swap_mm(0, lambda c: kh[:, 1024 + 512 * c:1536 + 512 * c])
                wait("scalar", "mT", qsw)
                if h > 0:
                    wait("scalar", "mT", wrel[h - 1])
                qsv = op("scalar", lambda E: E.activation(
                    stage2[:, 0:1024], pS[:, 0:1024], AF.Identity), inc="mC")
                wait("scalar", "mT", ksw0)
                ksv0 = op("scalar", lambda E: E.activation(
                    stage2[:, 1024:2048], pS[:, 1024:2048], AF.Identity), inc="mC")
                wait("vector", "mT", qsw)
                op("vector", lambda E: E.tensor_tensor(
                    qh[:], qh[:], coskb[:, 0:1024], ALU.mult))
                wait("vector", "mC", qsv)
                op("vector", lambda E: E.tensor_tensor(
                    stage2[:, 0:1024], stage2[:, 0:1024], sinkb[:, 0:1024],
                    ALU.mult))
                op("vector", lambda E: E.tensor_tensor(
                    qh[:], qh[:], stage2[:, 0:1024], ALU.add))
                qrope = op("vector", lambda E: E.tensor_scalar_mul(
                    qh[:], qh[:], ISQD), inc="mV")
                wait("scalar", "mT", ksw1)
                wait("scalar", "mV", qrope)
                ksv1 = op("scalar", lambda E: E.activation(
                    stage2[:, 0:1024], pA[:, 0:1024], AF.Identity), inc="mC")
                wait("vector", "mT", ksw0)
                op("vector", lambda E: E.tensor_tensor(
                    kh[:, 0:1024], kh[:, 0:1024], coskb[:, 0:1024], ALU.mult))
                wait("vector", "mC", ksv0)
                op("vector", lambda E: E.tensor_tensor(
                    stage2[:, 1024:2048], stage2[:, 1024:2048],
                    sinkb[:, 0:1024], ALU.mult))
                krope0 = op("vector", lambda E: E.tensor_tensor(
                    kh[:, 0:1024], kh[:, 0:1024], stage2[:, 1024:2048], ALU.add),
                    inc="mV")
                wait("vector", "mT", ksw1)
                op("vector", lambda E: E.tensor_tensor(
                    kh[:, 1024:2048], kh[:, 1024:2048], coskb[:, 1024:2048],
                    ALU.mult))
                wait("vector", "mC", ksv1)
                op("vector", lambda E: E.tensor_tensor(
                    stage2[:, 0:1024], stage2[:, 0:1024], sinkb[:, 1024:2048],
                    ALU.mult))
                kropeL = op("vector", lambda E: E.tensor_tensor(
                    kh[:, 1024:2048], kh[:, 1024:2048], stage2[:, 0:1024],
                    ALU.add), inc="mV")
                if stop == "h0q":
                    wait("gpsimd", "mV", qrope)
                    fin(dmaG(tap_d[:, 0:1024], qh[:]), "dG")
                    break
                if stop == "h0k":
                    wait("gpsimd", "mV", kropeL)
                    fin(dmaG(tap_d, kh[:]), "dG")
                    break
                wait("tensor", "mC", qsv)
                vmm0 = proj_mm(wvb[h % 2], 1, 0, 32 + h)
                wait("tensor", "mC", ksv0)
                vmm1 = proj_mm(wvb[h % 2], 2, 1024, 32 + h)
                wait("scalar", "mT", vmm0)
                wait("scalar", "mV", krope0)
                vev0 = op("scalar", lambda E: E.activation(
                    stage2[:, 1024:2048], pS[:, 0:1024], AF.Identity), inc="mC")
                wait("scalar", "mT", vmm1)
                wait("scalar", "mV", kropeL)
                vev1 = op("scalar", lambda E: E.activation(
                    stage2[:, 0:1024], pS[:, 1024:2048], AF.Identity), inc="mC")
                vevL = vev1
                vnev = [0, 0]
                for b2 in range(2):
                    pTx = pT if b2 == 0 else pT2
                    wait("tensor", "mC", vev0 if b2 == 0 else vev1)
                    wait("tensor", "mV", ptFree[b2])
                    for j8 in range(8):
                        scol = (1024 + 128 * j8) if b2 == 0 else (128 * j8)
                        fn = (lambda E, s_=scol, j8_=j8, px=pTx: E.transpose(
                            px[:, 128 * j8_:128 * (j8_ + 1)],
                            stage2[:, s_:s_ + 128], identb[:]))
                        if j8 == 7:
                            vtr = op("tensor", fn, inc="mT")
                        else:
                            op("tensor", fn)
                    wait("vector", "mT", vtr)
                    vnev[b2] = op("vector", lambda E, b2_=b2: E.tensor_copy(
                        vn[:, 1024 * b2_:1024 * (b2_ + 1)],
                        (pT if b2_ == 0 else pT2)[:]), inc="mV")
                ptFree = [vnev[0], vnev[1]]
                wrel[h] = vtr
                if stop == "h0v":
                    wait("gpsimd", "mV", vnev[1])
                    fin(dmaG(tap_d, vn[:]), "dG")
                    break

                pvE = [0] * ST
                pend = None

                def attn_tail(t, psc):
                    nonlocal ptFree
                    par = t % 2
                    L = 128 * (t + 1)
                    wait("tensor", "mV", psc)
                    ptEv = [0, 0]
                    for b2 in range(2):
                        pTx = pT if b2 == 0 else pT2
                        wait("tensor", "mV", ptFree[b2])
                        for u in range(t + 1):
                            src_col = (L if b2 else 0) + 128 * u
                            fn = (lambda E, s_=src_col, u_=u, px=pTx, p_=par:
                                  E.transpose(
                                      px[:, 128 * u_:128 * (u_ + 1)],
                                      Pb2[p_][:, s_:s_ + 128], identb[:]))
                            if u == t:
                                tr_ = op("tensor", fn, inc="mT")
                            else:
                                op("tensor", fn)
                        wait("vector", "mT", tr_)
                        ptEv[b2] = op("vector", lambda E, t_=t, b2_=b2, p_=par:
                                      E.tensor_copy(
                                          PTq2[p_][:, 1024 * b2_:
                                                    1024 * b2_ + 128 * (t_ + 1)],
                                          (pT if b2_ == 0 else pT2)
                                          [:, 0:128 * (t_ + 1)]), inc="mV")
                    ptFree = [ptEv[0], ptEv[1]]
                    trPar[par] = tr_
                    pvreg = 512 * par
                    wait("tensor", "mV", ptEv[1])
                    if t == 0:
                        wait("tensor", "mC", ksv1)
                    if t >= 2:
                        wait("tensor", "mV", pvE[t - 2])
                    elif h > 0:
                        wait("tensor", "mV", pvEv[h - 1])
                    first = True
                    for b2 in range(2):
                        for u in range(t + 1):
                            kslot = u if b2 == 0 else 8 + u
                            vcol = 128 * u if b2 == 0 else 1024 + 128 * u
                            last = (b2 == 1 and u == t)
                            fn = (lambda E, v_=vcol, k_=kslot, p_=pvreg, f_=first,
                                  l_=last, pp=par: E.matmul(
                                      pA[:, p_:p_ + 128], vn[:, v_:v_ + 128],
                                      PTq2[pp][:, 128 * k_:128 * (k_ + 1)],
                                      start=f_, stop=l_))
                            if last:
                                pvt_ = op("tensor", fn, inc="mT")
                            else:
                                op("tensor", fn)
                            first = False
                    wait("vector", "mT", pvt_)
                    pvE[t] = op("vector", lambda E, h_=h, t_=t, p_=pvreg:
                                E.tensor_copy(
                                    attn_out[:, 1024 * h_ + 128 * t_:
                                             1024 * h_ + 128 * (t_ + 1)],
                                    pA[:, p_:p_ + 128]), inc="mV")

                for t in range(ST):
                    L = 128 * (t + 1)
                    par = t % 2
                    chunks = []
                    for b2, base in ((0, 0), (1, 1024)):
                        off = 0
                        while off < L:
                            cw = min(512, L - off)
                            chunks.append((base, off, cw, b2))
                            off += cw
                    nch = len(chunks)
                    for ci, (base, off, cw, b2) in enumerate(chunks):
                        gci = len(expHist)
                        bank = gci % 4
                        bfree = vev0 if bank < 2 else vev1
                        wait("tensor", "mC", max(expBank[bank], bfree))
                        if t == 0 and ci == 0:
                            wait("tensor", "mV", qrope)
                            wait("tensor", "mV", kropeL)
                        scmm = op("tensor",
                                  lambda E, t_=t, b_=base, o_=off, w_=cw, bk=bank:
                                  E.matmul(
                                      pS[:, 512 * bk:512 * bk + w_],
                                      qh[:, 128 * t_:128 * (t_ + 1)],
                                      kh[:, b_ + o_:b_ + o_ + w_],
                                      start=True, stop=True), inc="mT")
                        scmmLast = scmm
                        ismask = (off + cw == L)
                        if ismask:
                            mlocal = 128 * t - off
                            wait("vector", "mT", scmm)
                            op("vector", lambda E, b2_=b2, bk=bank, ml=mlocal:
                               E.tensor_tensor(
                                   pS[:, 512 * bk + ml:512 * bk + ml + 128],
                                   pS[:, 512 * bk + ml:512 * bk + ml + 128],
                                   masks[:, 128 * b2_:128 * b2_ + 128],
                                   ALU.add))
                            mrd = vmark()
                            wait("scalar", "mV", mrd)
                        else:
                            wait("scalar", "mT", scmm)
                        if ci == 0:
                            wait("scalar", "mT", trPar[par])
                        outc = (L + off) if b2 else off
                        expi = op("scalar",
                                  lambda E, p_=par, oc=outc, w_=cw, bk=bank,
                                  ci_=ci: E.activation(
                                      Pb2[p_][:, oc:oc + w_],
                                      pS[:, 512 * bk:512 * bk + w_], AF.Exp,
                                      accum_out=rs8[p_][ci_][:]), inc="mC")
                        expHist.append(expi)
                        expBank[bank] = expi
                    wait("vector", "mC", expHist[-1])
                    rx = rsx[par]
                    if nch == 2:
                        op("vector", lambda E, p_=par: E.tensor_add(
                            rx["t"][:], rs8[p_][0][:], rs8[p_][1][:]))
                    else:
                        op("vector", lambda E, p_=par: E.tensor_add(
                            rx["a"][:], rs8[p_][0][:], rs8[p_][1][:]))
                        op("vector", lambda E, p_=par: E.tensor_add(
                            rx["b"][:], rs8[p_][2][:], rs8[p_][3][:]))
                        vgap()
                        op("vector", lambda E: E.tensor_add(
                            rx["t"][:], rx["a"][:], rx["b"][:]))
                    vgap()
                    op("vector", lambda E: E.reciprocal(rx["i"][:], rx["t"][:]))
                    vgap()
                    psc = op("vector", lambda E, p_=par, L_=L: E.tensor_scalar_mul(
                        Pb2[p_][:, 0:2 * L_], Pb2[p_][:, 0:2 * L_], rx["i"][:]),
                        inc="mV")
                    if pend is not None:
                        attn_tail(*pend)
                    pend = (t, psc)
                attn_tail(*pend)
                pvEv[h] = pvE[ST - 1]

            attn_done_V = pvEv[nheads - 1]

            if stop == "head0":
                wait("gpsimd", "mV", qrope)
                dmaG(tap_d[:, 0:1024], qh[:])
                wait("gpsimd", "mV", kropeL)
                dmaG(tap_d[:, 2048:4096], kh[:])
                wait("gpsimd", "mV", vnev[1])
                dmaG(tap_d[:, 4096:6144], vn[:])
                wait("gpsimd", "mV", attn_done_V)
                fin(dmaG(tap_d[:, 6144:7168], attn_out[:, 0:1024]), "dG")
            if stop == "attn":
                wait("gpsimd", "mV", attn_done_V)
                fin(dmaG(tap_d, attn_out[:]), "dG")

        if stop in (None, "oproj", "oproj0", "ffn"):
            es_a.close()
            bT, bV, bC = C["mT"], C["mV"], C["mC"]
            for eng in ENGINES:
                wait(eng, "mT", bT)
                wait(eng, "mV", bV)
                wait(eng, "mC", bC)

            es_f = ExitStack()
            sbf_ = lambda n_, s_, d_: es_f.enter_context(
                nc.sbuf_tensor(n_, list(s_), d_))
            psf_ = lambda n_, s_, d_: es_f.enter_context(
                nc.psum_tensor(n_, list(s_), d_))
            acc = sbf_("acc", (128, HT * 1024), F32)
            pW = psf_("pW", (128, 1024), F32)
            pX = psf_("pX", (128, 1024), F32)
            pY = psf_("pY", (128, 1024), F32)
            pZ = psf_("pZ", (128, 1024), F32)
            es_fo = ExitStack()
            wob = [es_fo.enter_context(nc.sbuf_tensor(f"wob{k}", [128, 2048], BF16))
                   for k in range(2)]

            woL = {}
            woR = {}
            oEv = [0] * HT
            woL[0] = dmaG(wob[0][:], dd["wo"][0].rearrange("p j d -> p (j d)"))
            woL[1] = dmaG(wob[1][:], dd["wo"][1].rearrange("p j d -> p (j d)"))
            for fb in range(HT if stop != "oproj0" else 16):
                pP = pW if fb % 2 == 0 else pX
                wait("tensor", "dG", woL[fb])
                if fb >= 2:
                    wait("tensor", "mC", oEv[fb - 2])
                for j in range(ET):
                    for c in range(2):
                        op("tensor", lambda E, f_=fb, c_=c, j_=j, p_=pP: E.matmul(
                            p_[:, 512 * c_:512 * (c_ + 1)],
                            wob[f_ % 2][:, 128 * j_:128 * (j_ + 1)],
                            attn_out[:, 1024 * j_ + 512 * c_:
                                     1024 * j_ + 512 * (c_ + 1)],
                            start=(j_ == 0), stop=False))
                for c in range(2):
                    fnb = (lambda E, f_=fb, c_=c, p_=pP: E.matmul(
                        p_[:, 512 * c_:512 * (c_ + 1)],
                        bass.AP(bias4,
                                32 * ((48 + f_) % 3) * 2816 + 128 * ((48 + f_) // 3),
                                [[2816, 1], [1, 128]]),
                        bass.AP(onesb, 32 * ((48 + f_) % 3) * 512,
                                [[512, 1], [1, 512]]),
                        start=False, stop=True))
                    if c == 1:
                        woR[fb] = op("tensor", fnb, inc="mT")
                    else:
                        op("tensor", fnb)
                if fb + 2 < HT:
                    wait("gpsimd", "mT", woR[fb])
                    woL[fb + 2] = dmaG(wob[fb % 2][:],
                                       dd["wo"][fb + 2].rearrange("p j d -> p (j d)"))
                wait("scalar", "mT", woR[fb])
                oEv[fb] = op("scalar", lambda E, f_=fb, p_=pP: E.activation(
                    acc[:, 1024 * f_:1024 * (f_ + 1)], p_[:], AF.Identity),
                    inc="mC")
            o_done = oEv[HT - 1] if stop != "oproj0" else oEv[0]
            if stop == "oproj0":
                wait("sync", "mC", o_done)
                fin(dmaS(tap_d, acc[:, 0:1024]))

            if stop == "oproj":
                wait("sync", "mC", o_done)
                fin(dmaS(tap_d, acc[:]))

        if stop in (None, "ffn"):
            es_fo.close()
            es_f1 = ExitStack()
            sbf1 = lambda n_, s_, d_: es_f1.enter_context(
                nc.sbuf_tensor(n_, list(s_), d_))
            gub = [sbf1(f"gub{k}", (128, 4096), BF16) for k in range(2)]
            dnb = [sbf1(f"dnb{k}", (128, 2816), BF16) for k in range(2)]
            act = sbf1("act", (128, FT * 1024), BF16)
            bc_sb = sbf1("bc_sb", (128, 1024), BF16)
            t1_sb = sbf1("t1_sb", (128, 1024), BF16)
            combT2 = sbf1("combT2", (1, 8192), BF16)
            wait("gpsimd", "mT", woR[HT - 1])
            wait("gpsimd", "mV", vct)
            for e in range(8):
                dG_comb = dmaG(
                    bass.AP(combT2, e * 1024, [[8192, 1], [128, 8], [1, 128]]),
                    bass.AP(combT64, e * 128, [[8 * 128, 8], [1, 128]]))
            guL = {}
            guR = {}
            dnL = {}
            dnR = {}
            guL[0] = dmaG(gub[0][:], dd["guw"][0, 0])
            guL[1] = dmaG(gub[1][:], dd["guw"][0, 1])
            dnL[0] = dmaG(dnb[0][:], dd["dnw"][0, 0])
            dnL[1] = dmaG(dnb[1][:], dd["dnw"][0, 1])

            dnAdd = 0
            bcV = 0
            guEvLast = 0
            for e in range(NE):
                gated = e < 8
                if gated:
                    wait("tensor", "dG", dG_comb)
                    if e > 0:
                        wait("tensor", "mV", bcV)
                        wait("tensor", "mV", guEvLast)
                    else:
                        wait("tensor", "mC", o_done)
                    for c in range(2):
                        fn = (lambda E, e_=e, c_=c: E.matmul(
                            pY[:, 512 * c_:512 * (c_ + 1)], onesb[0:1, 0:128],
                            combT2[0:1, 1024 * e_ + 512 * c_:
                                   1024 * e_ + 512 * (c_ + 1)],
                            start=True, stop=True))
                        if c == 1:
                            bcT = op("tensor", fn, inc="mT")
                        else:
                            op("tensor", fn)
                    wait("vector", "mT", bcT)
                    bcV = op("vector", lambda E: E.tensor_copy(bc_sb[:], pY[:]),
                             inc="mV")
                guEv = [0] * FT
                for jg in range(FT):
                    k = e * FT + jg
                    pair = (pW, pX) if jg % 2 == 0 else (pY, pZ)
                    wait("tensor", "dG", guL[k])
                    if jg >= 2:
                        wait("tensor", "mV", guEv[jg - 2])
                    elif e == 0:
                        wait("tensor", "mC", o_done)
                    else:
                        wait("tensor", "mV", dnAdd)
                        if jg == 1 and gated:
                            wait("tensor", "mV", bcV)
                    for half2 in range(2):
                        pP = pair[half2]
                        for j in range(ET):
                            for c in range(2):
                                fn = (lambda E, k_=k, h2=half2, c_=c, j_=j, p_=pP:
                                      E.matmul(
                                          p_[:, 512 * c_:512 * (c_ + 1)],
                                          gub[k_ % 2][:, 2048 * h2 + 128 * j_:
                                                      2048 * h2 + 128 * (j_ + 1)],
                                          xn_own[:, 1024 * j_ + 512 * c_:
                                                 1024 * j_ + 512 * (c_ + 1)],
                                          start=(j_ == 0), stop=(j_ == 15)))
                                if half2 == 1 and c == 1 and j == ET - 1:
                                    guR[k] = op("tensor", fn, inc="mT")
                                else:
                                    op("tensor", fn)
                    if k + 2 < NE * FT:
                        e2, jg2 = divmod(k + 2, FT)
                        wait("gpsimd", "mT", guR[k])
                        guL[k + 2] = dmaG(gub[k % 2][:], dd["guw"][e2, jg2])
                    wait("scalar", "mT", guR[k])
                    if jg >= 1:
                        wait("scalar", "mV", guEv[jg - 1])
                    elif e > 0:
                        wait("scalar", "mV", guEvPrevLast)
                    sil = op("scalar", lambda E, p_=pair[0]: E.activation(
                        t1_sb[:], p_[:], AF.Silu), inc="mC")
                    wait("vector", "mC", sil)
                    if gated:
                        wait("vector", "mV", bcV)
                    op("vector", lambda E, p_=pair[1]: E.tensor_tensor(
                        t1_sb[:], t1_sb[:], p_[:], ALU.mult))
                    if gated:
                        guEv[jg] = op("vector", lambda E, g_=jg: E.tensor_tensor(
                            act[:, 1024 * g_:1024 * (g_ + 1)], t1_sb[:], bc_sb[:],
                            ALU.mult), inc="mV")
                    else:
                        guEv[jg] = op("vector", lambda E, g_=jg: E.tensor_copy(
                            act[:, 1024 * g_:1024 * (g_ + 1)], t1_sb[:]), inc="mV")
                guEvPrevLast = guEv[FT - 1]
                guEvLast = guEv[FT - 1]
                dnAddPrev = [guEv[FT - 1], guEv[FT - 2]]
                for hb in range(HT):
                    c4, hb2 = hb // 2, hb % 2
                    m = e * 8 + c4
                    pP = pW if hb % 2 == 0 else pX
                    wait("tensor", "dG", dnL[m])
                    wait("tensor", "mV", guEvLast)
                    if hb >= 2:
                        wait("tensor", "mV", dnAddPrev[hb % 2])
                    for f in range(FT):
                        for c in range(2):
                            fn = (lambda E, m_=m, h2=hb2, f_=f, c_=c, p_=pP:
                                  E.matmul(
                                      p_[:, 512 * c_:512 * (c_ + 1)],
                                      dnb[m_ % 2][:, 1408 * h2 + 128 * f_:
                                                  1408 * h2 + 128 * (f_ + 1)],
                                      act[:, 1024 * f_ + 512 * c_:
                                          1024 * f_ + 512 * (c_ + 1)],
                                      start=(f_ == 0), stop=(f_ == FT - 1)))
                            if f == FT - 1 and c == 1:
                                dmm = op("tensor", fn, inc="mT")
                            else:
                                op("tensor", fn)
                    if hb2 == 1:
                        dnR[m] = dmm
                        if m + 2 < NE * 8:
                            e2, c42 = divmod(m + 2, 8)
                            wait("gpsimd", "mT", dnR[m])
                            dnL[m + 2] = dmaG(dnb[m % 2][:], dd["dnw"][e2, c42])
                    wait("vector", "mT", dmm)
                    dnAdd = op("vector", lambda E, hb_=hb, p_=pP: E.tensor_tensor(
                        acc[:, 1024 * hb_:1024 * (hb_ + 1)],
                        acc[:, 1024 * hb_:1024 * (hb_ + 1)], p_[:], ALU.add),
                        inc="mV")
                    dnAddPrev[hb % 2] = dnAdd
            ffn_done = dnAdd

            if stop == "ffn":
                wait("sync", "mV", ffn_done)
                fin(dmaS(tap_d, acc[:]))

        if stop is None:
            es_f1.close()
            xrb = []
            outb = []
            for k in range(2):
                xrb.append(es_f.enter_context(
                    nc.sbuf_tensor(f"xrb{k}", [128, 2048], F32)))
                outb.append(es_f.enter_context(
                    nc.sbuf_tensor(f"outb{k}", [128, 2048], F32)))
            wait("sync", "mV", ffn_done)
            finV = [0] * ST
            stO = [0] * ST
            for i in range(ST):
                if i >= 2:
                    wait("sync", "mV", finV[i - 2])
                xrl = dmaS(xrb[i % 2][:], dd["x"][128 * i:128 * (i + 1), :])
                pPair = (pY, pZ) if i % 2 == 0 else (pW, pX)
                wait("tensor", "mV", ffn_done)
                if i >= 2:
                    wait("tensor", "mV", finV[i - 2])
                trF = [0, 0]
                for g in range(2):
                    pP = pPair[g]
                    for b4 in range(8):
                        et = 8 * g + b4
                        fn = (lambda E, e_=et, b_=b4, p_=pP, i_=i: E.transpose(
                            p_[:, 128 * b_:128 * (b_ + 1)],
                            acc[:, 1024 * e_ + 128 * i_:1024 * e_ + 128 * (i_ + 1)],
                            identf[:]))
                        if b4 == 7:
                            trF[g] = op("tensor", fn, inc="mT")
                        else:
                            op("tensor", fn)
                wait("vector", "mT", trF[1])
                wait("vector", "dS", xrl)
                if i >= 2:
                    wait("vector", "dS", stO[i - 2])
                for g in range(2):
                    pP = pPair[g]
                    for c in range(2):
                        fn = (lambda E, g_=g, c_=c, p_=pP, i_=i: E.tensor_tensor(
                            outb[i_ % 2][:, 1024 * g_ + 512 * c_:
                                         1024 * g_ + 512 * (c_ + 1)],
                            xrb[i_ % 2][:, 1024 * g_ + 512 * c_:
                                        1024 * g_ + 512 * (c_ + 1)],
                            p_[:, 512 * c_:512 * (c_ + 1)], ALU.add))
                        if g == 1 and c == 1:
                            finV[i] = op("vector", fn, inc="mV")
                        else:
                            op("vector", fn)
                wait("sync", "mV", finV[i])
                stO[i] = dmaS(out_d[i], outb[i % 2][:])
            fin(stO[ST - 1])

        if es_f1 is not None:
            es_f1.close()
        if es_fo is not None:
            es_fo.close()
        if es_f is not None:
            es_f.close()
        es_a.close()

        block = es.enter_context(nc.Block())

        @block.tensor
        def _(E):
            bld.replay("tensor", E)

        @block.vector
        def _(E):
            bld.replay("vector", E)

        @block.scalar
        def _(E):
            bld.replay("scalar", E)

        @block.sync
        def _(E):
            bld.replay("sync", E)

        @block.gpsimd
        def _(E):
            bld.replay("gpsimd", E)

    nc.finalize()
    return nc


def kernel(**inputs):
    global _LAST
    shared = _prep_shared(inputs)
    cosg, sing = _rope_tables()
    nc = _build(_STOP)
    in_maps = []
    for c in range(8):
        b, half = c // 2, c % 2
        m = dict(shared)
        m.update(_prep_core(inputs, b, half, cosg, sing))
        in_maps.append(m)
    res = run_bass_kernel_spmd(nc, in_maps, core_ids=list(range(8)))
    out = np.zeros((4, 2048, 2048), np.float32)
    _LAST = {}
    for c in range(8):
        b, half = c // 2, c % 2
        own, _ = _core_perm(half)
        out[b][own] = np.asarray(res.results[c]["out"]).reshape(1024, 2048)
        if _STOP is not None:
            _LAST[c] = np.asarray(res.results[c]["tap"])
    return out


# version 12 (final)
# speedup vs baseline: 1.1223x; 1.0082x over previous
import numpy as np
from contextlib import ExitStack

import concourse.bass as bass
import concourse.mybir as mybir
from concourse.bass_utils import run_bass_kernel_spmd

F32 = mybir.dt.float32
BF16 = mybir.dt.bfloat16
AF = mybir.ActivationFunctionType
ALU = mybir.AluOpType
AX = mybir.AxisListType

E2 = 2048
H = 16
ET = 16
SOWN = 1024
ST = 8
NE = 10
FT = 11
HT = 16
NEG = -1.0e30
ISQD = 1.0 / float(np.sqrt(128.0))

ENGINES = ("tensor", "vector", "scalar", "sync", "gpsimd")

_STOP = None
_LAST = {}


class _Bld:
    def __init__(self, semnames):
        self.streams = {e: [] for e in ENGINES}
        self.C = {n: 0 for n in semnames}
        self.sems = {}
        self._trun = 0

    def raw(self, eng, fn):
        self.streams[eng].append(fn)

    def wait(self, eng, sem, val):
        if val <= 0:
            return
        assert val <= self.C[sem], f"wait {sem}>={val} but only {self.C[sem]} planned"
        if eng == "tensor":
            self._trun = 0
        self.raw(eng, lambda E, s=sem, v=val: E.wait_ge(self.sems[s], v))

    def op(self, eng, fn, inc=None, n=1):
        if eng == "tensor":
            self._trun += 1
            if self._trun >= 4:
                self.streams[eng].append(lambda E: E.nop())
                self._trun = 0
        if inc is None:
            self.raw(eng, fn)
            return None
        self.C[inc] += n
        self.raw(eng, lambda E, f=fn, s=inc, k=n: f(E).then_inc(self.sems[s], k))
        return self.C[inc]

    def replay(self, eng_name, E):
        for fn in self.streams[eng_name]:
            fn(E)


def _lay_qkv(W):
    return np.ascontiguousarray(
        W.T.reshape(16, 128, 16, 128).transpose(2, 1, 0, 3))


def _prep_shared(inp):
    out = {}
    out["wq"] = _lay_qkv(np.asarray(inp["L0_attn_Wq"], np.float32))
    out["wk"] = _lay_qkv(np.asarray(inp["L0_attn_Wk"], np.float32))
    out["wv"] = _lay_qkv(np.asarray(inp["L0_attn_Wv"], np.float32))
    out["wo"] = _lay_qkv(np.asarray(inp["L0_attn_Wo"], np.float32))
    bflat = np.concatenate(
        [np.asarray(inp[k], np.float32) for k in
         ("L0_attn_bq", "L0_attn_bk", "L0_attn_bv", "L0_attn_bo")]).reshape(64, 128)
    b4 = np.zeros((3, 22, 128), np.float32)
    for r in range(64):
        b4[r % 3, r // 3] = bflat[r]
    out["brows"] = b4.reshape(3, 2816)
    out["gw"] = np.ascontiguousarray(
        np.asarray(inp["L0_ffn_gate_weight"], np.float32).T
        .reshape(16, 128, 8).transpose(1, 0, 2))
    gus = []
    for e in range(8):
        Wt = np.asarray(inp["L0_ffn_gate_up_weights"][e], np.float32).T
        gus.append(Wt.reshape(16, 128, 2, 11, 128)
                   .transpose(3, 1, 2, 0, 4).reshape(11, 128, 4096))
    St = np.asarray(inp["L0_ffn_shared_gate_up"], np.float32).T
    sg = (St.reshape(16, 128, 2, 2, 11, 128)
          .transpose(3, 4, 1, 2, 0, 5).reshape(2, 11, 128, 4096))
    out["guw"] = np.ascontiguousarray(
        np.concatenate([np.stack(gus, 0), sg], axis=0))
    dns = []
    for e in range(8):
        Dt = np.asarray(inp["L0_ffn_down_weights"][e], np.float32).T
        dns.append(Dt.reshape(11, 128, 8, 2, 128)
                   .transpose(2, 1, 3, 0, 4).reshape(8, 128, 2816))
    St2 = np.asarray(inp["L0_ffn_shared_down"], np.float32).T
    sd2 = (St2.reshape(2, 11, 128, 8, 2, 128)
           .transpose(0, 3, 2, 4, 1, 5).reshape(2, 8, 128, 2816))
    out["dnw"] = np.ascontiguousarray(
        np.concatenate([np.stack(dns, 0), sd2], axis=0))
    out["ident"] = np.eye(128, dtype=np.float32)
    perm = np.zeros((128, 128), np.float32)
    for j in range(128):
        perm[j + 1 if j % 2 == 0 else j - 1, j] = 1.0
    out["perm"] = perm
    out["ones1"] = np.ones((128, 512), np.float32)
    return out


def _rope_tables():
    j = np.arange(128)
    theta = 10000.0 ** (-(2.0 * (j // 2)) / 128.0)
    ang = np.outer(theta, np.arange(2048).astype(np.float64))
    cosg = np.cos(ang).astype(np.float32)
    sign = np.where(j % 2 == 0, -1.0, 1.0)
    sing = (np.sin(ang) * sign[:, None]).astype(np.float32)
    return cosg, sing


def _core_perm(half):
    own = np.concatenate(
        [np.arange(256 * t + 128 * half, 256 * t + 128 * half + 128)
         for t in range(8)])
    other = np.concatenate(
        [np.arange(256 * t + 128 * (1 - half), 256 * t + 128 * (1 - half) + 128)
         for t in range(8)])
    return own, np.concatenate([own, other])


def _prep_core(inp, b, half, cosg, sing):
    own, perm = _core_perm(half)
    d = {}
    xb = np.asarray(inp["L0_attn_x"], np.float32)[b]
    d["x"] = np.ascontiguousarray(xb[perm])
    d["xT"] = np.ascontiguousarray(
        xb[own].T.reshape(16, 128, 1024).transpose(1, 0, 2))
    d["cosk"] = np.ascontiguousarray(cosg[:, perm])
    d["sink"] = np.ascontiguousarray(sing[:, perm])
    m = np.zeros((128, 256), np.float32)
    tri = np.where(np.arange(128)[None, :] <= np.arange(128)[:, None], 0.0, NEG)
    m[:, 0:128] = tri.astype(np.float32)
    if half == 0:
        m[:, 128:256] = NEG
    d["mask"] = m
    return d


def _build(stop=None):
    nc = bass.Bass("TRN2", target_bir_lowering=False)
    dd = {}

    def din(name, shape):
        dd[name] = nc.dram_tensor(name, list(shape), F32, kind="ExternalInput").ap()

    din("x", (2048, 2048))
    din("xT", (128, 16, 1024))
    din("wq", (16, 128, 16, 128)); din("wk", (16, 128, 16, 128))
    din("wv", (16, 128, 16, 128)); din("wo", (16, 128, 16, 128))
    din("brows", (3, 2816))
    din("gw", (128, 16, 8))
    din("guw", (NE, FT, 128, 4096))
    din("dnw", (NE, 8, 128, 2816))
    din("cosk", (128, 2048)); din("sink", (128, 2048))
    din("mask", (128, 256))
    din("ident", (128, 128)); din("perm", (128, 128)); din("ones1", (128, 512))
    out_d = nc.dram_tensor("out", [128, ET * 1024], F32, kind="ExternalOutput").ap()
    tap_shapes = {"xn": [128, ET * 2048], "gating": [128, 64],
                  "gat0": [128, 64], "gat1": [128, 64], "h0q": [128, 1024],
                  "h0k": [128, 2048], "h0v": [128, 2048], "h0t0": [128, 2048],
                  "head0": [128, 5 * 2048], "attn": [128, H * 1024],
                  "oproj": [128, HT * 1024], "oproj0": [128, 1024],
                  "ffn": [128, HT * 1024]}
    tap_d = None
    if stop is not None:
        tap_d = nc.dram_tensor("tap", tap_shapes[stop], F32,
                               kind="ExternalOutput").ap()

    SEMS = ("dG", "dS", "mT", "mV", "mC")
    bld = _Bld(SEMS)
    op, wait, C = bld.op, bld.wait, bld.C
    es = ExitStack()

    with es:
        xn_own = es.enter_context(nc.sbuf_tensor("xn_own", [128, ET * 1024], BF16))
        ao_w = 2 * 1024 if stop == "h0t0" else H * 1024
        attn_out = es.enter_context(nc.sbuf_tensor("attn_out", [128, ao_w], BF16))

        combT64 = es.enter_context(nc.sbuf_tensor("combT64", [64, 128], BF16))
        bias4 = es.enter_context(nc.sbuf_tensor("bias4", [128, 2816], BF16))
        identb = es.enter_context(nc.sbuf_tensor("identb", [128, 128], BF16))
        identf = es.enter_context(nc.sbuf_tensor("identf", [128, 128], F32))
        permb = es.enter_context(nc.sbuf_tensor("permb", [128, 128], BF16))
        onesb = es.enter_context(nc.sbuf_tensor("onesb", [128, 512], BF16))
        nm1 = {}
        for k in ("stdv", "vare", "rstd", "negmrs", "negmax", "negmaxs", "rsum",
                  "rsum2", "rsumT", "rinv", "gnm", "gsum", "grin", "bqc", "bkc",
                  "bvc", "boc"):
            nm1[k] = es.enter_context(nc.sbuf_tensor("v_" + k, [128, 1], F32))
        dumv = es.enter_context(nc.sbuf_tensor("dumv", [128, 1], F32))
        dums = es.enter_context(nc.sbuf_tensor("dums", [128, 1], F32))
        for s_ in SEMS:
            bld.sems[s_] = es.enter_context(nc.semaphore("sem_" + s_))

        def vgap():
            op("vector", lambda E: E.tensor_copy(dumv[:], identf[:, 0:1]))

        def vmark():
            return op("vector", lambda E: E.tensor_copy(
                dumv[:], identf[:, 0:1]), inc="mV")

        def cmark():
            return op("scalar", lambda E: E.copy(dums[:], identf[:, 0:1]),
                      inc="mC")

        def dmaG(out_ap, in_ap):
            return op("gpsimd", lambda E, o=out_ap, i=in_ap: E.dma_start(out=o, in_=i),
                      inc="dG", n=16)

        def dmaS(out_ap, in_ap):
            return op("sync", lambda E, o=out_ap, i=in_ap: E.dma_start(out=o, in_=i),
                      inc="dS", n=16)

        def fin(val, sem="dS"):
            wait("sync", sem, val)

        es_a = ExitStack()
        sba = lambda n_, s_, d_: es_a.enter_context(nc.sbuf_tensor(n_, list(s_), d_))
        psa = lambda n_, s_, d_: es_a.enter_context(nc.psum_tensor(n_, list(s_), d_))
        xnT = sba("xnT", (128, ET * 2048), BF16)
        coskb = sba("coskb", (128, 2048), BF16)
        sinkb = sba("sinkb", (128, 2048), BF16)

        masks = sba("masks", (128, 256), BF16)
        gw_sb = sba("gw_sb", (128, 128), BF16)
        logits = sba("logits", (128, 64), F32)
        probs = sba("probs", (128, 64), F32)
        top8 = sba("top8", (128, 8), F32)
        mask8 = sba("mask8", (128, 8), F32)
        combf = sba("combf", (128, 64), F32)
        comb_bf = sba("comb_bf", (128, 64), BF16)
        pA = psa("pA", (128, 1024), F32)
        pS = psa("pS", (128, 2048), F32)
        pT = psa("pT", (128, 1024), BF16)
        pT2 = psa("pT2", (128, 1024), BF16)

        dmaG(identb[:], dd["ident"])
        dmaG(permb[:], dd["perm"])
        dmaG(onesb[:], dd["ones1"])
        dmaG(coskb[:], dd["cosk"])
        dmaG(sinkb[:], dd["sink"])
        for p4 in range(3):
            dmaG(bass.AP(bias4, 32 * p4 * 2816, [[2816, 1], [1, 2816]]),
                 dd["brows"][p4:p4 + 1, :])
        dG_consts = dmaG(gw_sb[:].rearrange("p (j c) -> p j c", j=16), dd["gw"])
        dmaG(masks[:], dd["mask"])
        dS_c = dmaS(identf[:], dd["ident"])

        es_l = ExitStack()
        sbl = lambda n_, s_, d_: es_l.enter_context(nc.sbuf_tensor(n_, list(s_), d_))
        xbuf = [sbl(f"xbuf{k}", (128, 2048), F32) for k in range(2)]
        xnN = [sbl(f"xnN{k}", (128, 2048), BF16) for k in range(2)]
        sqs = sbl("sqs", (128, 2048), F32)
        lnS = [{k: sbl(f"ln_{k}{p}", (128, 1), F32)
                for k in ("s1", "s2", "mean", "varb", "vare", "stdv", "rstd",
                          "negmrs")} for p in range(2)]

        lnC = [0] * 16
        lnT = [[0, 0] for _ in range(16)]
        lnEv = [[0, 0] for _ in range(16)]
        lnXld = [0] * 16
        lnVng = [0] * 16

        def ln_front(i):
            p = lnS[i % 2]
            if i >= 2:
                wait("sync", "mC", lnC[i - 2])
            lnXld[i] = dmaS(xbuf[i % 2][:], dd["x"][128 * i:128 * (i + 1), :])
            wait("vector", "dS", lnXld[i])
            op("vector", lambda E, i_=i, p_=p: E.reduce_sum(
                p_["s1"][:], xbuf[i_ % 2][:], axis=AX.X))
            wait("scalar", "dS", lnXld[i])
            op("scalar", lambda E, i_=i, p_=p: E.activation(
                sqs[:], xbuf[i_ % 2][:], AF.Square, accum_out=p_["s2"][:]))
            csq = cmark()
            wait("vector", "mC", csq)
            op("vector", lambda E, p_=p: E.tensor_scalar_mul(
                p_["varb"][:], p_["s2"][:], 1.0 / 2048.0))
            op("vector", lambda E, p_=p: E.scalar_tensor_tensor(
                p_["vare"][:], p_["s1"][:], 1.0 / (2048.0 * 2048.0), p_["s1"][:],
                ALU.mult, ALU.mult))
            op("vector", lambda E, p_=p: E.tensor_scalar_mul(
                p_["mean"][:], p_["s1"][:], 1.0 / 2048.0))
            op("vector", lambda E, p_=p: E.scalar_tensor_tensor(
                p_["varb"][:], p_["vare"][:], -1.0, p_["varb"][:],
                ALU.mult, ALU.add))
            vgap()
            op("vector", lambda E, p_=p: E.tensor_scalar_add(
                p_["vare"][:], p_["varb"][:], 1.0e-5))
            vst = vmark()
            wait("scalar", "mV", vst)
            op("scalar", lambda E, p_=p: E.activation(
                p_["stdv"][:], p_["vare"][:], AF.Sqrt))
            cst = cmark()
            wait("vector", "mC", cst)
            op("vector", lambda E, p_=p: E.reciprocal(
                p_["rstd"][:], p_["stdv"][:]))
            vgap()
            op("vector", lambda E, p_=p: E.scalar_tensor_tensor(
                p_["negmrs"][:], p_["mean"][:], -1.0, p_["rstd"][:],
                ALU.mult, ALU.mult))
            lnVng[i] = vmark()

        def ln_back(i):
            p = lnS[i % 2]
            wait("scalar", "mV", lnVng[i])
            if i >= 2:
                wait("scalar", "mT", lnT[i - 2][1])
            lnC[i] = op("scalar", lambda E, i_=i, p_=p: E.activation(
                xnN[i_ % 2][:], xbuf[i_ % 2][:], AF.Identity,
                bias=p_["negmrs"][:], scale=p_["rstd"][:]), inc="mC")
            wait("tensor", "dG", dG_consts)
            wait("tensor", "mC", lnC[i])
            for b2 in range(2):
                pTx = pT if b2 == 0 else pT2
                if i >= 1:
                    wait("tensor", "mV", lnEv[i - 1][b2])
                for j8 in range(8):
                    j = 8 * b2 + j8
                    fn = (lambda E, i_=i, j_=j, j8_=j8, px=pTx: E.transpose(
                        px[:, 128 * j8_:128 * (j8_ + 1)],
                        xnN[i_ % 2][:, 128 * j_:128 * (j_ + 1)], identb[:]))
                    if j8 == 7:
                        lnT[i][b2] = op("tensor", fn, inc="mT")
                    else:
                        op("tensor", fn)
                wait("vector", "mT", lnT[i][b2])
                dst = bass.AP(xnT, (8 * b2) * 2048 + 128 * i,
                              [[ET * 2048, 128], [2048, 8], [1, 128]])
                src_ = bass.AP(pTx, 0, [[1024, 128], [128, 8], [1, 128]])
                lnEv[i][b2] = op("vector",
                                 lambda E, d_=dst, s_=src_: E.tensor_copy(d_, s_),
                                 inc="mV")

        for i in range(17):
            if i < 16:
                ln_front(i)
            if i >= 1:
                ln_back(i - 1)
        for j in range(ET):
            vcomp = op("vector", lambda E, j_=j: E.tensor_copy(
                xn_own[:, 1024 * j_:1024 * (j_ + 1)],
                xnT[:, 2048 * j_:2048 * j_ + 1024]), inc="mV")
        xn_own_done = vcomp

        if stop == "xn":
            wait("gpsimd", "mV", xn_own_done)
            fin(dmaG(tap_d, xnT[:]), "dG")

        if stop != "xn":
            wait("tensor", "dG", dG_consts)
            wait("tensor", "mV", xn_own_done)
            for i in range(ST):
                for j in range(ET):
                    fn = (lambda E, i_=i, j_=j: E.matmul(
                        pA[:, 8 * i_:8 * (i_ + 1)],
                        xn_own[:, 1024 * j_ + 128 * i_:1024 * j_ + 128 * (i_ + 1)],
                        gw_sb[:, 8 * j_:8 * (j_ + 1)],
                        start=(j_ == 0), stop=(j_ == 15)))
                    if i == ST - 1 and j == ET - 1:
                        gmm = op("tensor", fn, inc="mT")
                    else:
                        op("tensor", fn)
            wait("vector", "dS", dS_c)
            wait("vector", "mT", gmm)
            op("vector", lambda E: E.tensor_copy(logits[:], pA[:, 0:64]))
            glog = vmark()
            if stop == "gat0":
                wait("sync", "mV", glog)
                fin(dmaS(tap_d, logits[:]))
            for i in range(ST if stop != "gat0" else 0):
                sl = slice(8 * i, 8 * (i + 1))
                op("vector", lambda E, s_=sl: E.reduce_max(
                    nm1["gnm"][:], logits[:, s_], axis=AX.X, negate=True))
                vgap()
                vm = vmark()
                wait("scalar", "mV", vm)
                op("scalar", lambda E, s_=sl: E.activation(
                    probs[:, s_], logits[:, s_], AF.Exp, bias=nm1["gnm"][:],
                    accum_out=nm1["gsum"][:]))
                cse = cmark()
                wait("vector", "mC", cse)
                op("vector", lambda E: E.reciprocal(nm1["grin"][:], nm1["gsum"][:]))
                vgap()
                op("vector", lambda E, s_=sl: E.tensor_scalar_mul(
                    probs[:, s_], probs[:, s_], nm1["grin"][:]))
                vgap()
                op("vector", lambda E, s_=sl: E.max(out=top8[:], in_=probs[:, s_]))
                vgap()
                op("vector", lambda E, s_=sl: E.tensor_tensor(
                    mask8[:], probs[:, s_], top8[:, 1:2].to_broadcast([128, 8]),
                    ALU.is_ge))
                vgap()
                op("vector", lambda E, s_=sl: E.tensor_mul(
                    combf[:, s_], probs[:, s_], mask8[:]))
                vgap()
                vcb = op("vector", lambda E, s_=sl: E.tensor_copy(
                    comb_bf[:, s_], combf[:, s_]), inc="mV")
            if stop == "gat1":
                wait("sync", "mV", vcb)
                fin(dmaS(tap_d, combf[:]))
            if stop not in ("gat0", "gat1"):
                wait("tensor", "mV", vcb)
                wait("tensor", "mV", lnEv[15][0])
                ctr = op("tensor", lambda E: E.transpose(
                    pT[0:64, 0:128], comb_bf[:], identb[:]), inc="mT")
                wait("vector", "mT", ctr)
                vct = op("vector", lambda E: E.tensor_copy(
                    combT64[:], pT[0:64, 0:128]), inc="mV")

        if stop == "gating":
            wait("sync", "mV", vcb)
            fin(dmaS(tap_d, combf[:]))


        es_l.close()
        es_f = es_f1 = es_fo = None

        if stop not in ("xn", "gating", "gat0", "gat1"):
            pass
        if stop not in ("xn", "gating", "gat0", "gat1"):
            wqb = [sba(f"wqb{k}", (128, 2048), BF16) for k in range(2)]
            wkb = [sba(f"wkb{k}", (128, 2048), BF16) for k in range(2)]
            wvb = [sba(f"wvb{k}", (128, 2048), BF16) for k in range(2)]
            stage1 = sba("stage1", (128, 2048), BF16)
            stage2 = sba("stage2", (128, 2048), BF16)
            qh = sba("qh", (128, 1024), BF16)
            kh = sba("kh", (128, 2048), BF16)
            vn = sba("vn", (128, 2048), BF16)
            Pb2 = [sba(f"Pb{k}", (128, 2048), BF16) for k in range(2)]
            PTq2 = [sba(f"PTq{k}", (128, 2048), BF16) for k in range(2)]
            rs8 = [[sba(f"rs_{p}_{i}", (128, 1), F32) for i in range(4)]
                   for p in range(2)]
            rsx = [{k: sba(f"rsx_{k}{p}", (128, 1), F32)
                    for k in ("a", "b", "t", "i")} for p in range(2)]

            wld = [0] * H
            wrel = [0] * H
            pvEv = [0] * H
            nheads = 1 if stop in ("head0", "h0t0") else H
            expHist = []
            trPar = [0, 0]
            ptFree = [vct, lnEv[15][1]]
            kropeL = 0
            vevL = 0

            expBank = [0, 0, 0, 0]
            scmmLast = 0

            def preg(i, c):
                if i == 0:
                    return pA[:, 512 * c:512 * (c + 1)]
                if i == 1:
                    return pS[:, 512 * c:512 * (c + 1)]
                return pS[:, 1024 + 512 * c:1024 + 512 * (c + 1)]

            for h in range(nheads):
                if h >= 2:
                    wait("gpsimd", "mT", wrel[h - 2])
                else:
                    wait("gpsimd", "mT", lnT[15][1])
                    wait("gpsimd", "mC", lnC[15])
                for wb_, nm in ((wqb, "wq"), (wkb, "wk"), (wvb, "wv")):
                    wld[h] = dmaG(wb_[h % 2][:],
                                  dd[nm][h].rearrange("p j d -> p (j d)"))

                def proj_mm(wbuf, reg, bofs, bidx):
                    last = None
                    for j in range(ET):
                        for c in range(2):
                            op("tensor", lambda E, c_=c, j_=j, w_=wbuf, r_=reg:
                               E.matmul(
                                   preg(r_, c_),
                                   w_[:, 128 * j_:128 * (j_ + 1)],
                                   (xn_own[:, 1024 * j_ + 512 * c_:
                                           1024 * j_ + 512 * (c_ + 1)]
                                    if bofs is None else
                                    xnT[:, 2048 * j_ + bofs + 512 * c_:
                                        2048 * j_ + bofs + 512 * (c_ + 1)]),
                                   start=(j_ == 0), stop=False))
                    for c in range(2):
                        fnb = (lambda E, c_=c, r_=reg, rr=bidx: E.matmul(
                            preg(r_, c_),
                            bass.AP(bias4, 32 * (rr % 3) * 2816 + 128 * (rr // 3),
                                    [[2816, 1], [1, 128]]),
                            bass.AP(onesb, 32 * (rr % 3) * 512, [[512, 1], [1, 512]]),
                            start=False, stop=True))
                        if c == 1:
                            last = op("tensor", fnb, inc="mT")
                        else:
                            op("tensor", fnb)
                    return last

                def swap_mm(reg, src_ap_fn):
                    last = None
                    for c in range(2):
                        fn = (lambda E, c_=c, r_=reg, sf=src_ap_fn: E.matmul(
                            preg(r_, c_), permb[:], sf(c_), start=True, stop=True))
                        if c == 1:
                            last = op("tensor", fn, inc="mT")
                        else:
                            op("tensor", fn)
                    return last

                wait("tensor", "dG", wld[h])
                if h == 0:
                    wait("tensor", "mV", glog)
                else:
                    wait("tensor", "mV", pvEv[h - 1])
                qmm = proj_mm(wqb[h % 2], 0, None, h)
                wait("tensor", "mC", expBank[0])
                wait("tensor", "mC", expBank[1])
                kmm0 = proj_mm(wkb[h % 2], 1, 0, 16 + h)
                wait("tensor", "mC", expBank[2])
                wait("tensor", "mC", expBank[3])
                kmm1 = proj_mm(wkb[h % 2], 2, 1024, 16 + h)
                wait("scalar", "mT", qmm)
                if h > 0:
                    wait("scalar", "mT", scmmLast)
                qev = op("scalar", lambda E: E.activation(
                    qh[:], pA[:, 0:1024], AF.Identity), inc="mC")
                wait("scalar", "mT", kmm0)
                kev0 = op("scalar", lambda E: E.activation(
                    kh[:, 0:1024], pS[:, 0:1024], AF.Identity), inc="mC")
                wait("scalar", "mT", kmm1)
                kev1 = op("scalar", lambda E: E.activation(
                    kh[:, 1024:2048], pS[:, 1024:2048], AF.Identity), inc="mC")
                wait("tensor", "mC", qev)
                qsw = swap_mm(0, lambda c: qh[:, 512 * c:512 * (c + 1)])
                wait("tensor", "mC", kev0)
                ksw0 = swap_mm(1, lambda c: kh[:, 512 * c:512 * (c + 1)])
                wait("tensor", "mC", kev1)
                ksw1 = swap_mm(2, lambda c: kh[:, 1024 + 512 * c:1536 + 512 * c])
                wait("scalar", "mT", qsw)
                if h > 0:
                    wait("scalar", "mT", wrel[h - 1])
                qsv = op("scalar", lambda E: E.activation(
                    stage2[:, 0:1024], pA[:, 0:1024], AF.Identity), inc="mC")
                wait("scalar", "mT", ksw0)
                ksv0 = op("scalar", lambda E: E.activation(
                    stage2[:, 1024:2048], pS[:, 0:1024], AF.Identity), inc="mC")
                wait("vector", "mT", qsw)
                op("vector", lambda E: E.tensor_tensor(
                    qh[:], qh[:], coskb[:, 0:1024], ALU.mult))
                wait("vector", "mC", qsv)
                op("vector", lambda E: E.tensor_tensor(
                    stage2[:, 0:1024], stage2[:, 0:1024], sinkb[:, 0:1024],
                    ALU.mult))
                op("vector", lambda E: E.tensor_tensor(
                    qh[:], qh[:], stage2[:, 0:1024], ALU.add))
                qrope = op("vector", lambda E: E.tensor_scalar_mul(
                    qh[:], qh[:], ISQD), inc="mV")
                wait("scalar", "mT", ksw1)
                wait("scalar", "mV", qrope)
                ksv1 = op("scalar", lambda E: E.activation(
                    stage2[:, 0:1024], pS[:, 1024:2048], AF.Identity), inc="mC")
                wait("vector", "mT", ksw0)
                op("vector", lambda E: E.tensor_tensor(
                    kh[:, 0:1024], kh[:, 0:1024], coskb[:, 0:1024], ALU.mult))
                wait("vector", "mC", ksv0)
                op("vector", lambda E: E.tensor_tensor(
                    stage2[:, 1024:2048], stage2[:, 1024:2048],
                    sinkb[:, 0:1024], ALU.mult))
                krope0 = op("vector", lambda E: E.tensor_tensor(
                    kh[:, 0:1024], kh[:, 0:1024], stage2[:, 1024:2048], ALU.add),
                    inc="mV")
                wait("vector", "mT", ksw1)
                op("vector", lambda E: E.tensor_tensor(
                    kh[:, 1024:2048], kh[:, 1024:2048], coskb[:, 1024:2048],
                    ALU.mult))
                wait("vector", "mC", ksv1)
                op("vector", lambda E: E.tensor_tensor(
                    stage2[:, 0:1024], stage2[:, 0:1024], sinkb[:, 1024:2048],
                    ALU.mult))
                kropeL = op("vector", lambda E: E.tensor_tensor(
                    kh[:, 1024:2048], kh[:, 1024:2048], stage2[:, 0:1024],
                    ALU.add), inc="mV")
                if stop == "h0q":
                    wait("gpsimd", "mV", qrope)
                    fin(dmaG(tap_d[:, 0:1024], qh[:]), "dG")
                    break
                if stop == "h0k":
                    wait("gpsimd", "mV", kropeL)
                    fin(dmaG(tap_d, kh[:]), "dG")
                    break
                wait("tensor", "mC", qsv)
                vmm0 = proj_mm(wvb[h % 2], 0, 0, 32 + h)
                wait("tensor", "mC", ksv0)
                vmm1 = proj_mm(wvb[h % 2], 1, 1024, 32 + h)
                wait("scalar", "mT", vmm0)
                wait("scalar", "mV", krope0)
                vev0 = op("scalar", lambda E: E.activation(
                    stage2[:, 1024:2048], pA[:, 0:1024], AF.Identity), inc="mC")
                wait("scalar", "mT", vmm1)
                wait("scalar", "mV", kropeL)
                vev1 = op("scalar", lambda E: E.activation(
                    stage2[:, 0:1024], pS[:, 0:1024], AF.Identity), inc="mC")
                vevL = vev1
                vnev = [0, 0]
                for b2 in range(2):
                    pTx = pT if b2 == 0 else pT2
                    wait("tensor", "mC", vev0 if b2 == 0 else vev1)
                    wait("tensor", "mV", ptFree[b2])
                    for j8 in range(8):
                        scol = (1024 + 128 * j8) if b2 == 0 else (128 * j8)
                        fn = (lambda E, s_=scol, j8_=j8, px=pTx: E.transpose(
                            px[:, 128 * j8_:128 * (j8_ + 1)],
                            stage2[:, s_:s_ + 128], identb[:]))
                        if j8 == 7:
                            vtr = op("tensor", fn, inc="mT")
                        else:
                            op("tensor", fn)
                    wait("vector", "mT", vtr)
                    vnev[b2] = op("vector", lambda E, b2_=b2: E.tensor_copy(
                        vn[:, 1024 * b2_:1024 * (b2_ + 1)],
                        (pT if b2_ == 0 else pT2)[:]), inc="mV")
                ptFree = [vnev[0], vnev[1]]
                wrel[h] = vtr
                if stop == "h0v":
                    wait("gpsimd", "mV", vnev[1])
                    fin(dmaG(tap_d, vn[:]), "dG")
                    break

                pvE = [0] * ST
                pend = None

                def attn_tail(t, psc):
                    nonlocal ptFree
                    par = t % 2
                    L = 128 * (t + 1)
                    wait("tensor", "mV", psc)
                    ptEv = [0, 0]
                    for b2 in range(2):
                        pTx = pT if b2 == 0 else pT2
                        wait("tensor", "mV", ptFree[b2])
                        for u in range(t + 1):
                            src_col = (L if b2 else 0) + 128 * u
                            fn = (lambda E, s_=src_col, u_=u, px=pTx, p_=par:
                                  E.transpose(
                                      px[:, 128 * u_:128 * (u_ + 1)],
                                      Pb2[p_][:, s_:s_ + 128], identb[:]))
                            if u == t:
                                tr_ = op("tensor", fn, inc="mT")
                            else:
                                op("tensor", fn)
                        wait("vector", "mT", tr_)
                        ptEv[b2] = op("vector", lambda E, t_=t, b2_=b2, p_=par:
                                      E.tensor_copy(
                                          PTq2[p_][:, 1024 * b2_:
                                                    1024 * b2_ + 128 * (t_ + 1)],
                                          (pT if b2_ == 0 else pT2)
                                          [:, 0:128 * (t_ + 1)]), inc="mV")
                    ptFree = [ptEv[0], ptEv[1]]
                    trPar[par] = tr_
                    pvreg = 512 * par
                    wait("tensor", "mV", ptEv[1])
                    if t == 0:
                        wait("tensor", "mC", vevL)
                    if t >= 2:
                        wait("tensor", "mV", pvE[t - 2])
                    elif h > 0:
                        wait("tensor", "mV", pvEv[h - 1])
                    first = True
                    for b2 in range(2):
                        for u in range(t + 1):
                            kslot = u if b2 == 0 else 8 + u
                            vcol = 128 * u if b2 == 0 else 1024 + 128 * u
                            last = (b2 == 1 and u == t)
                            fn = (lambda E, v_=vcol, k_=kslot, p_=pvreg, f_=first,
                                  l_=last, pp=par: E.matmul(
                                      pA[:, p_:p_ + 128], vn[:, v_:v_ + 128],
                                      PTq2[pp][:, 128 * k_:128 * (k_ + 1)],
                                      start=f_, stop=l_))
                            if last:
                                pvt_ = op("tensor", fn, inc="mT")
                            else:
                                op("tensor", fn)
                            first = False
                    wait("vector", "mT", pvt_)
                    pvE[t] = op("vector", lambda E, h_=h, t_=t, p_=pvreg:
                                E.tensor_copy(
                                    attn_out[:, 1024 * h_ + 128 * t_:
                                             1024 * h_ + 128 * (t_ + 1)],
                                    pA[:, p_:p_ + 128]), inc="mV")

                for t in range(ST):
                    L = 128 * (t + 1)
                    par = t % 2
                    chunks = []
                    for b2, base in ((0, 0), (1, 1024)):
                        off = 0
                        while off < L:
                            cw = min(512, L - off)
                            chunks.append((base, off, cw, b2))
                            off += cw
                    nch = len(chunks)
                    for ci, (base, off, cw, b2) in enumerate(chunks):
                        gci = len(expHist)
                        bank = gci % 4
                        wait("tensor", "mC", expBank[bank])
                        if t == 0 and ci == 0:
                            wait("tensor", "mV", qrope)
                            wait("tensor", "mV", kropeL)
                            wait("tensor", "mC", vevL)
                        scmm = op("tensor",
                                  lambda E, t_=t, b_=base, o_=off, w_=cw, bk=bank:
                                  E.matmul(
                                      pS[:, 512 * bk:512 * bk + w_],
                                      qh[:, 128 * t_:128 * (t_ + 1)],
                                      kh[:, b_ + o_:b_ + o_ + w_],
                                      start=True, stop=True), inc="mT")
                        scmmLast = scmm
                        ismask = (off + cw == L)
                        if ismask:
                            mlocal = 128 * t - off
                            wait("vector", "mT", scmm)
                            op("vector", lambda E, b2_=b2, bk=bank, ml=mlocal:
                               E.tensor_tensor(
                                   pS[:, 512 * bk + ml:512 * bk + ml + 128],
                                   pS[:, 512 * bk + ml:512 * bk + ml + 128],
                                   masks[:, 128 * b2_:128 * b2_ + 128],
                                   ALU.add))
                            mrd = vmark()
                            wait("scalar", "mV", mrd)
                        else:
                            wait("scalar", "mT", scmm)
                        if ci == 0:
                            wait("scalar", "mT", trPar[par])
                        outc = (L + off) if b2 else off
                        expi = op("scalar",
                                  lambda E, p_=par, oc=outc, w_=cw, bk=bank,
                                  ci_=ci: E.activation(
                                      Pb2[p_][:, oc:oc + w_],
                                      pS[:, 512 * bk:512 * bk + w_], AF.Exp,
                                      accum_out=rs8[p_][ci_][:]), inc="mC")
                        expHist.append(expi)
                        expBank[bank] = expi
                    wait("vector", "mC", expHist[-1])
                    rx = rsx[par]
                    if nch == 2:
                        op("vector", lambda E, p_=par: E.tensor_add(
                            rx["t"][:], rs8[p_][0][:], rs8[p_][1][:]))
                    else:
                        op("vector", lambda E, p_=par: E.tensor_add(
                            rx["a"][:], rs8[p_][0][:], rs8[p_][1][:]))
                        op("vector", lambda E, p_=par: E.tensor_add(
                            rx["b"][:], rs8[p_][2][:], rs8[p_][3][:]))
                        vgap()
                        op("vector", lambda E: E.tensor_add(
                            rx["t"][:], rx["a"][:], rx["b"][:]))
                    vgap()
                    op("vector", lambda E: E.reciprocal(rx["i"][:], rx["t"][:]))
                    vgap()
                    psc = op("vector", lambda E, p_=par, L_=L: E.tensor_scalar_mul(
                        Pb2[p_][:, 0:2 * L_], Pb2[p_][:, 0:2 * L_], rx["i"][:]),
                        inc="mV")
                    if pend is not None:
                        attn_tail(*pend)
                    pend = (t, psc)
                attn_tail(*pend)
                pvEv[h] = pvE[ST - 1]

            attn_done_V = pvEv[nheads - 1]

            if stop == "head0":
                wait("gpsimd", "mV", qrope)
                dmaG(tap_d[:, 0:1024], qh[:])
                wait("gpsimd", "mV", kropeL)
                dmaG(tap_d[:, 2048:4096], kh[:])
                wait("gpsimd", "mV", vnev[1])
                dmaG(tap_d[:, 4096:6144], vn[:])
                wait("gpsimd", "mV", attn_done_V)
                fin(dmaG(tap_d[:, 6144:7168], attn_out[:, 0:1024]), "dG")
            if stop == "attn":
                wait("gpsimd", "mV", attn_done_V)
                fin(dmaG(tap_d, attn_out[:]), "dG")

        if stop in (None, "oproj", "oproj0", "ffn"):
            es_a.close()
            bT, bV, bC = C["mT"], C["mV"], C["mC"]
            for eng in ENGINES:
                wait(eng, "mT", bT)
                wait(eng, "mV", bV)
                wait(eng, "mC", bC)

            es_f = ExitStack()
            sbf_ = lambda n_, s_, d_: es_f.enter_context(
                nc.sbuf_tensor(n_, list(s_), d_))
            psf_ = lambda n_, s_, d_: es_f.enter_context(
                nc.psum_tensor(n_, list(s_), d_))
            acc = sbf_("acc", (128, HT * 1024), F32)
            pW = psf_("pW", (128, 1024), F32)
            pX = psf_("pX", (128, 1024), F32)
            pY = psf_("pY", (128, 1024), F32)
            pZ = psf_("pZ", (128, 1024), F32)
            es_fo = ExitStack()
            wob = [es_fo.enter_context(nc.sbuf_tensor(f"wob{k}", [128, 2048], BF16))
                   for k in range(2)]

            woL = {}
            woR = {}
            oEv = [0] * HT
            woL[0] = dmaG(wob[0][:], dd["wo"][0].rearrange("p j d -> p (j d)"))
            woL[1] = dmaG(wob[1][:], dd["wo"][1].rearrange("p j d -> p (j d)"))
            for fb in range(HT if stop != "oproj0" else 16):
                pP = pW if fb % 2 == 0 else pX
                wait("tensor", "dG", woL[fb])
                if fb >= 2:
                    wait("tensor", "mC", oEv[fb - 2])
                for j in range(ET):
                    for c in range(2):
                        op("tensor", lambda E, f_=fb, c_=c, j_=j, p_=pP: E.matmul(
                            p_[:, 512 * c_:512 * (c_ + 1)],
                            wob[f_ % 2][:, 128 * j_:128 * (j_ + 1)],
                            attn_out[:, 1024 * j_ + 512 * c_:
                                     1024 * j_ + 512 * (c_ + 1)],
                            start=(j_ == 0), stop=False))
                for c in range(2):
                    fnb = (lambda E, f_=fb, c_=c, p_=pP: E.matmul(
                        p_[:, 512 * c_:512 * (c_ + 1)],
                        bass.AP(bias4,
                                32 * ((48 + f_) % 3) * 2816 + 128 * ((48 + f_) // 3),
                                [[2816, 1], [1, 128]]),
                        bass.AP(onesb, 32 * ((48 + f_) % 3) * 512,
                                [[512, 1], [1, 512]]),
                        start=False, stop=True))
                    if c == 1:
                        woR[fb] = op("tensor", fnb, inc="mT")
                    else:
                        op("tensor", fnb)
                if fb + 2 < HT:
                    wait("gpsimd", "mT", woR[fb])
                    woL[fb + 2] = dmaG(wob[fb % 2][:],
                                       dd["wo"][fb + 2].rearrange("p j d -> p (j d)"))
                wait("scalar", "mT", woR[fb])
                oEv[fb] = op("scalar", lambda E, f_=fb, p_=pP: E.activation(
                    acc[:, 1024 * f_:1024 * (f_ + 1)], p_[:], AF.Identity),
                    inc="mC")
            o_done = oEv[HT - 1] if stop != "oproj0" else oEv[0]
            if stop == "oproj0":
                wait("sync", "mC", o_done)
                fin(dmaS(tap_d, acc[:, 0:1024]))

            if stop == "oproj":
                wait("sync", "mC", o_done)
                fin(dmaS(tap_d, acc[:]))

        if stop in (None, "ffn"):
            es_fo.close()
            es_f1 = ExitStack()
            sbf1 = lambda n_, s_, d_: es_f1.enter_context(
                nc.sbuf_tensor(n_, list(s_), d_))
            gub = [sbf1(f"gub{k}", (128, 4096), BF16) for k in range(2)]
            dnb = [sbf1(f"dnb{k}", (128, 2816), BF16) for k in range(2)]
            act = sbf1("act", (128, FT * 1024), BF16)
            bc_sb = sbf1("bc_sb", (128, 1024), BF16)
            t1_sb = sbf1("t1_sb", (128, 1024), BF16)
            combT2 = sbf1("combT2", (1, 8192), BF16)
            wait("gpsimd", "mT", woR[HT - 1])
            wait("gpsimd", "mV", vct)
            for e in range(8):
                dG_comb = dmaG(
                    bass.AP(combT2, e * 1024, [[8192, 1], [128, 8], [1, 128]]),
                    bass.AP(combT64, e * 128, [[8 * 128, 8], [1, 128]]))
            guL = {}
            guR = {}
            dnL = {}
            dnR = {}
            guL[0] = dmaG(gub[0][:], dd["guw"][0, 0])
            guL[1] = dmaG(gub[1][:], dd["guw"][0, 1])
            dnL[0] = dmaG(dnb[0][:], dd["dnw"][0, 0])
            dnL[1] = dmaG(dnb[1][:], dd["dnw"][0, 1])

            dnAdd = 0
            bcV = 0
            guEvLast = 0
            for e in range(NE):
                gated = e < 8
                if gated:
                    wait("tensor", "dG", dG_comb)
                    if e > 0:
                        wait("tensor", "mV", bcV)
                        wait("tensor", "mV", guEvLast)
                    else:
                        wait("tensor", "mC", o_done)
                    for c in range(2):
                        fn = (lambda E, e_=e, c_=c: E.matmul(
                            pY[:, 512 * c_:512 * (c_ + 1)], onesb[0:1, 0:128],
                            combT2[0:1, 1024 * e_ + 512 * c_:
                                   1024 * e_ + 512 * (c_ + 1)],
                            start=True, stop=True))
                        if c == 1:
                            bcT = op("tensor", fn, inc="mT")
                        else:
                            op("tensor", fn)
                    wait("vector", "mT", bcT)
                    bcV = op("vector", lambda E: E.tensor_copy(bc_sb[:], pY[:]),
                             inc="mV")
                guEv = [0] * FT
                for jg in range(FT):
                    k = e * FT + jg
                    pair = (pW, pX) if jg % 2 == 0 else (pY, pZ)
                    wait("tensor", "dG", guL[k])
                    if jg >= 2:
                        wait("tensor", "mV", guEv[jg - 2])
                    elif e == 0:
                        wait("tensor", "mC", o_done)
                    else:
                        wait("tensor", "mV", dnAdd)
                        if jg == 1 and gated:
                            wait("tensor", "mV", bcV)
                    for half2 in range(2):
                        pP = pair[half2]
                        for j in range(ET):
                            for c in range(2):
                                fn = (lambda E, k_=k, h2=half2, c_=c, j_=j, p_=pP:
                                      E.matmul(
                                          p_[:, 512 * c_:512 * (c_ + 1)],
                                          gub[k_ % 2][:, 2048 * h2 + 128 * j_:
                                                      2048 * h2 + 128 * (j_ + 1)],
                                          xn_own[:, 1024 * j_ + 512 * c_:
                                                 1024 * j_ + 512 * (c_ + 1)],
                                          start=(j_ == 0), stop=(j_ == 15)))
                                if half2 == 1 and c == 1 and j == ET - 1:
                                    guR[k] = op("tensor", fn, inc="mT")
                                else:
                                    op("tensor", fn)
                    if k + 2 < NE * FT:
                        e2, jg2 = divmod(k + 2, FT)
                        wait("gpsimd", "mT", guR[k])
                        guL[k + 2] = dmaG(gub[k % 2][:], dd["guw"][e2, jg2])
                    wait("scalar", "mT", guR[k])
                    if jg >= 1:
                        wait("scalar", "mV", guEv[jg - 1])
                    elif e > 0:
                        wait("scalar", "mV", guEvPrevLast)
                    sil = op("scalar", lambda E, p_=pair[0]: E.activation(
                        t1_sb[:], p_[:], AF.Silu), inc="mC")
                    wait("vector", "mC", sil)
                    if gated:
                        wait("vector", "mV", bcV)
                    op("vector", lambda E, p_=pair[1]: E.tensor_tensor(
                        t1_sb[:], t1_sb[:], p_[:], ALU.mult))
                    if gated:
                        guEv[jg] = op("vector", lambda E, g_=jg: E.tensor_tensor(
                            act[:, 1024 * g_:1024 * (g_ + 1)], t1_sb[:], bc_sb[:],
                            ALU.mult), inc="mV")
                    else:
                        guEv[jg] = op("vector", lambda E, g_=jg: E.tensor_copy(
                            act[:, 1024 * g_:1024 * (g_ + 1)], t1_sb[:]), inc="mV")
                guEvPrevLast = guEv[FT - 1]
                guEvLast = guEv[FT - 1]
                dnAddPrev = [guEv[FT - 1], guEv[FT - 2]]
                for hb in range(HT):
                    c4, hb2 = hb // 2, hb % 2
                    m = e * 8 + c4
                    pP = pW if hb % 2 == 0 else pX
                    wait("tensor", "dG", dnL[m])
                    wait("tensor", "mV", guEvLast)
                    if hb >= 2:
                        wait("tensor", "mV", dnAddPrev[hb % 2])
                    for f in range(FT):
                        for c in range(2):
                            fn = (lambda E, m_=m, h2=hb2, f_=f, c_=c, p_=pP:
                                  E.matmul(
                                      p_[:, 512 * c_:512 * (c_ + 1)],
                                      dnb[m_ % 2][:, 1408 * h2 + 128 * f_:
                                                  1408 * h2 + 128 * (f_ + 1)],
                                      act[:, 1024 * f_ + 512 * c_:
                                          1024 * f_ + 512 * (c_ + 1)],
                                      start=(f_ == 0), stop=(f_ == FT - 1)))
                            if f == FT - 1 and c == 1:
                                dmm = op("tensor", fn, inc="mT")
                            else:
                                op("tensor", fn)
                    if hb2 == 1:
                        dnR[m] = dmm
                        if m + 2 < NE * 8:
                            e2, c42 = divmod(m + 2, 8)
                            wait("gpsimd", "mT", dnR[m])
                            dnL[m + 2] = dmaG(dnb[m % 2][:], dd["dnw"][e2, c42])
                    wait("vector", "mT", dmm)
                    dnAdd = op("vector", lambda E, hb_=hb, p_=pP: E.tensor_tensor(
                        acc[:, 1024 * hb_:1024 * (hb_ + 1)],
                        acc[:, 1024 * hb_:1024 * (hb_ + 1)], p_[:], ALU.add),
                        inc="mV")
                    dnAddPrev[hb % 2] = dnAdd
            ffn_done = dnAdd

            if stop == "ffn":
                wait("sync", "mV", ffn_done)
                fin(dmaS(tap_d, acc[:]))

        if stop is None:
            es_f1.close()
            xtc = [es_f.enter_context(nc.sbuf_tensor(f"xtc{i}", [128, 4096], F32))
                   for i in range(2)]
            wait("sync", "mV", ffn_done)
            xld = [0] * 4
            adR = [0] * 4
            stO = [0] * 4
            for k in range(2):
                xld[k] = dmaS(xtc[k][:],
                              dd["xT"][:, 4 * k:4 * (k + 1), :]
                              .rearrange("p j s -> p (j s)"))
            for k in range(4):
                wait("vector", "dS", xld[k])
                adR[k] = op("vector", lambda E, k_=k: E.tensor_tensor(
                    acc[:, 4096 * k_:4096 * (k_ + 1)],
                    acc[:, 4096 * k_:4096 * (k_ + 1)],
                    xtc[k_ % 2][:], ALU.add), inc="mV")
                wait("sync", "mV", adR[k])
                stO[k] = dmaS(out_d[:, 4096 * k:4096 * (k + 1)],
                              acc[:, 4096 * k:4096 * (k + 1)])
                if k + 2 < 4:
                    xld[k + 2] = dmaS(xtc[k % 2][:],
                                      dd["xT"][:, 4 * (k + 2):4 * (k + 3), :]
                                      .rearrange("p j s -> p (j s)"))
            fin(stO[3])

        if es_f1 is not None:
            es_f1.close()
        if es_fo is not None:
            es_fo.close()
        if es_f is not None:
            es_f.close()
        es_a.close()

        block = es.enter_context(nc.Block())

        @block.tensor
        def _(E):
            bld.replay("tensor", E)

        @block.vector
        def _(E):
            bld.replay("vector", E)

        @block.scalar
        def _(E):
            bld.replay("scalar", E)

        @block.sync
        def _(E):
            bld.replay("sync", E)

        @block.gpsimd
        def _(E):
            bld.replay("gpsimd", E)

    nc.finalize()
    return nc


def kernel(**inputs):
    global _LAST
    shared = _prep_shared(inputs)
    cosg, sing = _rope_tables()
    nc = _build(_STOP)
    in_maps = []
    for c in range(8):
        b, half = c // 2, c % 2
        m = dict(shared)
        m.update(_prep_core(inputs, b, half, cosg, sing))
        in_maps.append(m)
    res = run_bass_kernel_spmd(nc, in_maps, core_ids=list(range(8)))
    out = np.zeros((4, 2048, 2048), np.float32)
    _LAST = {}
    for c in range(8):
        b, half = c // 2, c % 2
        own, _ = _core_perm(half)
        out[b][own] = (np.asarray(res.results[c]["out"])
                       .reshape(128, 16, 1024).transpose(2, 1, 0)
                       .reshape(1024, 2048))
        if _STOP is not None:
            _LAST[c] = np.asarray(res.results[c]["tap"])
    return out
